# version 05
# baseline (speedup 1.0000x reference)
import math
import numpy as np
import concourse.bass as bass
import concourse.mybir as mybir
from concourse.bass_utils import run_bass_kernel_spmd

F32 = mybir.dt.float32
BF16 = mybir.dt.bfloat16
U8 = mybir.dt.uint8
OP = mybir.AluOpType
AF = mybir.ActivationFunctionType

NCOR = 8
T = 2048
S = 16384
D = 1024
C8 = 8
TT = 512
NT = 4
DI = 2048
NDI = 16
DTR = 32
NEXP = 16
INTER = 512
HD = 128
EPS = 1e-5
MAGIC = 12582912.0
CW1 = 6.28125
CW2 = 0.0019353071795864769
INV2PI = 1.0 / (2 * math.pi)
HALFPI = math.pi / 2
PICL = 3.141592
ASCALE = 1.0 / math.sqrt(128.0)
RSCALE = 2.5
W1ELEMS = 4096.0 * 1024.0

N_PHASES = 6

_SZ = {F32: 4, BF16: 2, U8: 1}


class Arena:
    def __init__(self, nc, total_bytes):
        self.nc = nc
        self.h = nc.alloc_sbuf_tensor("ARENA", [128, total_bytes], U8)
        self.base = nc.lookup_mloc(self.h).addr
        self.off = 0
        self.total = total_bytes
        self.n = 0

    def at(self, off, name, shape, dtype):
        per = int(np.prod(shape[1:])) * _SZ[dtype]
        assert off % 32 == 0 and off + per <= self.total, (name, off, per, self.total)
        self.n += 1
        return self.nc.alloc_sbuf_tensor_at(
            f"{name}_{self.n}", list(shape), dtype, offset=self.base + off)

    def t(self, name, shape, dtype):
        off = (self.off + 31) // 32 * 32
        h = self.at(off, name, shape, dtype)
        self.off = off + int(np.prod(shape[1:])) * _SZ[dtype]
        return h

    def mark(self):
        return (self.off + 31) // 32 * 32

    def reset(self, off):
        self.off = off


class K:
    def __init__(self, nc, block, sems):
        self.nc = nc
        self.block = block
        self.sem = sems
        self.c = {s: 0 for s in sems}

    def inc(self, instr, s, n=1):
        instr.then_inc(self.sem[s], n)
        self.c[s] += n
        return self.c[s]

    def w(self, eng, s, v):
        if v > 0:
            eng.wait_ge(self.sem[s], v)

    def barrier(self):
        snap = dict(self.c)
        own = {"tensor": "pe", "vector": "dv", "scalar": "ac", "gpsimd": "gp",
               "sync": None}
        for ename in ("sync", "tensor", "vector", "scalar", "gpsimd"):
            def _mk(ename=ename):
                def _(eng):
                    for s, v in snap.items():
                        if s != own[ename]:
                            self.w(eng, s, v)
                return _
            getattr(self.block, ename)(_mk())


SEMS = ["ld", "lxa", "lxb", "gd", "gwa", "gwb", "qwa", "qwb",
        "twa", "twb", "sxr", "sto", "cc", "pe", "dv", "ac", "gp"]


def build_program(taps=(), n_phases=N_PHASES, p4cut=9):
    nc = bass.Bass("TRN2", target_bir_lowering=False)
    taps = list(taps)

    def din(name, shape, dtype=F32):
        return nc.dram_tensor(name, list(shape), dtype, kind="ExternalInput").ap()

    xT = din("xT", [D, T])
    xprevT = din("xprevT", [D, 3])
    halomask = din("halomask", [128, 1])
    ident_in = din("ident_in", [128, 128])
    eall_in = din("eall_in", [16, 16 * 128])
    w_in_t = din("w_in_t", [D, 4096])
    w_x32_t = din("w_x32_t", [DI, DTR])
    w_dt_t = din("w_dt_t", [DTR, DI])
    w_out_t = din("w_out_t", [DI, D])
    conv_w_in = din("conv_w_in", [128, 64])
    conv_b_in = din("conv_b_in", [128, 16])
    b_dt_in = din("b_dt_in", [128, 16])
    gate_w_t = din("gate_w_t", [D, 16])
    gate_b_in = din("gate_b_in", [16, 1])
    w13_t = din("w13_t", [NEXP, D, 2 * INTER])
    w2_t = din("w2_t", [NEXP, INTER, D])
    w13s_t = din("w13s_t", [D, 2 * INTER])
    w2s_t = din("w2s_t", [INTER, D])
    wq_t = din("wq_t", [D, HD]); wk_t = din("wk_t", [D, HD]); wv_t = din("wv_t", [D, HD])
    wo_t = din("wo_t", [HD, D])
    bq_in = din("bq_in", [128, 1]); bk_in = din("bk_in", [128, 1]); bv_in = din("bv_in", [128, 1])
    bo_in = din("bo_in", [128, 8])
    wavel_in = din("wavel_in", [128, 1]); phaseb_in = din("phaseb_in", [128, 1])
    w1_t = din("w1_t", [D, 4096])
    w2l_t = din("w2l_t", [4096, D])
    b1_in = din("b1_in", [128, 32]); b2_in = din("b2_in", [128, 8])
    gamma_in = din("gamma_in", [128, 8]); beta_in = din("beta_in", [128, 8])
    cmod = din("cmod", [2, 128, 8])
    wmodsl = din("wmodsl", [2, D, D])

    yout = nc.dram_tensor("yout", [D, T], F32, kind="ExternalOutput").ap()
    dbg = {tp: nc.dram_tensor("dbg_" + tp, [D, T], F32, kind="ExternalOutput").ap()
           for tp in taps}

    xres = nc.dram_tensor("xres", [D, T], F32).ap()
    modin = nc.dram_tensor("modin", [2, D], F32)
    modout = nc.dram_tensor("modout", [16, D], F32)
    kvin = nc.dram_tensor("kvin", [384, T], BF16)
    kvout = nc.dram_tensor("kvout", [384 * NCOR, T], BF16)
    wq1d = nc.dram_tensor("wq1d", [D, 4096], BF16).ap()
    wq2d = nc.dram_tensor("wq2d", [4096, D], BF16).ap()

    AR = Arena(nc, 212480)
    _pp0 = nc.alloc_psum_tensor("pp0", [128, 1024], F32).ap()
    _pp1 = nc.alloc_psum_tensor("pp1", [128, 1024], F32).ap()
    _ps_mid = [nc.alloc_psum_tensor(f"ps{i}", [128, 512], F32).ap() for i in range(2, 6)]
    ps = [_pp0[:, 0:512], _pp0[:, 512:1024],
          _ps_mid[0], _ps_mid[1], _ps_mid[2], _ps_mid[3],
          _pp1[:, 0:512], _pp1[:, 512:1024]]
    pspair = [_pp0, _pp1]

    def chw(t, c, w):
        return t.ap()[:, c * w:(c + 1) * w]

    identf = AR.t("identf", [128, 128], F32)
    identb = AR.t("identb", [128, 128], BF16)
    eall = AR.t("eall", [16, 16 * 128], BF16)
    ones_cb = AR.t("ones_cb", [128, 1], BF16)
    ones_cf = AR.t("ones_cf", [128, 1], F32)
    ones_rb = AR.t("ones_rb", [1, 128], BF16)
    ones_rf = AR.t("ones_rf", [1, 128], F32)
    cst = AR.t("cst", [128, 2], F32)
    hmsk = AR.t("hmsk", [128, 1], F32)
    convw = AR.t("convw", [128, 64], F32)
    convb = AR.t("convb", [128, 16], F32)
    bdt = AR.t("bdt", [128, 16], F32)
    gatewt = AR.t("gatewt", [128, 8 * 16], F32)
    gateb = AR.t("gateb", [16, 1], F32)
    cmod_s = AR.t("cmod_s", [128, 16], F32)
    bq_s = AR.t("bq_s", [128, 1], F32); bk_s = AR.t("bk_s", [128, 1], F32)
    bv_s = AR.t("bv_s", [128, 1], F32); bo_s = AR.t("bo_s", [128, 8], F32)
    invw = AR.t("invw", [128, 1], F32); phb = AR.t("phb", [128, 1], F32)
    b1_s = AR.t("b1_s", [128, 32], F32); b2_s = AR.t("b2_s", [128, 8], F32)
    gam_s = AR.t("gam_s", [128, 8], F32); bet_s = AR.t("bet_s", [128, 8], F32)
    wqkv_bf = AR.t("wqkv_bf", [128, C8 * 3 * HD], BF16)
    wo_bf = AR.t("wo_bf", [HD, 1024], BF16)
    wmean = AR.t("wmean", [128, 2], F32)
    wred = AR.t("wred", [128, 1], F32)
    swrep = AR.t("swrep", [128, 2], F32)
    cmrep = AR.t("cmrep", [128, 2], F32)
    modv = {}
    for l in (0, 1):
        for nm in ("sa", "ca1", "ga", "sb", "cb1", "gb"):
            modv[(l, nm)] = AR.t(f"m{l}{nm}", [128, 8], F32)
    gtmp = AR.t("gtmp", [128, 8], F32)
    gtmp2 = AR.t("gtmp2", [128, 8], F32)
    rowm = AR.t("rowm", [1, TT], F32)
    rowq = AR.t("rowq", [1, TT], F32)
    rowt = AR.t("rowt", [1, TT], F32)
    rhiA = AR.t("rhiA", [1, TT], BF16)
    rloA = AR.t("rloA", [1, TT], F32)
    rloAb = AR.t("rloAb", [1, TT], BF16)
    rhiB = AR.t("rhiB", [1, TT], BF16)
    rloB = AR.t("rloB", [1, TT], F32)
    rloBb = AR.t("rloBb", [1, TT], BF16)
    rw4 = AR.t("rw4", [1, 4], F32)
    rw4h = AR.t("rw4h", [1, 4], BF16)
    rw4l = AR.t("rw4l", [1, 4], F32)
    rw4lb = AR.t("rw4lb", [1, 4], BF16)

    P_BASE = AR.mark()

    import contextlib
    _es = contextlib.ExitStack()
    sems = {s: _es.enter_context(nc.semaphore("sm_" + s)) for s in SEMS}
    block = _es.enter_context(nc.Block())
    k = K(nc, block, sems)

    def SY(f): block.sync(f)
    def TE(f): block.tensor(f)
    def VE(f): block.vector(f)
    def SC(f): block.scalar(f)
    def GP(f): block.gpsimd(f)

    def emit_ln(xap, xbf_t, out_fn, s1c, s2c, W, tmpf, tmp2, prev_apply_dv,
                extra_gp=None, extra_ac=None):
        def _gp(g):
            if extra_gp is not None:
                k.w(g, extra_gp[0], extra_gp[1])
            for c in range(C8):
                ins = g.tensor_copy(chw(xbf_t, c, W), xap(c))
            k.inc(ins, "gp", 1)
        GP(_gp); gp_cast = k.c["gp"]

        def _pe1(t):
            k.w(t, "gp", gp_cast)
            k.w(t, "dv", prev_apply_dv)
            for c in range(C8):
                ins = t.matmul(ps[2][0:1, 0:W], ones_cb.ap(), chw(xbf_t, c, W),
                               start=(c == 0), stop=(c == C8 - 1))
            k.inc(ins, "pe", 1)
        TE(_pe1); pe_s1 = k.c["pe"]

        def _ac(a):
            k.w(a, "pe", pe_s1)
            if extra_ac is not None:
                k.w(a, extra_ac[0], extra_ac[1])
            for c in range(C8):
                ins = a.activation(chw(xbf_t, c, W), xap(c), AF.Square)
            k.inc(ins, "ac", 1)
        SC(_ac); ac_sq = k.c["ac"]

        def _pe2(t):
            k.w(t, "ac", ac_sq)
            for c in range(C8):
                ins = t.matmul(ps[3][0:1, 0:W], ones_cb.ap(), chw(xbf_t, c, W),
                               start=(c == 0), stop=(c == C8 - 1))
            k.inc(ins, "pe", 1)
        TE(_pe2); pe_s2 = k.c["pe"]

        def _dvr(v):
            k.w(v, "pe", pe_s2)
            rm = rowm.ap()[:, 0:W]; rq = rowq.ap()[:, 0:W]; rt = rowt.ap()[:, 0:W]
            v.tensor_scalar(rm, ps[2][0:1, 0:W], 1.0 / D, None, op0=OP.mult)
            v.tensor_scalar(rq, ps[3][0:1, 0:W], 1.0 / D, None, op0=OP.mult)
            v.drain()
            v.scalar_tensor_tensor(rt, rm, -1.0, rm, op0=OP.mult, op1=OP.mult)
            v.drain()
            v.tensor_tensor(rt, rq, rt, op=OP.add)
            k.inc(v.drain(), "dv", 1)
        VE(_dvr); dv_r1 = k.c["dv"]

        def _acs(a):
            k.w(a, "dv", dv_r1)
            ins = a.activation(rowt.ap()[0:1, 0:W], rowt.ap()[0:1, 0:W],
                               AF.Sqrt, bias=cst.ap()[0:1, 0:1])
            k.inc(ins, "ac", 1)
        SC(_acs); ac_sd = k.c["ac"]

        def _dvr2(v):
            k.w(v, "ac", ac_sd)
            rt = rowt.ap()[:, 0:W]
            v.reciprocal(rt, rt)
            v.drain()
            v.tensor_copy(rhiA.ap()[:, 0:W], rowm.ap()[:, 0:W])
            v.tensor_copy(rhiB.ap()[:, 0:W], rt)
            v.drain()
            v.tensor_tensor(rloAb.ap()[:, 0:W], rowm.ap()[:, 0:W], rhiA.ap()[:, 0:W],
                            op=OP.subtract)
            v.tensor_tensor(rloBb.ap()[:, 0:W], rt, rhiB.ap()[:, 0:W], op=OP.subtract)
            k.inc(v.drain(), "dv", 1)
        VE(_dvr2); dv_r2 = k.c["dv"]

        def _pebc(t):
            k.w(t, "dv", dv_r2)
            t.matmul(ps[4][:, 0:W], ones_rb.ap(), rhiA.ap()[:, 0:W], start=True, stop=False)
            t.matmul(ps[4][:, 0:W], ones_rb.ap(), rloAb.ap()[:, 0:W], start=False, stop=True)
            t.matmul(ps[5][:, 0:W], ones_rb.ap(), rhiB.ap()[:, 0:W], start=True, stop=False)
            ins = t.matmul(ps[5][:, 0:W], ones_rb.ap(), rloBb.ap()[:, 0:W], start=False, stop=True)
            k.inc(ins, "pe", 1)
        TE(_pebc); pe_bc = k.c["pe"]

        def _dva(v):
            k.w(v, "pe", pe_bc)
            t0 = tmpf.ap()[:, 0:W]; t1 = tmp2.ap()[:, 0:W]
            for c0 in range(0, C8, 2):
                v.tensor_tensor(t0, xap(c0), ps[4][:, 0:W], op=OP.subtract)
                v.tensor_tensor(t1, xap(c0 + 1), ps[4][:, 0:W], op=OP.subtract)
                v.drain()
                v.tensor_tensor(out_fn(c0), t0, ps[5][:, 0:W], op=OP.mult)
                v.tensor_tensor(out_fn(c0 + 1), t1, ps[5][:, 0:W], op=OP.mult)
                v.drain()
                v.tensor_scalar(out_fn(c0), out_fn(c0), s1c(c0), s2c(c0),
                                op0=OP.mult, op1=OP.add)
                v.tensor_scalar(out_fn(c0 + 1), out_fn(c0 + 1), s1c(c0 + 1), s2c(c0 + 1),
                                op0=OP.mult, op1=OP.add)
                v.drain()
            k.inc(v.nop(), "dv", 1)
        VE(_dva)
        return k.c["dv"]


    def emit_sincos_multi(chains, W, pre_dv=0):
        ac_prev = k.c["ac"]

        def _dv1(v):
            k.w(v, "ac", ac_prev)
            k.w(v, "dv", pre_dv)
            for th, co, si, u, n in chains:
                v.tensor_scalar(n, th, INV2PI, MAGIC, op0=OP.mult, op1=OP.add)
            v.drain()
            for th, co, si, u, n in chains:
                v.tensor_scalar(n, n, MAGIC, None, op0=OP.subtract)
            v.drain()
            k.inc(v.nop(), "dv", 1)
        VE(_dv1); e1 = k.c["dv"]

        def _gpf(g):
            k.w(g, "dv", e1)
            for th, co, si, u, n in chains:
                ins = g.tensor_copy(n.bitcast(mybir.dt.int32), n.bitcast(mybir.dt.int32))
            k.inc(ins, "gp", 1)
        GP(_gpf); e2 = k.c["gp"]

        def _dv2(v):
            k.w(v, "gp", e2)
            for th, co, si, u, n in chains:
                v.scalar_tensor_tensor(u, n, -CW1, th, op0=OP.mult, op1=OP.add)
            v.drain()
            for th, co, si, u, n in chains:
                v.scalar_tensor_tensor(u, n, -CW2, u, op0=OP.mult, op1=OP.add)
            v.drain()
            for th, co, si, u, n in chains:
                v.tensor_scalar(u, u, PICL, -PICL, op0=OP.min, op1=OP.max)
            v.drain()
            k.inc(v.nop(), "dv", 1)
        VE(_dv2); e3 = k.c["dv"]

        def _ac1(a):
            k.w(a, "dv", e3)
            for th, co, si, u, n in chains:
                ins = a.activation(si, u, AF.Sin)
            k.inc(ins, "ac", 1)
        SC(_ac1); e4 = k.c["ac"]

        def _dv3(v):
            k.w(v, "ac", e4)
            for th, co, si, u, n in chains:
                v.tensor_scalar(n, th, INV2PI, 0.25, op0=OP.mult, op1=OP.add)
            v.drain()
            for th, co, si, u, n in chains:
                v.tensor_scalar(n, n, MAGIC, None, op0=OP.add)
            v.drain()
            for th, co, si, u, n in chains:
                v.tensor_scalar(n, n, MAGIC, None, op0=OP.subtract)
            v.drain()
            k.inc(v.nop(), "dv", 1)
        VE(_dv3); e5 = k.c["dv"]

        def _gpf2(g):
            k.w(g, "dv", e5)
            for th, co, si, u, n in chains:
                ins = g.tensor_copy(n.bitcast(mybir.dt.int32), n.bitcast(mybir.dt.int32))
            k.inc(ins, "gp", 1)
        GP(_gpf2); e6 = k.c["gp"]

        def _dv4(v):
            k.w(v, "gp", e6)
            for th, co, si, u, n in chains:
                v.scalar_tensor_tensor(u, n, -CW1, th, op0=OP.mult, op1=OP.add)
            v.drain()
            for th, co, si, u, n in chains:
                v.scalar_tensor_tensor(u, n, -CW2, u, op0=OP.mult, op1=OP.add)
            v.drain()
            for th, co, si, u, n in chains:
                v.tensor_scalar(u, u, PICL - HALFPI, -PICL - HALFPI, op0=OP.min, op1=OP.max)
            v.drain()
            k.inc(v.nop(), "dv", 1)
        VE(_dv4); e7 = k.c["dv"]

        def _ac2(a):
            k.w(a, "dv", e7)
            for th, co, si, u, n in chains:
                ins = a.activation(co, u, AF.Sin, bias=cst.ap()[:, 1:2])
            k.inc(ins, "ac", 1)
        SC(_ac2)
        return k.c["ac"]

    def emit_sincos(th_ap, cos_out, sin_out, W, u_f, n_f):
        ac_prev = k.c["ac"]

        def _dv1(v):
            k.w(v, "ac", ac_prev)
            v.tensor_scalar(n_f, th_ap, INV2PI, MAGIC, op0=OP.mult, op1=OP.add)
            v.drain()
            v.tensor_scalar(n_f, n_f, MAGIC, None, op0=OP.subtract)
            v.drain()
            k.inc(v.nop(), "dv", 1)
        VE(_dv1); e1 = k.c["dv"]

        def _gpf(g):
            k.w(g, "dv", e1)
            ins = g.tensor_copy(n_f.bitcast(mybir.dt.int32), n_f.bitcast(mybir.dt.int32))
            k.inc(ins, "gp", 1)
        GP(_gpf); e2 = k.c["gp"]

        def _dv2(v):
            k.w(v, "gp", e2)
            v.scalar_tensor_tensor(u_f, n_f, -CW1, th_ap, op0=OP.mult, op1=OP.add)
            v.drain()
            v.scalar_tensor_tensor(u_f, n_f, -CW2, u_f, op0=OP.mult, op1=OP.add)
            v.drain()
            v.tensor_scalar(u_f, u_f, PICL, -PICL, op0=OP.min, op1=OP.max)
            v.drain()
            k.inc(v.nop(), "dv", 1)
        VE(_dv2); e3 = k.c["dv"]

        def _ac1(a):
            k.w(a, "dv", e3)
            ins = a.activation(sin_out, u_f, AF.Sin)
            k.inc(ins, "ac", 1)
        SC(_ac1); e4 = k.c["ac"]

        def _dv3(v):
            k.w(v, "ac", e4)
            v.tensor_scalar(n_f, th_ap, INV2PI, 0.25, op0=OP.mult, op1=OP.add)
            v.drain()
            v.tensor_scalar(n_f, n_f, MAGIC, None, op0=OP.add)
            v.drain()
            v.tensor_scalar(n_f, n_f, MAGIC, None, op0=OP.subtract)
            v.drain()
            k.inc(v.nop(), "dv", 1)
        VE(_dv3); e5 = k.c["dv"]

        def _gpf2(g):
            k.w(g, "dv", e5)
            ins = g.tensor_copy(n_f.bitcast(mybir.dt.int32), n_f.bitcast(mybir.dt.int32))
            k.inc(ins, "gp", 1)
        GP(_gpf2); e6 = k.c["gp"]

        def _dv4(v):
            k.w(v, "gp", e6)
            v.scalar_tensor_tensor(u_f, n_f, -CW1, th_ap, op0=OP.mult, op1=OP.add)
            v.drain()
            v.scalar_tensor_tensor(u_f, n_f, -CW2, u_f, op0=OP.mult, op1=OP.add)
            v.drain()
            v.tensor_scalar(u_f, u_f, PICL - HALFPI, -PICL - HALFPI, op0=OP.min, op1=OP.max)
            v.drain()
            k.inc(v.nop(), "dv", 1)
        VE(_dv4); e7 = k.c["dv"]

        def _ac2(a):
            k.w(a, "dv", e7)
            ins = a.activation(cos_out, u_f, AF.Sin, bias=cst.ap()[:, 1:2])
            k.inc(ins, "ac", 1)
        SC(_ac2)
        return k.c["ac"]

    def _sy0(sy):
        for src, dst in [(ident_in, identf.ap()), (conv_w_in, convw.ap()),
                         (conv_b_in, convb.ap()), (b_dt_in, bdt.ap()),
                         (gate_b_in, gateb.ap()),
                         (bq_in, bq_s.ap()), (bk_in, bk_s.ap()), (bv_in, bv_s.ap()),
                         (bo_in, bo_s.ap()), (wavel_in, invw.ap()), (phaseb_in, phb.ap()),
                         (b1_in, b1_s.ap()), (b2_in, b2_s.ap()),
                         (gamma_in, gam_s.ap()), (beta_in, bet_s.ap()),
                         (halomask, hmsk.ap())]:
            k.inc(sy.dma_start(out=dst, in_=src), "ld", 16)
        k.inc(sy.dma_start(out=gatewt.ap().rearrange("p (c e) -> p c e", e=16),
                           in_=gate_w_t.rearrange("(c p) e -> p c e", p=128)), "ld", 16)
        k.inc(sy.dma_start(out=cmod_s.ap().rearrange("p (s c) -> p s c", c=8),
                           in_=cmod.rearrange("s p c -> p s c")), "ld", 16)
    SY(_sy0)
    ld0 = k.c["ld"]

    AR.reset(P_BASE)
    w_in_bf = AR.t("w_in_bf", [128, C8 * 4096], BF16)
    w_out_bf = AR.t("w_out_bf", [128, NDI * 1024], BF16)
    w_x32_bf = AR.t("w_x32_bf", [128, NDI * DTR], BF16)
    w_dt_bf = AR.t("w_dt_bf", [DTR, DI], BF16)

    def _gp0(g):
        k.inc(g.dma_start(out=w_in_bf.ap().rearrange("p (c f) -> p c f", f=4096),
                          in_=w_in_t.rearrange("(c p) f -> p c f", p=128)), "gd", 16)
        k.inc(g.dma_start(out=w_out_bf.ap().rearrange("p (c f) -> p c f", f=1024),
                          in_=w_out_t.rearrange("(c p) f -> p c f", p=128)), "gd", 16)
        k.inc(g.dma_start(out=w_x32_bf.ap().rearrange("p (c f) -> p c f", f=DTR),
                          in_=w_x32_t.rearrange("(c p) f -> p c f", p=128)), "gd", 16)
        k.inc(g.dma_start(out=w_dt_bf.ap(), in_=w_dt_t), "gd", 16)
        for i, wsrc in enumerate((wq_t, wk_t, wv_t)):
            k.inc(g.dma_start(
                out=wqkv_bf.ap().rearrange("p (c i f) -> p i c f", i=3, f=HD)[:, i],
                in_=wsrc.rearrange("(c p) f -> p c f", p=128)), "gd", 16)
        k.inc(g.dma_start(out=wo_bf.ap(), in_=wo_t), "gd", 16)
        k.inc(g.dma_start(out=eall.ap(), in_=eall_in), "gd", 16)
    GP(_gp0)
    gd0 = k.c["gd"]

    def _v0(v):
        k.w(v, "ld", ld0)
        v.memset(ones_cb.ap(), 1.0)
        v.memset(ones_cf.ap(), 1.0)
        v.memset(ones_rb.ap(), 1.0)
        v.memset(ones_rf.ap(), 1.0)
        v.memset(cst.ap()[:, 0:1], EPS)
        v.memset(cst.ap()[:, 1:2], HALFPI)
        v.memset(wmean.ap(), 0.0)
        v.tensor_copy(identb.ap(), identf.ap())
        v.tensor_scalar(invw.ap(), invw.ap(), 1e-8, None, op0=OP.add)
        v.drain()
        v.reciprocal(invw.ap(), invw.ap())
        v.drain()
        k.inc(v.nop(), "dv", 1)
    VE(_v0)
    dv0 = k.c["dv"]

    ph0_mark = AR.mark()
    wms = AR.t("wms", [128, C8 * 1024], F32)
    silc = AR.t("silc", [128, 8], F32)
    modrow = AR.t("modrow", [1, 1024], F32)

    for sl in (0, 1):
        def _sy(sy, sl=sl):
            k.w(sy, "pe", k.c["pe"])
            k.inc(sy.dma_start(out=wms.ap().rearrange("p (c f) -> p c f", f=1024),
                               in_=wmodsl[sl].rearrange("(c p) f -> p c f", p=128)), "ld", 16)
        SY(_sy); ldv = k.c["ld"]

        def _ac(a, sl=sl, ldv=ldv):
            k.w(a, "ld", ldv)
            k.inc(a.activation(silc.ap(), cmod_s.ap()[:, sl * 8:(sl + 1) * 8], AF.Silu), "ac", 1)
        SC(_ac); acv = k.c["ac"]

        def _pe(t, ldv=ldv, acv=acv):
            k.w(t, "ld", ldv)
            k.w(t, "ac", acv)
            k.w(t, "dv", max(dv0, k.c["dv"]))
            for half in (0, 1):
                for kk in range(C8):
                    ins = t.matmul(ps[half][0:1, :], silc.ap()[:, kk:kk + 1],
                                   chw(wms, kk, 1024)[:, half * 512:(half + 1) * 512],
                                   start=(kk == 0), stop=(kk == C8 - 1))
            k.inc(ins, "pe", 1)
        TE(_pe); pev = k.c["pe"]

        def _dv(v, pev=pev):
            k.w(v, "pe", pev)
            v.tensor_copy(modrow.ap()[:, 0:512], ps[0][0:1, :])
            k.inc(v.tensor_copy(modrow.ap()[:, 512:1024], ps[1][0:1, :]), "dv", 1)
        VE(_dv); dvv = k.c["dv"]

        def _sy2(sy, sl=sl, dvv=dvv):
            k.w(sy, "dv", dvv)
            k.inc(sy.dma_start(out=modin[sl:sl + 1, :], in_=modrow.ap()), "sxr", 16)
        SY(_sy2)
    sxr_mod = k.c["sxr"]

    def _gpcc(g):
        k.w(g, "sxr", sxr_mod)
        k.inc(g.collective_compute(
            "AllGather", OP.bypass, replica_groups=[list(range(NCOR))],
            ins=[modin.ap().opt()], outs=[modout.ap().opt()]), "cc", 1)
    GP(_gpcc)
    cc_mod = k.c["cc"]

    names6 = ("sa", "ca1", "ga", "sb", "cb1", "gb")
    def _symod(sy):
        k.w(sy, "cc", cc_mod)
        for l in (0, 1):
            for vv in range(6):
                s = 6 * l + vv
                row = 2 * s if s < 8 else 2 * (s - 8) + 1
                k.inc(sy.dma_start(
                    out=modv[(l, names6[vv])].ap(),
                    in_=modout.ap()[row:row + 1, :].rearrange("o (p c) -> o p c", p=128)[0]),
                    "ld", 16)
    SY(_symod)
    ld_mod = k.c["ld"]

    def _vca(v):
        k.w(v, "ld", ld_mod)
        for l in (0, 1):
            v.tensor_scalar(modv[(l, "ca1")].ap(), modv[(l, "ca1")].ap(), 1.0, None, op0=OP.add)
            v.tensor_scalar(modv[(l, "cb1")].ap(), modv[(l, "cb1")].ap(), 1.0, None, op0=OP.add)
        v.drain()
        k.inc(v.nop(), "dv", 1)
    VE(_vca)

    for l in (0, 1):
        for gn in ("ga", "gb"):
            def _acg(a, l=l, gn=gn):
                k.w(a, "ld", ld_mod)
                k.w(a, "dv", k.c["dv"])
                a.activation(gtmp.ap(), modv[(l, gn)].ap(), AF.Abs)
                a.drain()
                a.activation(gtmp2.ap(), modv[(l, gn)].ap(), AF.Sign)
                a.drain()
                k.inc(a.nop(), "ac", 1)
            SC(_acg); acv = k.c["ac"]

            def _vg(v, l=l, gn=gn, acv=acv):
                k.w(v, "ac", acv)
                v.tensor_scalar(gtmp.ap(), gtmp.ap(), 0.25, None, op0=OP.max)
                v.drain()
                v.tensor_tensor(modv[(l, gn)].ap(), gtmp2.ap(), gtmp.ap(), op=OP.mult)
                v.drain()
                k.inc(v.nop(), "dv", 1)
            VE(_vg)

    AR.reset(ph0_mark)
    k.barrier()

    if "modv" in dbg:
        def _symv(sy):
            i = 0
            for l in (0, 1):
                for nm in names6:
                    k.inc(sy.dma_start(out=dbg["modv"][0:128, i * 8:(i + 1) * 8],
                                       in_=modv[(l, nm)].ap()), "sto", 16)
                    i += 1
        SY(_symv)
        k.barrier()

    tap_points = {}

    def emit_tap(name):
        if name not in dbg:
            return
        def _syt(sy, name=name):
            k.inc(sy.dma_start(out=dbg[name], in_=xres), "sto", 16)
        SY(_syt)
        k.barrier()

    if n_phases >= 1:
        ph1_mark = AR.mark()
        _xt0 = AR.t("xtl0", [128, C8 * TT], F32)
        xtile = [_xt0, _xt0]
        h_bf = AR.t("h_bf", [128, C8 * TT], BF16)
        zs_b = AR.t("zs_b", [128, NDI * TT], BF16)
        xbf = AR.at(AR.mark() - NDI * TT * 2, "xbf", [128, C8 * TT], BF16)
        xs_b = AR.t("xs_b", [128, NDI * TT], BF16)
        tail_b = AR.t("tail_b", [128, NDI * 3], BF16)
        dt_b = AR.t("dt_b", [DTR, TT], BF16)
        scr0 = AR.mark()
        tmpD = AR.at(scr0, "tmpD", [128, 2 * TT], F32)
        tmpL = AR.at(scr0, "tmpL", [128, 2 * TT], F32)
        tmpG = AR.at(scr0 + 4096, "tmpG", [128, 2 * TT], F32)
        avf = AR.at(scr0 + 4096, "avf", [128, 2 * TT], F32)
        tGm = AR.at(scr0 + 8192, "tGm", [128, 2 * TT], F32)
        tmpG2 = AR.at(scr0 + 8192, "tmpG2", [128, TT], F32)
        tmpA = AR.at(scr0 + 8192 + 2048, "tmpA", [128, TT], F32)
        hhalo = AR.at(scr0, "hhalo", [128, C8 * 4], BF16)
        xhal = AR.at(scr0 + 256, "xhal", [128, C8 * 4], F32)
        xsqh = AR.at(scr0 + 512, "xsqh", [128, C8 * 4], F32)
        hrow = AR.at(scr0 + 768, "hrow", [1, 16], F32)
        xshT = AR.at(scr0 + 1024, "xshT", [3, 512], F32)
        AR.reset(scr0 + 12288)
        lnt2a = AR.t("lnt2a", [128, TT], F32)
        assert AR.mark() <= 212480

        def _syh(sy):
            k.inc(sy.dma_start(
                out=xhal.ap().rearrange("p (c w) -> p c w", w=4)[:, :, 0:3],
                in_=xprevT.rearrange("(c p) t -> p c t", p=128)), "ld", 16)
        SY(_syh); ld_h = k.c["ld"]

        def _peh1(t):
            k.w(t, "ld", ld_h)
            for c in range(C8):
                ins = t.matmul(ps[4][0:1, 0:3], ones_cf.ap(), chw(xhal, c, 4)[:, 0:3],
                               start=(c == 0), stop=(c == C8 - 1))
            k.inc(ins, "pe", 1)
        TE(_peh1); pe_h1 = k.c["pe"]

        def _ach(a):
            k.w(a, "ld", ld_h)
            for c in range(C8):
                ins = a.activation(chw(xsqh, c, 4)[:, 0:3], chw(xhal, c, 4)[:, 0:3], AF.Square)
            k.inc(ins, "ac", 1)
        SC(_ach); ac_h = k.c["ac"]

        def _peh2(t):
            k.w(t, "ac", ac_h)
            for c in range(C8):
                ins = t.matmul(ps[5][0:1, 0:3], ones_cf.ap(), chw(xsqh, c, 4)[:, 0:3],
                               start=(c == 0), stop=(c == C8 - 1))
            k.inc(ins, "pe", 1)
        TE(_peh2); pe_h2 = k.c["pe"]

        def _dvh(v):
            k.w(v, "pe", pe_h2)
            rm = hrow.ap()[:, 0:3]; rq = hrow.ap()[:, 4:7]; rt = hrow.ap()[:, 8:11]
            v.tensor_scalar(rm, ps[4][0:1, 0:3], 1.0 / D, None, op0=OP.mult)
            v.tensor_scalar(rq, ps[5][0:1, 0:3], 1.0 / D, None, op0=OP.mult)
            v.drain()
            v.tensor_tensor(rt, rm, rm, op=OP.mult)
            v.drain()
            v.tensor_tensor(rt, rq, rt, op=OP.subtract)
            v.drain()
            k.inc(v.nop(), "dv", 1)
        VE(_dvh); dv_h1 = k.c["dv"]

        def _ach2(a):
            k.w(a, "dv", dv_h1)
            ins = a.activation(hrow.ap()[0:1, 8:11], hrow.ap()[0:1, 8:11],
                               AF.Sqrt, bias=cst.ap()[0:1, 0:1])
            k.inc(ins, "ac", 1)
        SC(_ach2); ac_h2 = k.c["ac"]

        def _dvh2(v):
            k.w(v, "ac", ac_h2)
            v.reciprocal(hrow.ap()[:, 8:11], hrow.ap()[:, 8:11])
            v.drain()
            v.tensor_copy(rhiA.ap()[:, 0:3], hrow.ap()[:, 0:3])
            v.tensor_copy(rhiB.ap()[:, 0:3], hrow.ap()[:, 8:11])
            v.drain()
            v.tensor_tensor(rloA.ap()[:, 0:3], hrow.ap()[:, 0:3], rhiA.ap()[:, 0:3], op=OP.subtract)
            v.tensor_tensor(rloB.ap()[:, 0:3], hrow.ap()[:, 8:11], rhiB.ap()[:, 0:3], op=OP.subtract)
            v.drain()
            v.tensor_copy(rloAb.ap()[:, 0:3], rloA.ap()[:, 0:3])
            v.tensor_copy(rloBb.ap()[:, 0:3], rloB.ap()[:, 0:3])
            k.inc(v.nop(), "dv", 1)
        VE(_dvh2); dv_h2 = k.c["dv"]

        def _peh3(t):
            k.w(t, "dv", dv_h2)
            t.matmul(ps[4][:, 4:7], ones_rb.ap(), rhiA.ap()[:, 0:3], start=True, stop=False)
            t.matmul(ps[4][:, 4:7], ones_rb.ap(), rloAb.ap()[:, 0:3], start=False, stop=True)
            t.matmul(ps[5][:, 4:7], ones_rb.ap(), rhiB.ap()[:, 0:3], start=True, stop=False)
            ins = t.matmul(ps[5][:, 4:7], ones_rb.ap(), rloBb.ap()[:, 0:3], start=False, stop=True)
            k.inc(ins, "pe", 1)
        TE(_peh3); pe_h3 = k.c["pe"]

        def _dvh3(v):
            k.w(v, "pe", pe_h3)
            for c in range(C8):
                tf = tmpA.ap()[:, 0:3]
                v.tensor_tensor(tf, chw(xhal, c, 4)[:, 0:3], ps[4][:, 4:7], op=OP.subtract)
                v.drain()
                v.tensor_tensor(tf, tf, ps[5][:, 4:7], op=OP.mult)
                v.drain()
                v.tensor_scalar(chw(hhalo, c, 4)[:, 0:3], tf,
                                modv[(0, "ca1")].ap()[:, c:c + 1],
                                modv[(0, "sa")].ap()[:, c:c + 1],
                                op0=OP.mult, op1=OP.add)
                v.drain()
            k.inc(v.nop(), "dv", 1)
        VE(_dvh3); dv_h3 = k.c["dv"]

        def _peh4(t):
            k.w(t, "dv", dv_h3)
            k.w(t, "gd", 16)
            for kk in range(C8):
                for ft in range(4):
                    ins = t.matmul(ps[ft][0:3, :], chw(hhalo, kk, 4)[:, 0:3],
                                   chw(w_in_bf, kk, 4096)[:, ft * 512:(ft + 1) * 512],
                                   start=(kk == 0), stop=(kk == C8 - 1))
            k.inc(ins, "pe", 1)
        TE(_peh4); pe_h4 = k.c["pe"]

        for ft in range(4):
            def _dvh4(v, ft=ft):
                k.w(v, "pe", pe_h4)
                v.tensor_copy(xshT.ap()[0:3, :], ps[ft][0:3, :])
                v.drain()
                k.inc(v.nop(), "dv", 1)
            VE(_dvh4); dv_x = k.c["dv"]

            def _peh5(t, dv_x=dv_x):
                k.w(t, "dv", dv_x)
                for c4 in range(4):
                    ins = t.transpose(ps[6][:, c4 * 4:c4 * 4 + 3],
                                      xshT.ap()[0:3, c4 * 128:(c4 + 1) * 128],
                                      identf.ap()[0:3, 0:3])
                k.inc(ins, "pe", 1)
            TE(_peh5); pe_x = k.c["pe"]

            def _dvh5(v, ft=ft, pe_x=pe_x):
                k.w(v, "pe", pe_x)
                for c4 in range(4):
                    c = ft * 4 + c4
                    v.tensor_scalar(chw(tail_b, c, 3), ps[6][:, c4 * 4:c4 * 4 + 3],
                                    hmsk.ap(), None, op0=OP.mult)
                v.drain()
                k.inc(v.nop(), "dv", 1)
            VE(_dvh5)

        k.barrier()

        store_ev = [0, 0]
        prev_apply = 0
        prev_y2gp = 0
        for j in range(NT):
            slot = 0
            lsem = "lxa" if j % 2 == 0 else "lxb"

            def _syx(sy, j=j, slot=slot, lsem=lsem):
                k.w(sy, "sxr", store_ev[slot])
                k.inc(sy.dma_start(
                    out=xtile[slot].ap().rearrange("p (c w) -> p c w", w=TT),
                    in_=xT.rearrange("(c p) t -> p c t", p=128)[:, :, j * TT:(j + 1) * TT]),
                    lsem, 16)
            SY(_syx)
            lx_ev = k.c[lsem]

            def xap(c, slot=slot):
                return chw(xtile[slot], c, TT)

            hb = emit_ln(xap, xbf,
                         lambda c: chw(h_bf, c, TT),
                         lambda c: modv[(0, "ca1")].ap()[:, c:c + 1],
                         lambda c: modv[(0, "sa")].ap()[:, c:c + 1],
                         TT, tmpA, lnt2a, prev_apply,
                         extra_gp=(lsem, lx_ev),
                         extra_ac=("gp", prev_y2gp))
            prev_apply = hb

            pe_base = k.c["pe"]
            dv_base = k.c["dv"]
            ac_base = k.c["ac"]
            cons_pred = []
            dvp, acp = dv_base, ac_base
            for m in range(32):
                if m < 16:
                    dvp += 1; cons_pred.append(("dv", dvp))
                else:
                    acp += 1; cons_pred.append(("ac", acp))

            def _pexz(t, hb=hb, cons_pred=cons_pred):
                k.w(t, "dv", hb)
                k.w(t, "gd", 16)
                for m in range(32):
                    if m >= 2:
                        s, vv = cons_pred[m - 2]
                        k.w(t, s, vv)
                    for kk in range(C8):
                        ins = t.matmul(ps[m % 2][:, 0:TT],
                                       chw(w_in_bf, kk, 4096)[:, m * 128:(m + 1) * 128],
                                       chw(h_bf, kk, TT),
                                       start=(kk == 0), stop=(kk == C8 - 1))
                    k.inc(ins, "pe", 1)
            TE(_pexz)

            def _dvxz(v, pe_base=pe_base):
                for m in range(16):
                    k.w(v, "pe", pe_base + m + 1)
                    v.tensor_copy(chw(xs_b, m, TT), ps[m % 2][:, 0:TT])
                    k.inc(v.drain(), "dv", 1)
            VE(_dvxz)

            def _acxz(a, pe_base=pe_base, prev_y2gp=prev_y2gp):
                k.w(a, "gp", prev_y2gp)
                for m in range(16, 32):
                    k.w(a, "pe", pe_base + m + 1)
                    ins = a.activation(chw(zs_b, m - 16, TT), ps[m % 2][:, 0:TT], AF.Silu)
                    k.inc(ins, "ac", 1)
            SC(_acxz)
            xz_ac_done = cons_pred[31][1]

            wv = convw.ap()

            def conv_emit(eng, c, accap, is_gp, scr):
                first = True
                for kk in range(4):
                    n_tail = 3 - kk
                    if n_tail > 0:
                        seg_t = tail_b.ap()[:, c * 3 + (3 - n_tail):c * 3 + 3]
                        if first:
                            eng.tensor_scalar(accap[:, 0:n_tail], seg_t,
                                              wv[:, c * 4 + kk:c * 4 + kk + 1], 0.0,
                                              op0=OP.mult, op1=OP.add)
                        elif is_gp:
                            eng.tensor_scalar(scr[:, 0:n_tail], seg_t,
                                              wv[:, c * 4 + kk:c * 4 + kk + 1], 0.0,
                                              op0=OP.mult, op1=OP.add)
                            eng.drain()
                            eng.tensor_tensor(accap[:, 0:n_tail], accap[:, 0:n_tail],
                                              scr[:, 0:n_tail], op=OP.add)
                        else:
                            eng.scalar_tensor_tensor(accap[:, 0:n_tail], seg_t,
                                                     wv[:, c * 4 + kk:c * 4 + kk + 1],
                                                     accap[:, 0:n_tail],
                                                     op0=OP.mult, op1=OP.add)
                        eng.drain()
                    seg_x = chw(xs_b, c, TT)[:, 0:TT - n_tail]
                    oseg = accap[:, n_tail:TT]
                    if first:
                        eng.tensor_scalar(oseg, seg_x, wv[:, c * 4 + kk:c * 4 + kk + 1],
                                          0.0, op0=OP.mult, op1=OP.add)
                        first = False
                    elif is_gp:
                        eng.tensor_scalar(scr[:, n_tail:TT], seg_x,
                                          wv[:, c * 4 + kk:c * 4 + kk + 1], 0.0,
                                          op0=OP.mult, op1=OP.add)
                        eng.drain()
                        eng.tensor_tensor(oseg, oseg, scr[:, n_tail:TT], op=OP.add)
                    else:
                        eng.scalar_tensor_tensor(oseg, seg_x,
                                                 wv[:, c * 4 + kk:c * 4 + kk + 1],
                                                 oseg, op0=OP.mult, op1=OP.add)
                    eng.drain()

            silu_ev = {}
            convD = [0, 2, 4, 6, 8, 10, 12, 14]
            convG = [1, 3, 5, 7, 9, 11, 13, 15]
            dorder = []
            for i in range(8):
                dorder.append(convD[i]); dorder.append(convG[i])
            for idx, c in enumerate(dorder):
                is_gp = c in convG
                sidx = (c // 2) % 2
                if is_gp:
                    def _g(g, c=c, sidx=sidx):
                        k.w(g, "dv", dv_base + c + 1)
                        prevc = c - 4
                        if prevc in silu_ev:
                            k.w(g, "ac", silu_ev[prevc])
                        acc = tmpG.ap()[:, sidx * TT:(sidx + 1) * TT]
                        conv_emit(g, c, acc, True, tmpG2.ap()[:, 0:TT])
                        g.tensor_copy(tail_b.ap()[:, c * 3:(c + 1) * 3],
                                      chw(xs_b, c, TT)[:, TT - 3:TT])
                        k.inc(g.drain(), "gp", 1)
                    GP(_g); cev = ("gp", k.c["gp"])
                else:
                    def _v(v, c=c, sidx=sidx):
                        prevc = c - 4
                        if prevc in silu_ev:
                            k.w(v, "ac", silu_ev[prevc])
                        acc = tmpD.ap()[:, sidx * TT:(sidx + 1) * TT]
                        conv_emit(v, c, acc, False, None)
                        v.tensor_copy(tail_b.ap()[:, c * 3:(c + 1) * 3],
                                      chw(xs_b, c, TT)[:, TT - 3:TT])
                        k.inc(v.drain(), "dv", 1)
                    VE(_v); cev = ("dv", k.c["dv"])

                def _a(a, c=c, sidx=sidx, cev=cev, is_gp=is_gp):
                    k.w(a, cev[0], cev[1])
                    src = (tmpG if is_gp else tmpD).ap()[:, sidx * TT:(sidx + 1) * TT]
                    ins = a.activation(chw(xs_b, c, TT), src, AF.Silu,
                                       bias=convb.ap()[:, c:c + 1])
                    k.inc(ins, "ac", 1)
                SC(_a)
                silu_ev[c] = k.c["ac"]
            xc_ready = k.c["ac"]

            if "p1" in dbg and j == 0:
                def _syd1(g, xc_ready=xc_ready):
                    k.w(g, "ac", xc_ready)
                    k.w(g, "dv", k.c["dv"])
                    for c in range(C8):
                        k.inc(g.dma_start(out=dbg["p1"][c * 128:(c + 1) * 128, 0:512],
                                          in_=chw(h_bf, c, TT)), "sto", 16)
                        k.inc(g.dma_start(out=dbg["p1"][c * 128:(c + 1) * 128, 512:1024],
                                          in_=chw(zs_b, c, TT)), "sto", 16)
                        k.inc(g.dma_start(out=dbg["p1"][c * 128:(c + 1) * 128, 1024:1536],
                                          in_=chw(xs_b, c, TT)), "sto", 16)
                GP(_syd1)
                dmp_ev = k.c["sto"]
                def _ped(t, dmp_ev=dmp_ev):
                    k.w(t, "sto", dmp_ev)
                TE(_ped)
                def _gpd(g, dmp_ev=dmp_ev):
                    k.w(g, "sto", dmp_ev)
                GP(_gpd)

            def _pedt(t, xc_ready=xc_ready):
                k.w(t, "ac", xc_ready)
                k.w(t, "gd", 64)
                for kk in range(NDI):
                    ins = t.matmul(ps[6][0:DTR, 0:TT],
                                   chw(w_x32_bf, kk, DTR),
                                   chw(xs_b, kk, TT),
                                   start=(kk == 0), stop=(kk == NDI - 1))
                k.inc(ins, "pe", 1)
            TE(_pedt); pe_dt = k.c["pe"]

            def _dvdt(v, pe_dt=pe_dt):
                k.w(v, "pe", pe_dt)
                v.tensor_copy(dt_b.ap(), ps[6][0:DTR, 0:TT])
                k.inc(v.drain(), "dv", 1)
            VE(_dvdt); dv_dt = k.c["dv"]

            pa_base = k.c["pe"]
            aa_base = k.c["ac"]
            ga_base = k.c["gp"]

            def _pea(t, dv_dt=dv_dt, aa_base=aa_base):
                k.w(t, "dv", dv_dt)
                for m in range(NDI):
                    if m >= 4:
                        k.w(t, "ac", aa_base + 4 * (m - 4) + 4)
                    bank = [0, 1, 6, 7][m % 4]
                    ins = t.matmul(ps[bank][0:128, 0:TT],
                                   w_dt_bf.ap()[:, m * 128:(m + 1) * 128],
                                   dt_b.ap(),
                                   start=True, stop=True)
                    k.inc(ins, "pe", 1)
            TE(_pea)

            def _aca(a, pa_base=pa_base, ga_base=ga_base):
                for m in range(NDI):
                    k.w(a, "pe", pa_base + m + 1)
                    if m >= 2:
                        k.w(a, "gp", ga_base + (m - 2) + 1)
                    bank = [0, 1, 6, 7][m % 4]
                    sl = m % 2
                    L = tmpL.ap()[:, sl * TT:(sl + 1) * TT]
                    R = avf.ap()[:, sl * TT:(sl + 1) * TT]
                    a.activation(L, ps[bank][0:128, 0:TT], AF.Abs, bias=bdt.ap()[:, m:m + 1])
                    a.drain()
                    a.activation(L, L, AF.Exp, scale=-1.0)
                    a.drain()
                    a.activation(L, L, AF.Ln, bias=ones_cf.ap())
                    a.drain()
                    ins = a.activation(R, ps[bank][0:128, 0:TT], AF.Relu,
                                       bias=bdt.ap()[:, m:m + 1])
                    k.inc(ins, "ac", 4)
            SC(_aca)

            def _gpa(g, aa_base=aa_base):
                for m in range(NDI):
                    k.w(g, "ac", aa_base + 4 * m + 4)
                    sl = m % 2
                    L = tmpL.ap()[:, sl * TT:(sl + 1) * TT]
                    R = avf.ap()[:, sl * TT:(sl + 1) * TT]
                    G = tGm.ap()[:, sl * TT:(sl + 1) * TT]
                    g.tensor_tensor(G, R, L, op=OP.add)
                    g.drain()
                    g.tensor_tensor(G, G, chw(xs_b, m, TT), op=OP.mult)
                    g.drain()
                    g.tensor_tensor(chw(xs_b, m, TT), G, chw(zs_b, m, TT), op=OP.mult)
                    k.inc(g.drain(), "gp", 1)
            GP(_gpa)
            y2_gp = k.c["gp"]
            prev_y2gp = y2_gp

            if "p1" in dbg and j == 0:
                def _syd2(g, y2_gp=y2_gp):
                    for c in range(C8):
                        k.inc(g.dma_start(out=dbg["p1"][c * 128:(c + 1) * 128, 1536:2048],
                                          in_=chw(xs_b, c, TT)), "sto", 16)
                GP(_syd2)
                dmp2 = k.c["sto"]
                def _ped2(t, dmp2=dmp2):
                    k.w(t, "sto", dmp2)
                TE(_ped2)

            po_base = k.c["pe"]
            do_base = k.c["dv"]

            def _peo(t, y2_gp=y2_gp, do_base=do_base):
                k.w(t, "gp", y2_gp)
                k.w(t, "gd", 32)
                for m in range(C8):
                    if m >= 2:
                        k.w(t, "dv", do_base + (m - 2) + 1)
                    for kk in range(NDI):
                        ins = t.matmul(ps[m % 2][:, 0:TT],
                                       chw(w_out_bf, kk, 1024)[:, m * 128:(m + 1) * 128],
                                       chw(xs_b, kk, TT),
                                       start=(kk == 0), stop=(kk == NDI - 1))
                    k.inc(ins, "pe", 1)
            TE(_peo)

            def _dvo(v, po_base=po_base, slot=slot):
                for m in range(C8):
                    k.w(v, "pe", po_base + m + 1)
                    v.tensor_tensor(tmpA.ap()[:, 0:TT], ps[m % 2][:, 0:TT],
                                    chw(h_bf, m, TT), op=OP.add)
                    v.drain()
                    v.scalar_tensor_tensor(chw(xtile[slot], m, TT), tmpA.ap()[:, 0:TT],
                                           modv[(0, "ga")].ap()[:, m:m + 1],
                                           chw(xtile[slot], m, TT),
                                           op0=OP.mult, op1=OP.add)
                    k.inc(v.drain(), "dv", 1)
            VE(_dvo)
            dv_upd = k.c["dv"]

            def _syst(sy, j=j, slot=slot, dv_upd=dv_upd):
                k.w(sy, "dv", dv_upd)
                k.inc(sy.dma_start(
                    out=xres.rearrange("(c p) t -> p c t", p=128)[:, :, j * TT:(j + 1) * TT],
                    in_=xtile[slot].ap().rearrange("p (c w) -> p c w", w=TT)), "sxr", 16)
            SY(_syst)
            store_ev[slot] = k.c["sxr"]

        AR.reset(P_BASE)
        k.barrier()
        emit_tap("l0attn")

    ph23_mark = AR.mark()
    h2_b = AR.t("h2_b", [128, C8 * T], BF16)
    comb_b = AR.t("comb_b", [16, T], BF16)
    y_t = AR.t("y_t", [128, C8 * T], F32)
    ph2_mark = AR.mark()

    if n_phases >= 2:
        xtile = [AR.t(f"x2tl{i}", [128, C8 * TT], F32) for i in range(2)]
        h2f = AR.t("h2f", [128, C8 * TT], F32)
        xbf2 = AR.t("xbf2", [128, C8 * TT], BF16)
        biasedf = AR.at(AR.mark() - C8 * TT * 2, "biasedf", [16, T], F32)
        tmpA2 = AR.t("tmpA2", [128, TT], F32)
        lnt2b = AR.t("lnt2b", [128, TT], F32)
        logits = AR.t("logits", [16, T], F32)
        btm = AR.t("btm", [128, 16 * 16], F32)
        stm = AR.t("stm", [128, 16 * 16], F32)
        selm = AR.t("selm", [128, 16], F32)
        mskb = AR.t("mskb", [128, 16], F32)
        wbuf = AR.t("wbuf", [128, 16], F32)
        mrow = AR.t("mrow", [128, 1], F32)
        wsum = AR.t("wsum", [128, 1], F32)
        assert AR.mark() <= 212480

        prev_apply = 0
        for j in range(NT):
            slot = j % 2
            lsem = "lxa" if slot == 0 else "lxb"

            def _syx(sy, j=j, slot=slot, lsem=lsem):
                k.w(sy, "pe", k.c["pe"])
                k.inc(sy.dma_start(
                    out=xtile[slot].ap().rearrange("p (c w) -> p c w", w=TT),
                    in_=xres.rearrange("(c p) t -> p c t", p=128)[:, :, j * TT:(j + 1) * TT]),
                    lsem, 16)
            SY(_syx)
            lx_ev = k.c[lsem]

            def xap(c, slot=slot):
                return chw(xtile[slot], c, TT)

            hb = emit_ln(xap, xbf2,
                         lambda c: chw(h2f, c, TT),
                         lambda c: modv[(0, "cb1")].ap()[:, c:c + 1],
                         lambda c: modv[(0, "sb")].ap()[:, c:c + 1],
                         TT, tmpA2, lnt2b, prev_apply,
                         extra_gp=(lsem, lx_ev))
            prev_apply = hb

            def _gph(g, j=j, hb=hb):
                k.w(g, "dv", hb)
                for c in range(C8):
                    ins = g.tensor_copy(
                        h2_b.ap()[:, c * T + j * TT:c * T + (j + 1) * TT],
                        chw(h2f, c, TT))
                k.inc(ins, "gp", 1)
            GP(_gph)

            def _peg(t, hb=hb):
                k.w(t, "dv", hb)
                for kk in range(C8):
                    ins = t.matmul(ps[6][0:16, 0:TT], gatewt.ap()[:, kk * 16:(kk + 1) * 16],
                                   chw(h2f, kk, TT), start=(kk == 0), stop=(kk == C8 - 1))
                k.inc(ins, "pe", 1)
            TE(_peg); pe_g = k.c["pe"]

            def _dvg(v, j=j, pe_g=pe_g):
                k.w(v, "pe", pe_g)
                v.tensor_copy(logits.ap()[:, j * TT:(j + 1) * TT], ps[6][0:16, 0:TT])
                k.inc(v.drain(), "dv", 1)
            VE(_dvg)
        dv_logit = k.c["dv"]

        def _acsg(a):
            k.w(a, "dv", dv_logit)
            ins = a.activation(logits.ap(), logits.ap(), AF.Sigmoid)
            k.inc(ins, "ac", 1)
        SC(_acsg); ac_sg = k.c["ac"]

        def _dvbi(v):
            k.w(v, "ac", ac_sg)
            v.tensor_scalar(biasedf.ap(), logits.ap(), gateb.ap(), None, op0=OP.add)
            k.inc(v.drain(), "dv", 1)
        VE(_dvbi); dv_bi = k.c["dv"]

        for g16 in range(16):
            def _peT(t, g16=g16):
                k.w(t, "dv", dv_bi if g16 == 0 else k.c["dv"])
                t.transpose(ps[6][:, 0:16],
                            logits.ap()[0:16, g16 * 128:(g16 + 1) * 128],
                            identf.ap()[0:16, 0:16])
                ins = t.transpose(ps[7][:, 0:16],
                                  biasedf.ap()[0:16, g16 * 128:(g16 + 1) * 128],
                                  identf.ap()[0:16, 0:16])
                k.inc(ins, "pe", 1)
            TE(_peT); peT = k.c["pe"]

            def _dvt(v, g16=g16, peT=peT):
                k.w(v, "pe", peT)
                stmg = chw(stm, g16, 16)
                btmg = chw(btm, g16, 16)
                v.tensor_copy(stmg, ps[6][:, 0:16])
                v.tensor_copy(btmg, ps[7][:, 0:16])
                v.drain()
                for it in range(4):
                    v.tensor_reduce(mrow.ap(), btmg, axis=mybir.AxisListType.X, op=OP.max)
                    v.drain()
                    dst = selm.ap() if it == 0 else mskb.ap()
                    v.tensor_scalar(dst, btmg, mrow.ap(), None, op0=OP.is_ge)
                    v.drain()
                    if it > 0:
                        v.tensor_tensor(selm.ap(), selm.ap(), mskb.ap(), op=OP.add)
                    if it < 3:
                        msk = selm.ap() if it == 0 else mskb.ap()
                        v.scalar_tensor_tensor(btmg, msk, -1e30, btmg,
                                               op0=OP.mult, op1=OP.add)
                    v.drain()
                v.tensor_tensor(wbuf.ap(), stmg, selm.ap(), op=OP.mult)
                v.drain()
                v.tensor_reduce(wsum.ap(), wbuf.ap(), axis=mybir.AxisListType.X, op=OP.add)
                v.drain()
                v.reciprocal(wsum.ap(), wsum.ap())
                v.drain()
                v.tensor_scalar(wbuf.ap(), wbuf.ap(), wsum.ap(), RSCALE,
                                op0=OP.mult, op1=OP.mult)
                k.inc(v.drain(), "dv", 1)
            VE(_dvt); dv_c = k.c["dv"]

            def _peT2(t, g16=g16, dv_c=dv_c):
                k.w(t, "dv", dv_c)
                ins = t.transpose(ps[6][0:16, 0:128], wbuf.ap(), identf.ap())
                k.inc(ins, "pe", 1)
            TE(_peT2); peT2 = k.c["pe"]

            def _dvc2(v, g16=g16, peT2=peT2):
                k.w(v, "pe", peT2)
                v.tensor_copy(comb_b.ap()[0:16, g16 * 128:(g16 + 1) * 128],
                              ps[6][0:16, 0:128])
                k.inc(v.drain(), "dv", 1)
            VE(_dvc2)

        AR.reset(ph2_mark)
        k.barrier()

    if n_phases >= 3:
        ph3_mark = AR.mark()
        w13sl = [AR.t(f"w13sl{i}", [128, C8 * 1024], BF16) for i in range(2)]
        w2sl = [AR.t(f"w2sl{i}", [128, 4 * 1024], BF16) for i in range(2)]
        he_b = AR.t("he_b", [128, 2 * 4 * 512], BF16)
        sgf = AR.t("sgf", [128, 2 * 512], F32)
        uuf = AR.t("uuf", [128, 2 * 512], F32)
        combsb = AR.t("combsb", [128, 512], F32)
        tGm3 = AR.t("tGm3", [128, 2 * 512], F32)
        _mw = AR.t("mwsl0", [128, 1024], F32)
        mwsl = [_mw, _mw]
        xtile = [AR.at(ph3_mark + i * C8 * TT * 4, f"x3tl{i}", [128, C8 * TT], F32)
                 for i in range(2)]
        assert AR.mark() <= 212480

        def _gpy(g):
            ins = g.memset(y_t.ap(), 0.0)
            k.inc(ins, "gp", 1)
        GP(_gpy)
        gp_y0 = k.c["gp"]

        mean_jobs = ([("w1", kk, q) for kk in range(C8) for q in range(4)]
                     + [("w2", kk, 0) for kk in range(32)])
        mean_store_ev = [0, 0]
        mean_idx = [0]

        def emit_mean_jobs(nj):
            for _ in range(nj):
                if mean_idx[0] >= len(mean_jobs):
                    return
                kind, kk, q = mean_jobs[mean_idx[0]]
                msl = mean_idx[0] % 2
                tsem = "twa" if msl == 0 else "twb"
                W = 1024
                src = (w1_t if kind == "w1" else w2l_t)

                def _sym(sy, kind=kind, kk=kk, q=q, msl=msl, tsem=tsem, W=W, src=src):
                    k.w(sy, "dv", max(mean_store_ev))
                    k.inc(sy.dma_start(out=mwsl[msl].ap()[:, 0:W],
                                       in_=src[kk * 128:(kk + 1) * 128,
                                               q * 1024:(q + 1) * 1024]), tsem, 16)
                SY(_sym)
                tev = k.c[tsem]

                def _dvm(v, kind=kind, msl=msl, tsem=tsem, W=W, tev=tev):
                    k.w(v, tsem, tev)
                    v.tensor_reduce(wred.ap(), mwsl[msl].ap()[:, 0:W],
                                    axis=mybir.AxisListType.X, op=OP.add,
                                    apply_absolute_value=True)
                    v.drain()
                    col = 0 if kind == "w1" else 1
                    v.tensor_tensor(wmean.ap()[:, col:col + 1], wmean.ap()[:, col:col + 1],
                                    wred.ap(), op=OP.add)
                    k.inc(v.drain(), "dv", 1)
                VE(_dvm)
                mean_store_ev[msl] = k.c["dv"]
                mean_idx[0] += 1

        pe_done = {}
        comb_ac_ev = 0
        combsb_gp = 0
        he_w2_pe = [0, 0]
        for e in range(NEXP + 1):
            shared = (e == NEXP)
            sl = e % 2
            gsem = "gwa" if sl == 0 else "gwb"

            def _gpw(g, e=e, sl=sl, gsem=gsem, shared=shared):
                if e >= 2:
                    k.w(g, "pe", pe_done[e - 2])
                if e == 0:
                    k.w(g, "gp", gp_y0)
                src13 = w13s_t if shared else w13_t[e]
                src2 = w2s_t if shared else w2_t[e]
                k.inc(g.dma_start(
                    out=w13sl[sl].ap().rearrange("p (c f) -> p c f", f=1024),
                    in_=src13.rearrange("(c p) f -> p c f", p=128)), gsem, 16)
                k.inc(g.dma_start(
                    out=w2sl[sl].ap().rearrange("p (c f) -> p c f", f=1024),
                    in_=src2.rearrange("(c p) f -> p c f", p=128)), gsem, 16)
            GP(_gpw)
            gw_ev = k.c[gsem]

            stage = []
            for ct in range(NT + 1):
                if ct < NT:
                    hsl = ct % 2
                    if not shared:
                        def _pec(t, e=e, ct=ct):
                            k.w(t, "ac", comb_ac_ev)
                            k.w(t, "gd", 144)
                            ins = t.matmul(ps[6][:, 0:512],
                                           eall.ap()[:, e * 128:(e + 1) * 128],
                                           comb_b.ap()[0:16, ct * 512:(ct + 1) * 512],
                                           start=True, stop=True)
                            k.inc(ins, "pe", 1)
                        TE(_pec); pe_c = k.c["pe"]

                        def _acc(a, pe_c=pe_c, combsb_gp=combsb_gp):
                            k.w(a, "pe", pe_c)
                            k.w(a, "gp", combsb_gp)
                            ins = a.copy(combsb.ap(), ps[6][:, 0:512])
                            k.inc(ins, "ac", 1)
                        SC(_acc)
                        comb_ac_ev = k.c["ac"]

                    ac13_base = k.c["ac"]
                    gp13_base = k.c["gp"]

                    def _pe13(t, e=e, ct=ct, sl=sl, gw_ev=gw_ev, gsem=gsem,
                              ac13_base=ac13_base):
                        if ct == 0:
                            k.w(t, gsem, gw_ev)
                        for c in range(4):
                            if c >= 2:
                                k.w(t, "ac", ac13_base + 2 * (c - 2) + 2)
                            b0 = [0, 2][c % 2]; b1 = [1, 3][c % 2]
                            for kk in range(C8):
                                t.matmul(ps[b0][:, 0:512],
                                         chw(w13sl[sl], kk, 1024)[:, c * 128:(c + 1) * 128],
                                         h2_b.ap()[:, kk * T + ct * 512:kk * T + (ct + 1) * 512],
                                         start=(kk == 0), stop=(kk == C8 - 1))
                            for kk in range(C8):
                                ins = t.matmul(ps[b1][:, 0:512],
                                               chw(w13sl[sl], kk, 1024)[:, 512 + c * 128:512 + (c + 1) * 128],
                                               h2_b.ap()[:, kk * T + ct * 512:kk * T + (ct + 1) * 512],
                                               start=(kk == 0), stop=(kk == C8 - 1))
                            k.inc(ins, "pe", 1)
                    TE(_pe13)
                    pe13_base = k.c["pe"] - 4

                    def _ac13(a, gp13_base=gp13_base, pe13_base=pe13_base):
                        for c in range(4):
                            k.w(a, "pe", pe13_base + c + 1)
                            if c >= 2:
                                k.w(a, "gp", gp13_base + (c - 2) + 1)
                            b0 = [0, 2][c % 2]; b1 = [1, 3][c % 2]
                            csl = c % 2
                            a.activation(sgf.ap()[:, csl * 512:(csl + 1) * 512],
                                         ps[b0][:, 0:512], AF.Silu)
                            a.drain()
                            ins = a.copy(uuf.ap()[:, csl * 512:(csl + 1) * 512],
                                         ps[b1][:, 0:512])
                            k.inc(ins, "ac", 2)
                    SC(_ac13)

                    def _gp13(g, shared=shared, hsl=hsl, ac13_base=ac13_base,
                              he_free=he_w2_pe[hsl]):
                        for c in range(4):
                            k.w(g, "ac", ac13_base + 2 * c + 2)
                            if c == 0:
                                k.w(g, "pe", he_free)
                            csl = c % 2
                            G = tGm3.ap()[:, csl * 512:(csl + 1) * 512]
                            g.tensor_tensor(G, sgf.ap()[:, csl * 512:(csl + 1) * 512],
                                            uuf.ap()[:, csl * 512:(csl + 1) * 512], op=OP.mult)
                            g.drain()
                            dst = he_b.ap()[:, hsl * 2048 + c * 512:hsl * 2048 + (c + 1) * 512]
                            if shared:
                                g.tensor_copy(dst, G)
                            else:
                                g.tensor_tensor(dst, G, combsb.ap(), op=OP.mult)
                            k.inc(g.drain(), "gp", 1)
                    GP(_gp13)
                    combsb_gp = k.c["gp"]
                    stage.append((ct, k.c["gp"]))

                if ct >= 1:
                    pct, pgp = stage[ct - 1]
                    phsl = pct % 2
                    dvy_base = k.c["dv"]

                    def _pe2(t, sl=sl, pgp=pgp, phsl=phsl, dvy_base=dvy_base):
                        k.w(t, "gp", pgp)
                        k.w(t, "dv", dvy_base)
                        for m in range(C8):
                            if m >= 2:
                                k.w(t, "dv", dvy_base + (m - 2) + 1)
                            for kk in range(4):
                                ins = t.matmul(ps[4 + m % 2][:, 0:512],
                                               chw(w2sl[sl], kk, 1024)[:, m * 128:(m + 1) * 128],
                                               he_b.ap()[:, phsl * 2048 + kk * 512:phsl * 2048 + (kk + 1) * 512],
                                               start=(kk == 0), stop=(kk == 3))
                            k.inc(ins, "pe", 1)
                    TE(_pe2)
                    pe2_base = k.c["pe"] - 8
                    he_w2_pe[phsl] = k.c["pe"]

                    def _dvy(v, pct=pct, pe2_base=pe2_base):
                        for m in range(C8):
                            k.w(v, "pe", pe2_base + m + 1)
                            yap = y_t.ap()[:, m * T + pct * 512:m * T + (pct + 1) * 512]
                            v.tensor_tensor(yap, yap, ps[4 + m % 2][:, 0:512], op=OP.add)
                            k.inc(v.drain(), "dv", 1)
                    VE(_dvy)

            pe_done[e] = k.c["pe"]
            emit_mean_jobs(4)
        emit_mean_jobs(99)

        dv_mean = k.c["dv"]

        def _pesw(t):
            k.w(t, "dv", dv_mean)
            ins = t.matmul(ps[6][0:1, 0:2], ones_cf.ap(), wmean.ap(), start=True, stop=True)
            k.inc(ins, "pe", 1)
        TE(_pesw); pe_sw = k.c["pe"]

        def _dvsw(v, pe_sw=pe_sw):
            k.w(v, "pe", pe_sw)
            r4 = rw4.ap()
            v.tensor_scalar(r4[:, 2:4], ps[6][0:1, 0:2], 1.0 / W1ELEMS, EPS,
                            op0=OP.mult, op1=OP.max)
            v.drain()
            v.reciprocal(r4[:, 0:2], r4[:, 2:4])
            v.drain()
            v.tensor_copy(rw4h.ap(), r4)
            v.drain()
            v.tensor_tensor(rw4l.ap(), r4, rw4h.ap(), op=OP.subtract)
            v.drain()
            v.tensor_copy(rw4lb.ap(), rw4l.ap())
            k.inc(v.drain(), "dv", 1)
        VE(_dvsw); dv_sw = k.c["dv"]

        def _pesw2(t, dv_sw=dv_sw):
            k.w(t, "dv", dv_sw)
            t.matmul(ps[6][:, 0:4], ones_rb.ap(), rw4h.ap(), start=True, stop=False)
            ins = t.matmul(ps[6][:, 0:4], ones_rb.ap(), rw4lb.ap(), start=False, stop=True)
            k.inc(ins, "pe", 1)
        TE(_pesw2); pe_sw2 = k.c["pe"]

        def _dvsw2(v, pe_sw2=pe_sw2):
            k.w(v, "pe", pe_sw2)
            v.tensor_copy(swrep.ap(), ps[6][:, 0:2])
            v.tensor_copy(cmrep.ap(), ps[6][:, 2:4])
            k.inc(v.drain(), "dv", 1)
        VE(_dvsw2)

        store_ev = [0, 0]
        for j in range(NT):
            slot = j % 2
            lsem = "lxa" if slot == 0 else "lxb"

            def _syx(sy, j=j, slot=slot, lsem=lsem):
                k.w(sy, "sxr", store_ev[slot])
                k.w(sy, "pe", pe_done[NEXP])
                k.w(sy, "gp", k.c["gp"])
                k.inc(sy.dma_start(
                    out=xtile[slot].ap().rearrange("p (c w) -> p c w", w=TT),
                    in_=xres.rearrange("(c p) t -> p c t", p=128)[:, :, j * TT:(j + 1) * TT]),
                    lsem, 16)
            SY(_syx)
            lx_ev = k.c[lsem]

            def _dvf(v, j=j, slot=slot, lsem=lsem, lx_ev=lx_ev):
                k.w(v, lsem, lx_ev)
                for c in range(C8):
                    ins = v.scalar_tensor_tensor(
                        chw(xtile[slot], c, TT),
                        y_t.ap()[:, c * T + j * TT:c * T + (j + 1) * TT],
                        modv[(0, "gb")].ap()[:, c:c + 1],
                        chw(xtile[slot], c, TT),
                        op0=OP.mult, op1=OP.add)
                k.inc(ins, "dv", 1)
            VE(_dvf); dv_f = k.c["dv"]

            def _syst(sy, j=j, slot=slot, dv_f=dv_f):
                k.w(sy, "dv", dv_f)
                k.inc(sy.dma_start(
                    out=xres.rearrange("(c p) t -> p c t", p=128)[:, :, j * TT:(j + 1) * TT],
                    in_=xtile[slot].ap().rearrange("p (c w) -> p c w", w=TT)), "sxr", 16)
            SY(_syst)
            store_ev[slot] = k.c["sxr"]

        AR.reset(ph23_mark)
        k.barrier()
        emit_tap("l0")

    if n_phases >= 4:
        ph4_mark = AR.mark()
        cosq = AR.t("cosq", [128, T], BF16)
        sinq = AR.t("sinq", [128, T], BF16)
        xtile = [AR.t(f"x4tl{i}", [128, C8 * TT], F32) for i in range(2)]
        ph4a = AR.mark()
        h1_b = AR.t("h1_b", [128, C8 * T], BF16)
        xbf4 = AR.t("xbf4", [128, C8 * TT], BF16)
        tmpA4 = AR.t("tmpA4", [128, TT], F32)
        lnt2c = AR.t("lnt2c", [128, TT], F32)
        thk = AR.t("thk", [128, T], F32)
        thq = AR.t("thq", [128, T], F32)
        u_f4 = AR.t("u_f4", [128, T], F32)
        n_f4 = AR.t("n_f4", [128, T], F32)
        ksh_b = AR.t("ksh_b", [128, 2 * T], BF16)
        vtm_b = AR.t("vtm_b", [128, T], BF16)
        vf_f = AR.t("vf_f", [128, TT], F32)
        vf_b = AR.t("vf_b", [128, TT], BF16)
        assert AR.mark() <= 212480

        prev_apply = 0
        for j in range(NT):
            slot = j % 2
            lsem = "lxa" if slot == 0 else "lxb"

            def _syx(sy, j=j, slot=slot, lsem=lsem):
                k.w(sy, "dv", prev_apply)
                k.inc(sy.dma_start(
                    out=xtile[slot].ap().rearrange("p (c w) -> p c w", w=TT),
                    in_=xres.rearrange("(c p) t -> p c t", p=128)[:, :, j * TT:(j + 1) * TT]),
                    lsem, 16)
            SY(_syx)
            lx_ev = k.c[lsem]

            def xap(c, slot=slot):
                return chw(xtile[slot], c, TT)

            def h1out(c, j=j):
                return h1_b.ap()[:, c * T + j * TT:c * T + (j + 1) * TT]

            prev_apply = emit_ln(xap, xbf4, h1out,
                                 lambda c: modv[(1, "ca1")].ap()[:, c:c + 1],
                                 lambda c: modv[(1, "sa")].ap()[:, c:c + 1],
                                 TT, tmpA4, lnt2c, prev_apply,
                                 extra_gp=(lsem, lx_ev))
        h1_done = prev_apply

        def qkvw(i):
            return lambda c: wqkv_bf.ap()[:, (c * 3 + i) * HD:(c * 3 + i) * HD + HD]

        for ct in range(NT):
            def _pek(t, ct=ct):
                k.w(t, "dv", h1_done if ct == 0 else k.c["dv"])
                k.w(t, "gd", 112)
                for kk in range(C8):
                    ins = t.matmul(ps[0][:, 0:TT], qkvw(1)(kk),
                                   h1_b.ap()[:, kk * T + ct * TT:kk * T + (ct + 1) * TT],
                                   start=(kk == 0), stop=(kk == C8 - 1))
                k.inc(ins, "pe", 1)
                for kk in range(C8):
                    ins = t.matmul(ps[2][:, 0:TT], qkvw(2)(kk),
                                   h1_b.ap()[:, kk * T + ct * TT:kk * T + (ct + 1) * TT],
                                   start=(kk == 0), stop=(kk == C8 - 1))
                k.inc(ins, "pe", 1)
                for kk in range(C8):
                    ins = t.matmul(ps[1][:, 0:TT], qkvw(0)(kk),
                                   h1_b.ap()[:, kk * T + ct * TT:kk * T + (ct + 1) * TT],
                                   start=(kk == 0), stop=(kk == C8 - 1))
                k.inc(ins, "pe", 1)
            TE(_pek); pe_k = k.c["pe"]

            def _dvk(v, ct=ct, pe_k=pe_k):
                k.w(v, "pe", pe_k - 2)
                thw = thk.ap()[:, ct * TT:(ct + 1) * TT]
                v.tensor_scalar(thw, ps[0][:, 0:TT], bk_s.ap(), None, op0=OP.add)
                v.drain()
                v.tensor_scalar(thw, thw, invw.ap(), phb.ap(), op0=OP.mult, op1=OP.add)
                v.drain()
                k.w(v, "pe", pe_k - 1)
                v.tensor_scalar(vf_f.ap(), ps[2][:, 0:TT], bv_s.ap(), None, op0=OP.add)
                v.drain()
                v.tensor_copy(vf_b.ap(), vf_f.ap())
                v.drain()
                k.w(v, "pe", pe_k)
                thw = thq.ap()[:, ct * TT:(ct + 1) * TT]
                v.tensor_scalar(thw, ps[1][:, 0:TT], bq_s.ap(), None, op0=OP.add)
                v.drain()
                v.tensor_scalar(thw, thw, invw.ap(), phb.ap(), op0=OP.mult, op1=OP.add)
                k.inc(v.drain(), "dv", 1)
            VE(_dvk); dv_v = k.c["dv"]

            pevt_base = k.c["pe"]
            dvvt_base = k.c["dv"]

            def _pevt(t, ct=ct, dv_v=dv_v, dvvt_base=dvvt_base):
                k.w(t, "dv", dv_v)
                for b4 in range(4):
                    if b4 >= 2:
                        k.w(t, "dv", dvvt_base + (b4 - 2) + 1)
                    ins = t.transpose(ps[6 + b4 % 2].bitcast(BF16)[:, 0:128],
                                      vf_b.ap()[:, b4 * 128:(b4 + 1) * 128], identb.ap())
                    k.inc(ins, "pe", 1)
            TE(_pevt)

            def _dvvt(v, ct=ct, pevt_base=pevt_base):
                for b4 in range(4):
                    k.w(v, "pe", pevt_base + b4 + 1)
                    v.tensor_copy(
                        vtm_b.ap()[:, ct * TT + b4 * 128:ct * TT + (b4 + 1) * 128],
                        ps[6 + b4 % 2].bitcast(BF16)[:, 0:128])
                    k.inc(v.drain(), "dv", 1)
            VE(_dvvt)
        thq_dv = k.c["dv"]

        ksc_ac = emit_sincos_multi(
            [(thk.ap(), ksh_b.ap()[:, 0:T], ksh_b.ap()[:, T:2 * T], u_f4.ap(), n_f4.ap())],
            T, pre_dv=thq_dv)
        kv_dv = k.c["dv"]
        kv_ac = ksc_ac

        kv_dv = k.c["dv"]
        kv_ac = k.c["ac"]

        if p4cut < 2:
            def _syskip(sy):
                k.inc(sy.dma_start(out=yout, in_=xres), "sto", 16)
            SY(_syskip)
            k.barrier()
            _es.close()
            return nc

        def _sykv(sy):
            k.w(sy, "dv", kv_dv)
            k.w(sy, "ac", kv_ac)
            k.inc(sy.dma_start(out=kvin[0:128, :], in_=ksh_b.ap()[:, 0:T]), "sxr", 16)
            k.inc(sy.dma_start(out=kvin[128:256, :], in_=ksh_b.ap()[:, T:2 * T]), "sxr", 16)
            k.inc(sy.dma_start(out=kvin[256:384, :], in_=vtm_b.ap()), "sxr", 16)
        SY(_sykv)
        sxr_kv = k.c["sxr"]

        def _gpkv(g):
            k.w(g, "sxr", sxr_kv)
            k.inc(g.collective_compute(
                "AllGather", OP.bypass, replica_groups=[list(range(NCOR))],
                ins=[kvin.ap().opt()], outs=[kvout.ap().opt()]), "cc", 1)
        GP(_gpkv)
        cc_kv = k.c["cc"]

        emit_sincos_multi(
            [(thq.ap(), cosq.ap(), sinq.ap(), u_f4.ap(), n_f4.ap())], T, pre_dv=thq_dv)
        q_dv = k.c["dv"]
        q_ac = k.c["ac"]

        if p4cut < 3:
            def _syskip2(sy):
                k.w(sy, "cc", cc_kv)
                k.w(sy, "dv", q_dv)
                k.w(sy, "ac", q_ac)
                k.inc(sy.dma_start(out=yout, in_=xres), "sto", 16)
            SY(_syskip2)
            k.barrier()
            _es.close()
            return nc

        AR.reset(ph4a)
        ckv = AR.t("ckv", [128, 16384], BF16)
        skv = AR.t("skv", [128, 16384], BF16)
        vfull = AR.t("vfull", [128, 129 * 128], BF16)
        pb_b = AR.t("pb_b", [128, 2 * 1024], BF16)
        ofm = AR.t("ofm", [128, T], BF16)
        otm = AR.t("otm", [128, 128], BF16)
        rl4 = AR.t("rl4", [128, 1], F32)
        twsl = AR.t("twsl", [128, 4096], F32)
        tq_b = AR.t("tq_b", [128, 4096], BF16)
        tmpWo = AR.t("tmpWo", [128, TT], F32)
        assert AR.mark() <= 212480

        def _sykv2(sy):
            k.w(sy, "cc", cc_kv)
            k.w(sy, "dv", q_dv)
            k.w(sy, "ac", q_ac)
            k.w(sy, "gp", k.c["gp"])
            kvr = kvout.ap().rearrange("(r s p) t -> s p r t", s=3, p=128)
            k.inc(sy.dma_start(
                out=ckv.ap().rearrange("p (r t) -> p r t", t=T), in_=kvr[0]), "ld", 16)
            k.inc(sy.dma_start(
                out=skv.ap().rearrange("p (r t) -> p r t", t=T), in_=kvr[1]), "ld", 16)
            vtv = vfull.ap().rearrange("p (kt w) -> p kt w", w=129)
            for r8 in range(8):
                k.inc(sy.dma_start(
                    out=vtv[:, r8 * 16:(r8 + 1) * 16, 0:128],
                    in_=kvr[2][:, r8].rearrange("p (t w) -> p t w", w=128)), "ld", 16)
        SY(_sykv2)
        ld_kv = k.c["ld"]

        def _dvones(v):
            k.w(v, "ld", ld_kv)
            ins = v.memset(vfull.ap().rearrange("p (kt w) -> p kt w", w=129)[:, :, 128:129], 1.0)
            k.inc(ins, "dv", 1)
        VE(_dvones)
        vones_dv = k.c["dv"]

        tern_jobs = [("w1", kk) for kk in range(C8)] + [("w2", kk) for kk in range(32)]
        tern_idx = [0]
        tern_state = {"dv": 0, "st": 0}

        def emit_tern_jobs(nj):
            for _ in range(nj):
                if tern_idx[0] >= len(tern_jobs):
                    return
                kind, kk = tern_jobs[tern_idx[0]]
                tsem = "twa" if tern_idx[0] % 2 == 0 else "twb"
                W = 4096 if kind == "w1" else 1024
                src = w1_t if kind == "w1" else w2l_t
                dst = wq1d if kind == "w1" else wq2d
                scol = 0 if kind == "w1" else 1

                def _syt(sy, kind=kind, kk=kk, tsem=tsem, W=W, src=src):
                    k.w(sy, "dv", tern_state["dv"])
                    k.inc(sy.dma_start(out=twsl.ap()[:, 0:W],
                                       in_=src[kk * 128:(kk + 1) * 128, :]), tsem, 16)
                SY(_syt)
                tev = k.c[tsem]

                def _dvt(v, tsem=tsem, W=W, scol=scol, tev=tev, stp=tern_state["st"]):
                    k.w(v, tsem, tev)
                    k.w(v, "sxr", stp)
                    tw = twsl.ap()[:, 0:W]
                    v.tensor_scalar(tw, tw, swrep.ap()[:, scol:scol + 1],
                                    MAGIC, op0=OP.mult, op1=OP.add)
                    v.drain()
                    v.tensor_scalar(tw, tw, MAGIC + 1.0, MAGIC - 1.0, op0=OP.min, op1=OP.max)
                    v.drain()
                    v.tensor_scalar(tq_b.ap()[:, 0:W], tw,
                                    -MAGIC, 1.0, op0=OP.add, op1=OP.mult)
                    k.inc(v.drain(), "dv", 1)
                VE(_dvt)
                tern_state["dv"] = k.c["dv"]
                dv_ev = k.c["dv"]

                def _syt2(sy, kind=kind, kk=kk, W=W, dst=dst, dv_ev=dv_ev):
                    k.w(sy, "dv", dv_ev)
                    k.inc(sy.dma_start(out=dst[kk * 128:(kk + 1) * 128, :],
                                       in_=tq_b.ap()[:, 0:W]), "sxr", 16)
                SY(_syt2)
                tern_state["st"] = k.c["sxr"]
                tern_idx[0] += 1

        if p4cut < 4:
            def _syskip3(sy):
                k.w(sy, "ld", ld_kv)
                k.w(sy, "dv", k.c["dv"])
                k.inc(sy.dma_start(out=yout, in_=xres), "sto", 16)
            SY(_syskip3)
            k.barrier()
            _es.close()
            return nc

        NKT = 128
        o_evac_dv = 0
        for qb in range(NT):
            pe_base = k.c["pe"]
            ac_base = k.c["ac"]
            def _peat(t, qb=qb, pe_base=pe_base, ac_base=ac_base, o_evac_dv=o_evac_dv):
                if qb == 0:
                    k.w(t, "ld", ld_kv)
                    k.w(t, "dv", max(q_dv, vones_dv))
                    k.w(t, "ac", q_ac)
                else:
                    k.w(t, "dv", o_evac_dv)
                NKP = NKT // 2
                for kp in range(NKP):
                    pr = kp % 2
                    if kp >= 2:
                        k.w(t, "ac", ac_base + (kp - 2) + 1)
                    for half in (0, 1):
                        kt = 2 * kp + half
                        t.matmul(ps[[0, 6][pr] + half][:, 0:TT],
                                 ckv.ap()[:, kt * 128:(kt + 1) * 128],
                                 cosq.ap()[:, qb * TT:(qb + 1) * TT],
                                 start=True, stop=False)
                        ins = t.matmul(ps[[0, 6][pr] + half][:, 0:TT],
                                       skv.ap()[:, kt * 128:(kt + 1) * 128],
                                       sinq.ap()[:, qb * TT:(qb + 1) * TT],
                                       start=False, stop=True)
                    k.inc(ins, "pe", 1)
                    if kp >= 1:
                        kpp = kp - 1
                        k.w(t, "ac", ac_base + kpp + 1)
                        for half in (0, 1):
                            ktp = 2 * kpp + half
                            for qs in range(4):
                                ins = t.matmul(
                                    ps[2 + qs][:, 0:129],
                                    pb_b.ap()[:, (kpp % 2) * 1024 + half * TT + qs * 128:
                                              (kpp % 2) * 1024 + half * TT + (qs + 1) * 128],
                                    vfull.ap()[:, ktp * 129:(ktp + 1) * 129],
                                    start=(ktp == 0), stop=False)
                        k.inc(ins, "pe", 1)
                kpp = NKP - 1
                k.w(t, "ac", ac_base + kpp + 1)
                for half in (0, 1):
                    ktp = 2 * kpp + half
                    for qs in range(4):
                        ins = t.matmul(
                            ps[2 + qs][:, 0:129],
                            pb_b.ap()[:, (kpp % 2) * 1024 + half * TT + qs * 128:
                                      (kpp % 2) * 1024 + half * TT + (qs + 1) * 128],
                            vfull.ap()[:, ktp * 129:(ktp + 1) * 129],
                            start=False, stop=(ktp == NKT - 1))
                k.inc(ins, "pe", 1)
            TE(_peat)

            def s_ev(kp, pe_base=pe_base):
                return pe_base + 2 * kp + 1

            def _acat(a, qb=qb, pe_base=pe_base, ac_base=ac_base):
                for kp in range(NKT // 2):
                    k.w(a, "pe", s_ev(kp))
                    ins = a.activation(
                        pb_b.ap()[:, (kp % 2) * 1024:(kp % 2 + 1) * 1024],
                        pspair[kp % 2][:, 0:1024], AF.Exp, scale=ASCALE)
                    k.inc(ins, "ac", 1)
            SC(_acat)
            pe_attn_done = k.c["pe"]

            for qs in range(4):
                def _dvo(v, qb=qb, qs=qs, pe_attn_done=pe_attn_done):
                    k.w(v, "pe", pe_attn_done)
                    v.reciprocal(rl4.ap(), ps[2 + qs][:, 128:129])
                    v.drain()
                    v.tensor_scalar(otm.ap(), ps[2 + qs][:, 0:128], rl4.ap(), None,
                                    op0=OP.mult)
                    k.inc(v.drain(), "dv", 1)
                VE(_dvo); dv_o = k.c["dv"]

                def _peot(t, dv_o=dv_o):
                    k.w(t, "dv", dv_o)
                    ins = t.transpose(ps[6].bitcast(BF16)[:, 0:128], otm.ap(), identb.ap())
                    k.inc(ins, "pe", 1)
                TE(_peot); pe_ot = k.c["pe"]

                def _dvo2(v, qb=qb, qs=qs, pe_ot=pe_ot):
                    k.w(v, "pe", pe_ot)
                    v.tensor_copy(ofm.ap()[:, qb * TT + qs * 128:qb * TT + (qs + 1) * 128],
                                  ps[6].bitcast(BF16)[:, 0:128])
                    k.inc(v.drain(), "dv", 1)
                VE(_dvo2)
            o_evac_dv = k.c["dv"]
            emit_tern_jobs(10)
        emit_tern_jobs(99)

        if p4cut < 5:
            def _syskip4(sy):
                k.w(sy, "dv", o_evac_dv)
                k.inc(sy.dma_start(out=yout, in_=xres), "sto", 16)
            SY(_syskip4)
            k.barrier()
            _es.close()
            return nc

        store_ev = [0, 0]
        for j in range(NT):
            slot = j % 2
            lsem = "lxa" if slot == 0 else "lxb"

            def _syx(sy, j=j, slot=slot, lsem=lsem):
                k.w(sy, "sxr", store_ev[slot])
                k.w(sy, "dv", o_evac_dv)
                k.inc(sy.dma_start(
                    out=xtile[slot].ap().rearrange("p (c w) -> p c w", w=TT),
                    in_=xres.rearrange("(c p) t -> p c t", p=128)[:, :, j * TT:(j + 1) * TT]),
                    lsem, 16)
            SY(_syx)
            lx_ev = k.c[lsem]

            dwo_base = k.c["dv"]

            def _pewo(t, j=j, lsem=lsem, lx_ev=lx_ev, dwo_base=dwo_base):
                k.w(t, "dv", max(o_evac_dv, dwo_base))
                k.w(t, "gd", 128)
                for m in range(C8):
                    if m >= 2:
                        k.w(t, "dv", dwo_base + (m - 2) + 1)
                    ins = t.matmul(ps[m % 2][:, 0:TT],
                                   wo_bf.ap()[:, m * 128:(m + 1) * 128],
                                   ofm.ap()[:, j * TT:(j + 1) * TT],
                                   start=True, stop=True)
                    k.inc(ins, "pe", 1)
            TE(_pewo)
            pwo_base = k.c["pe"] - 8

            def _dvwo(v, j=j, slot=slot, lsem=lsem, lx_ev=lx_ev, pwo_base=pwo_base):
                k.w(v, lsem, lx_ev)
                for m in range(C8):
                    k.w(v, "pe", pwo_base + m + 1)
                    v.tensor_scalar(tmpWo.ap(), ps[m % 2][:, 0:TT],
                                    bo_s.ap()[:, m:m + 1], modv[(1, "ga")].ap()[:, m:m + 1],
                                    op0=OP.add, op1=OP.mult)
                    v.drain()
                    v.tensor_tensor(chw(xtile[slot], m, TT), chw(xtile[slot], m, TT),
                                    tmpWo.ap(), op=OP.add)
                    k.inc(v.drain(), "dv", 1)
            VE(_dvwo)
            dv_upd = k.c["dv"]

            def _syst(sy, j=j, slot=slot, dv_upd=dv_upd):
                k.w(sy, "dv", dv_upd)
                k.inc(sy.dma_start(
                    out=xres.rearrange("(c p) t -> p c t", p=128)[:, :, j * TT:(j + 1) * TT],
                    in_=xtile[slot].ap().rearrange("p (c w) -> p c w", w=TT)), "sxr", 16)
            SY(_syst)
            store_ev[slot] = k.c["sxr"]

        AR.reset(ph4_mark)
        k.barrier()
        emit_tap("attn")

    if n_phases >= 5:
        TT5 = 512
        NT5 = 4
        ph5_mark = AR.mark()
        xtile = [AR.t(f"x5tl{i}", [128, C8 * TT5], F32) for i in range(2)]
        h2f5 = AR.t("h2f5", [128, C8 * TT5], F32)
        tmpA5 = AR.t("tmpA5", [128, TT5], F32)
        trb = AR.t("trb", [128, TT5], F32)
        am5 = AR.t("am5", [128, 12], F32)
        xq1_b = AR.t("xq1_b", [128, C8 * TT5], BF16)
        h5 = AR.t("h5", [128, 32 * TT5], F32)
        xq2_b = AR.at(AR.mark() - 32 * TT5 * 4, "xq2_b", [128, 32 * TT5], BF16)
        xbf5 = AR.t("xbf5", [128, C8 * TT5], BF16)
        sc1sb = AR.t("sc1sb", [128, TT5], F32)
        rs1sb = AR.t("rs1sb", [128, TT5], F32)
        sc2sb = AR.t("sc2sb", [128, TT5], F32)
        rs2sb = AR.t("rs2sb", [128, TT5], F32)
        tq5 = AR.t("tq5", [128, 2 * TT5], F32)
        qw1sl = [AR.t(f"qw1sl{i}", [128, C8 * 512], BF16) for i in range(2)]
        qw2sl = [AR.t(f"qw2sl{i}", [128, 32 * 128], BF16) for i in range(2)]
        assert AR.mark() <= 212480, AR.mark()

        def emit_quant_scales(parent, nch, cmcol, scb, rsb, pre_dv=0, pre_ac=0):
            W = TT5
            def _dvt(v):
                k.w(v, "dv", pre_dv)
                k.w(v, "ac", pre_ac)
                xv = parent.ap()[:, 0:nch * W].rearrange("p (c t) -> p t c", c=nch)
                v.tensor_reduce(trb.ap()[:, 0:W], xv, axis=mybir.AxisListType.X, op=OP.max)
                v.tensor_reduce(ps[7][:, 0:W], xv, axis=mybir.AxisListType.X, op=OP.min)
                v.drain()
                v.scalar_tensor_tensor(trb.ap()[:, 0:W], ps[7][:, 0:W], -1.0,
                                       trb.ap()[:, 0:W], op0=OP.mult, op1=OP.max)
                k.inc(v.drain(), "dv", 1)
            VE(_dvt)
            dv_tree = k.c["dv"]

            def _peA(t, dv_tree=dv_tree):
                k.w(t, "dv", dv_tree)
                for g4 in range(4):
                    ins = t.transpose(ps[6][:, g4 * 128:(g4 + 1) * 128],
                                      trb.ap()[:, g4 * 128:(g4 + 1) * 128], identf.ap())
                k.inc(ins, "pe", 1)
            TE(_peA); pe_trA = k.c["pe"]

            def _dvB(v, pe_trA=pe_trA, cmcol=cmcol):
                k.w(v, "pe", pe_trA)
                for g4 in range(4):
                    v.tensor_reduce(am5.ap()[:, g4:g4 + 1],
                                    ps[6][:, g4 * 128:(g4 + 1) * 128],
                                    axis=mybir.AxisListType.X, op=OP.max)
                v.drain()
                v.tensor_scalar(am5.ap()[:, 0:4], am5.ap()[:, 0:4], EPS, None, op0=OP.max)
                v.drain()
                v.reciprocal(am5.ap()[:, 4:8], am5.ap()[:, 0:4])
                v.tensor_scalar(am5.ap()[:, 8:12], am5.ap()[:, 0:4],
                                cmrep.ap()[:, cmcol:cmcol + 1], 1.0 / 127.0,
                                op0=OP.mult, op1=OP.mult)
                v.drain()
                v.tensor_scalar(am5.ap()[:, 4:8], am5.ap()[:, 4:8], 127.0, None,
                                op0=OP.mult)
                k.inc(v.drain(), "dv", 1)
            VE(_dvB); dv_sc = k.c["dv"]

            def _peB(t, dv_sc=dv_sc):
                k.w(t, "dv", dv_sc)
                for g4 in range(4):
                    t.transpose(ps[6][0:1, g4 * 128:(g4 + 1) * 128],
                                am5.ap()[:, 4 + g4:5 + g4], identf.ap())
                    ins = t.transpose(ps[7][0:1, g4 * 128:(g4 + 1) * 128],
                                      am5.ap()[:, 8 + g4:9 + g4], identf.ap())
                k.inc(ins, "pe", 1)
            TE(_peB); pe_trB = k.c["pe"]

            def _dvC(v, pe_trB=pe_trB):
                k.w(v, "pe", pe_trB)
                v.tensor_copy(rowm.ap()[:, 0:W], ps[6][0:1, 0:W])
                v.tensor_copy(rowq.ap()[:, 0:W], ps[7][0:1, 0:W])
                v.drain()
                v.tensor_copy(rhiA.ap()[:, 0:W], rowm.ap()[:, 0:W])
                v.tensor_copy(rhiB.ap()[:, 0:W], rowq.ap()[:, 0:W])
                v.drain()
                v.tensor_tensor(rloAb.ap()[:, 0:W], rowm.ap()[:, 0:W],
                                rhiA.ap()[:, 0:W], op=OP.subtract)
                v.tensor_tensor(rloBb.ap()[:, 0:W], rowq.ap()[:, 0:W],
                                rhiB.ap()[:, 0:W], op=OP.subtract)
                k.inc(v.drain(), "dv", 1)
            VE(_dvC); dv_hl = k.c["dv"]

            def _peC(t, dv_hl=dv_hl):
                k.w(t, "dv", dv_hl)
                t.matmul(ps[6][:, 0:W], ones_rb.ap(), rhiA.ap()[:, 0:W],
                         start=True, stop=False)
                t.matmul(ps[6][:, 0:W], ones_rb.ap(), rloAb.ap()[:, 0:W],
                         start=False, stop=True)
                t.matmul(ps[7][:, 0:W], ones_rb.ap(), rhiB.ap()[:, 0:W],
                         start=True, stop=False)
                ins = t.matmul(ps[7][:, 0:W], ones_rb.ap(), rloBb.ap()[:, 0:W],
                               start=False, stop=True)
                k.inc(ins, "pe", 1)
            TE(_peC); pe_bcq = k.c["pe"]

            def _dvD(v, pe_bcq=pe_bcq, scb=scb, rsb=rsb):
                k.w(v, "pe", pe_bcq)
                v.tensor_copy(scb.ap()[:, 0:W], ps[6][:, 0:W])
                v.tensor_copy(rsb.ap()[:, 0:W], ps[7][:, 0:W])
                k.inc(v.drain(), "dv", 1)
            VE(_dvD)
            return k.c["dv"]

        st5 = {"prev_apply": 0, "store": [0, 0], "qw_pe": {0: 0, 1: 0},
               "w1_pe_end": 0, "w2_pe_end": 0, "xq1_dv": 0, "xq2_dv": 0,
               "gelu_end": 0, "out_dv": 0}

        def front_end(j):
            slot = j % 2
            lsem = "lxa" if slot == 0 else "lxb"

            def _syx(sy, j=j, slot=slot, lsem=lsem):
                k.w(sy, "sto", st5["store"][slot])
                k.inc(sy.dma_start(
                    out=xtile[slot].ap().rearrange("p (c w) -> p c w", w=TT5),
                    in_=xres.rearrange("(c p) t -> p c t", p=128)[:, :, j * TT5:(j + 1) * TT5]),
                    lsem, 16)
            SY(_syx)
            lx_ev = k.c[lsem]

            def xap(c, slot=slot):
                return chw(xtile[slot], c, TT5)

            ap_ev = emit_ln(xap, xbf5,
                            lambda c: chw(h2f5, c, TT5),
                            lambda c: gam_s.ap()[:, c:c + 1],
                            lambda c: bet_s.ap()[:, c:c + 1],
                            TT5, tmpA5, trb, st5["prev_apply"],
                            extra_gp=(lsem, lx_ev),
                            extra_ac=("gp", st5["xq1_dv"]))
            st5["prev_apply"] = ap_ev

            emit_quant_scales(h2f5, C8, 0, sc1sb, rs1sb,
                              pre_dv=ap_ev, pre_ac=st5["gelu_end"])
            sc1_dv = k.c["dv"]

            w1_done = st5["w1_pe_end"]
            def _g1(g, sc1_dv=sc1_dv, w1_done=w1_done):
                k.w(g, "dv", sc1_dv)
                k.w(g, "pe", w1_done)
                for c in range(C8):
                    sl = tq5.ap()[:, (c % 2) * TT5:(c % 2 + 1) * TT5]
                    g.tensor_tensor(sl, chw(h2f5, c, TT5), sc1sb.ap(), op=OP.mult)
                    g.drain()
                    g.tensor_scalar(chw(xq1_b, c, TT5), sl,
                                    MAGIC, MAGIC, op0=OP.add, op1=OP.subtract)
                    g.drain()
                k.inc(g.nop(), "gp", 1)
            GP(_g1)
            st5["xq1_dv"] = k.c["gp"]

        def w1_stage(j):
            w1_dv0 = k.c["dv"]
            w1_ac0 = k.c["ac"]
            xq1_dv = st5["xq1_dv"]
            for mb in range(C8):
                qsl = mb % 2
                qsem = "qwa" if qsl == 0 else "qwb"

                def _syw(sy, mb=mb, qsl=qsl, qsem=qsem):
                    k.w(sy, "pe", st5["qw_pe"][qsl])
                    k.inc(sy.dma_start(
                        out=qw1sl[qsl].ap().rearrange("p (c f) -> p c f", f=512),
                        in_=wq1d.rearrange("(c p) f -> p c f", p=128)[:, :, mb * 512:(mb + 1) * 512]),
                        qsem, 16)
                SY(_syw)
                qw_ev = k.c[qsem]

                def _pew1(t, mb=mb, qsl=qsl, qsem=qsem, qw_ev=qw_ev, xq1_dv=xq1_dv):
                    k.w(t, qsem, qw_ev)
                    if mb == 0:
                        k.w(t, "gp", xq1_dv)
                        k.w(t, "dv", st5["out_dv"])
                    for mc in range(4):
                        gi = mb * 4 + mc
                        if gi >= 2:
                            k.w(t, "dv", w1_dv0 + (gi - 2) + 1)
                        for kk in range(C8):
                            ins = t.matmul(ps[mc % 2][:, 0:TT5],
                                           chw(qw1sl[qsl], kk, 512)[:, mc * 128:(mc + 1) * 128],
                                           chw(xq1_b, kk, TT5),
                                           start=(kk == 0), stop=(kk == C8 - 1))
                        k.inc(ins, "pe", 1)
                TE(_pew1)
                st5["qw_pe"][qsl] = k.c["pe"]
                p1_base = k.c["pe"] - 4

                for mc in range(4):
                    def _dvw1(v, mb=mb, mc=mc, p1_base=p1_base):
                        k.w(v, "pe", p1_base + mc + 1)
                        gi = mb * 4 + mc
                        if gi >= 2:
                            k.w(v, "ac", w1_ac0 + (gi - 2) + 1)
                        v.tensor_tensor(tq5.ap()[:, (mc % 2) * TT5:(mc % 2 + 1) * TT5],
                                        ps[mc % 2][:, 0:TT5], rs1sb.ap(), op=OP.mult)
                        k.inc(v.drain(), "dv", 1)
                    VE(_dvw1)
                    dv_m = k.c["dv"]

                    def _acw1(a, mb=mb, mc=mc, dv_m=dv_m):
                        k.w(a, "dv", dv_m)
                        mm = mb * 4 + mc
                        ins = a.activation(chw(h5, mm, TT5),
                                           tq5.ap()[:, (mc % 2) * TT5:(mc % 2 + 1) * TT5],
                                           AF.Gelu, bias=b1_s.ap()[:, mm:mm + 1])
                        k.inc(ins, "ac", 1)
                    SC(_acw1)
            st5["w1_pe_end"] = k.c["pe"]
            st5["gelu_end"] = k.c["ac"]

        def quant2_stage(j):
            emit_quant_scales(h5, 32, 1, sc2sb, rs2sb,
                              pre_ac=st5["gelu_end"])
            sc2_dv = k.c["dv"]
            w2_done = st5["w2_pe_end"]
            def _g2(g, sc2_dv=sc2_dv, w2_done=w2_done):
                k.w(g, "dv", sc2_dv)
                k.w(g, "pe", w2_done)
                for c in range(32):
                    sl = tq5.ap()[:, (c % 2) * TT5:(c % 2 + 1) * TT5]
                    g.tensor_tensor(sl, chw(h5, c, TT5), sc2sb.ap(), op=OP.mult)
                    g.drain()
                    g.tensor_scalar(chw(xq2_b, c, TT5), sl,
                                    MAGIC, MAGIC, op0=OP.add, op1=OP.subtract)
                    g.drain()
                k.inc(g.nop(), "gp", 1)
            GP(_g2)
            st5["xq2_dv"] = k.c["gp"]

        def w2_stage(j):
            slot = j % 2
            xq2_dv = st5["xq2_dv"]
            dvw2_pred = []
            for mb in range(C8):
                qsl = mb % 2
                qsem = "qwa" if qsl == 0 else "qwb"

                def _syw2(sy, mb=mb, qsl=qsl, qsem=qsem):
                    k.w(sy, "pe", st5["qw_pe"][qsl])
                    k.inc(sy.dma_start(
                        out=qw2sl[qsl].ap().rearrange("p (c f) -> p c f", f=128),
                        in_=wq2d.rearrange("(c p) f -> p c f", p=128)[:, :, mb * 128:(mb + 1) * 128]),
                        qsem, 16)
                SY(_syw2)
                qw_ev = k.c[qsem]

                def _pew2(t, mb=mb, qsl=qsl, qsem=qsem, qw_ev=qw_ev, xq2_dv=xq2_dv,
                          dvw2_pred=dvw2_pred):
                    k.w(t, qsem, qw_ev)
                    if mb == 0:
                        k.w(t, "gp", xq2_dv)
                    if mb >= 2:
                        k.w(t, "dv", dvw2_pred[mb - 2])
                    for kk in range(32):
                        ins = t.matmul(ps[mb % 2][:, 0:TT5],
                                       chw(qw2sl[qsl], kk, 128),
                                       chw(xq2_b, kk, TT5),
                                       start=(kk == 0), stop=(kk == 31))
                    k.inc(ins, "pe", 1)
                TE(_pew2)
                st5["qw_pe"][qsl] = k.c["pe"]
                p2_ev = k.c["pe"]

                def _dvw2(v, mb=mb, slot=slot, p2_ev=p2_ev):
                    k.w(v, "pe", p2_ev)
                    v.tensor_tensor(tmpA5.ap(), ps[mb % 2][:, 0:TT5], rs2sb.ap(), op=OP.mult)
                    v.drain()
                    v.tensor_scalar(tmpA5.ap(), tmpA5.ap(), b2_s.ap()[:, mb:mb + 1],
                                    modv[(1, "gb")].ap()[:, mb:mb + 1],
                                    op0=OP.add, op1=OP.mult)
                    v.drain()
                    v.tensor_tensor(chw(xtile[slot], mb, TT5), chw(xtile[slot], mb, TT5),
                                    tmpA5.ap(), op=OP.add)
                    k.inc(v.drain(), "dv", 1)
                VE(_dvw2)
                dvw2_pred.append(k.c["dv"])
            st5["w2_pe_end"] = k.c["pe"]
            st5["out_dv"] = k.c["dv"]
            dv_out = k.c["dv"]

            def _syout(sy, j=j, slot=slot, dv_out=dv_out):
                k.w(sy, "dv", dv_out)
                k.inc(sy.dma_start(
                    out=yout.rearrange("(c p) t -> p c t", p=128)[:, :, j * TT5:(j + 1) * TT5],
                    in_=xtile[slot].ap().rearrange("p (c w) -> p c w", w=TT5)), "sto", 16)
            SY(_syout)
            st5["store"][slot] = k.c["sto"]

        front_end(0)
        w1_stage(0)
        for j in range(NT5):
            if j + 1 < NT5:
                front_end(j + 1)
            quant2_stage(j)
            w2_stage(j)
            if j + 1 < NT5:
                w1_stage(j + 1)

        AR.reset(ph5_mark)

    if n_phases < 5:
        def _syfin(sy):
            k.inc(sy.dma_start(out=yout, in_=xres), "sto", 16)
        SY(_syfin)

    k.barrier()
    sto_fin = k.c["sto"]

    def _syw8(sy):
        k.w(sy, "sto", sto_fin)
    SY(_syw8)

    _es.close()
    return nc



def _vec(v, p=128):
    v = np.asarray(v, np.float32).reshape(-1)
    n = v.size // p
    return np.ascontiguousarray(v.reshape(n, p).T)


_CACHE = {}


def _get_nc(taps=(), n_phases=N_PHASES, p4cut=9):
    key = (tuple(taps), n_phases, p4cut)
    if key not in _CACHE:
        _CACHE[key] = build_program(taps, n_phases, p4cut)
    return _CACHE[key]


def kernel(L0_attn_x, L0_attn_W_in, L0_attn_conv_w, L0_attn_conv_b, L0_attn_W_x,
           L0_attn_W_dt, L0_attn_b_dt, L0_attn_W_out, L0_ffn_input_ids,
           L0_ffn_gate_w, L0_ffn_gate_b, L0_ffn_w13, L0_ffn_w2, L0_ffn_w13_s,
           L0_ffn_w2_s, L0_c, L0_wmod, L1_attn_wavelengths, L1_attn_phase_bias,
           L1_attn_Wq, L1_attn_bq, L1_attn_Wk, L1_attn_bk, L1_attn_Wv, L1_attn_bv,
           L1_attn_Wo, L1_attn_bo, L1_ffn_w1, L1_ffn_b1, L1_ffn_w2, L1_ffn_b2,
           L1_ffn_gamma, L1_ffn_beta, L1_c, L1_wmod, _taps=(), _n_phases=N_PHASES, _p4cut=9):
    f32 = lambda a: np.ascontiguousarray(np.asarray(a, np.float32))
    x = f32(L0_attn_x).reshape(S, D)

    perm = np.empty(1024, np.int64)
    for p in range(128):
        for cc in range(8):
            perm[8 * p + cc] = 128 * cc + p

    wmods = [f32(L0_wmod), f32(L1_wmod)]
    cs = [f32(L0_c).reshape(-1), f32(L1_c).reshape(-1)]

    eall_h = np.zeros((16, 16 * 128), np.float32)
    for e in range(16):
        eall_h[e, e * 128:(e + 1) * 128] = 1.0

    common = dict(
        ident_in=np.eye(128, dtype=np.float32),
        eall_in=eall_h,
        w_in_t=f32(L0_attn_W_in).T.copy(),
        w_x32_t=f32(L0_attn_W_x)[:DTR].T.copy(),
        w_dt_t=f32(L0_attn_W_dt).T.copy(),
        w_out_t=f32(L0_attn_W_out).T.copy(),
        conv_w_in=np.ascontiguousarray(
            f32(L0_attn_conv_w).reshape(16, 128, 4).transpose(1, 0, 2).reshape(128, 64)),
        conv_b_in=_vec(L0_attn_conv_b),
        b_dt_in=_vec(L0_attn_b_dt),
        gate_w_t=f32(L0_ffn_gate_w).T.copy(),
        gate_b_in=f32(L0_ffn_gate_b).reshape(16, 1),
        w13_t=np.ascontiguousarray(f32(L0_ffn_w13).transpose(0, 2, 1)),
        w2_t=np.ascontiguousarray(f32(L0_ffn_w2).transpose(0, 2, 1)),
        w13s_t=f32(L0_ffn_w13_s).T.copy(),
        w2s_t=f32(L0_ffn_w2_s).T.copy(),
        wq_t=f32(L1_attn_Wq).T.copy(), wk_t=f32(L1_attn_Wk).T.copy(),
        wv_t=f32(L1_attn_Wv).T.copy(), wo_t=f32(L1_attn_Wo).T.copy(),
        bq_in=f32(L1_attn_bq).reshape(128, 1), bk_in=f32(L1_attn_bk).reshape(128, 1),
        bv_in=f32(L1_attn_bv).reshape(128, 1), bo_in=_vec(L1_attn_bo),
        wavel_in=f32(L1_attn_wavelengths).reshape(128, 1),
        phaseb_in=f32(L1_attn_phase_bias).reshape(128, 1),
        w1_t=f32(L1_ffn_w1).T.copy(),
        w2l_t=f32(L1_ffn_w2).T.copy(),
        b1_in=_vec(L1_ffn_b1), b2_in=_vec(L1_ffn_b2),
        gamma_in=_vec(L1_ffn_gamma), beta_in=_vec(L1_ffn_beta),
    )

    in_maps = []
    for r in range(NCOR):
        m = dict(common)
        xs = x[r * T:(r + 1) * T]
        m["xT"] = np.ascontiguousarray(xs.T)
        if r == 0:
            m["xprevT"] = np.zeros((D, 3), np.float32)
            m["halomask"] = np.zeros((128, 1), np.float32)
        else:
            m["xprevT"] = np.ascontiguousarray(x[r * T - 3:r * T].T)
            m["halomask"] = np.ones((128, 1), np.float32)
        cm = np.zeros((2, 128, 8), np.float32)
        wsl = np.zeros((2, D, D), np.float32)
        for idx, s in enumerate((r, 8 + r)):
            if s >= 12:
                continue
            l, v = divmod(s, 6)
            cm[idx] = _vec(cs[l])
            wsl[idx] = wmods[l][:, v * 1024:(v + 1) * 1024][:, perm]
        m["cmod"] = cm
        m["wmodsl"] = wsl
        in_maps.append(m)

    nc = _get_nc(_taps, _n_phases, _p4cut)
    res = run_bass_kernel_spmd(nc, in_maps, core_ids=list(range(NCOR)))
    out = np.concatenate([np.asarray(r["yout"]).T for r in res.results], axis=0)
    if _taps:
        taps_out = {tp: np.concatenate(
            [np.asarray(r["dbg_" + tp]).T for r in res.results], axis=0)
            for tp in _taps}
        return out.reshape(1, S, D), taps_out
    return out.reshape(1, S, D)


# version 06
# speedup vs baseline: 1.1489x; 1.1489x over previous
import math
import numpy as np
import concourse.bass as bass
import concourse.mybir as mybir
from concourse.bass_utils import run_bass_kernel_spmd

F32 = mybir.dt.float32
BF16 = mybir.dt.bfloat16
U8 = mybir.dt.uint8
OP = mybir.AluOpType
AF = mybir.ActivationFunctionType

NCOR = 8
T = 2048
S = 16384
D = 1024
C8 = 8
TT = 512
NT = 4
DI = 2048
NDI = 16
DTR = 32
NEXP = 16
INTER = 512
HD = 128
EPS = 1e-5
MAGIC = 12582912.0
CW1 = 6.28125
CW2 = 0.0019353071795864769
INV2PI = 1.0 / (2 * math.pi)
HALFPI = math.pi / 2
PICL = 3.141592
ASCALE = 1.0 / math.sqrt(128.0)
RSCALE = 2.5
W1ELEMS = 4096.0 * 1024.0

N_PHASES = 6

_SZ = {F32: 4, BF16: 2, U8: 1}


class Arena:
    def __init__(self, nc, total_bytes):
        self.nc = nc
        self.h = nc.alloc_sbuf_tensor("ARENA", [128, total_bytes], U8)
        self.base = nc.lookup_mloc(self.h).addr
        self.off = 0
        self.total = total_bytes
        self.n = 0

    def at(self, off, name, shape, dtype):
        per = int(np.prod(shape[1:])) * _SZ[dtype]
        assert off % 32 == 0 and off + per <= self.total, (name, off, per, self.total)
        self.n += 1
        return self.nc.alloc_sbuf_tensor_at(
            f"{name}_{self.n}", list(shape), dtype, offset=self.base + off)

    def t(self, name, shape, dtype):
        off = (self.off + 31) // 32 * 32
        h = self.at(off, name, shape, dtype)
        self.off = off + int(np.prod(shape[1:])) * _SZ[dtype]
        return h

    def mark(self):
        return (self.off + 31) // 32 * 32

    def reset(self, off):
        self.off = off


class K:
    def __init__(self, nc, block, sems):
        self.nc = nc
        self.block = block
        self.sem = sems
        self.c = {s: 0 for s in sems}

    def inc(self, instr, s, n=1):
        instr.then_inc(self.sem[s], n)
        self.c[s] += n
        return self.c[s]

    def w(self, eng, s, v):
        if v > 0:
            eng.wait_ge(self.sem[s], v)

    def barrier(self):
        snap = dict(self.c)
        own = {"tensor": "pe", "vector": "dv", "scalar": "ac", "gpsimd": "gp",
               "sync": None}
        for ename in ("sync", "tensor", "vector", "scalar", "gpsimd"):
            def _mk(ename=ename):
                def _(eng):
                    for s, v in snap.items():
                        if s != own[ename]:
                            self.w(eng, s, v)
                return _
            getattr(self.block, ename)(_mk())


SEMS = ["ld", "lxa", "lxb", "gd", "gwa", "gwb", "qwa", "qwb",
        "twa", "twb", "sxr", "sto", "cc", "pe", "dv", "ac", "gp"]


def build_program(taps=(), n_phases=N_PHASES, p4cut=9):
    nc = bass.Bass("TRN2", target_bir_lowering=False)
    taps = list(taps)

    def din(name, shape, dtype=F32):
        return nc.dram_tensor(name, list(shape), dtype, kind="ExternalInput").ap()

    xT = din("xT", [D, T])
    xprevT = din("xprevT", [D, 3])
    halomask = din("halomask", [128, 1])
    ident_in = din("ident_in", [128, 128])
    eall_in = din("eall_in", [16, 16 * 128])
    w_in_t = din("w_in_t", [D, 4096])
    w_x32_t = din("w_x32_t", [DI, DTR])
    w_dt_t = din("w_dt_t", [DTR, DI])
    w_out_t = din("w_out_t", [DI, D])
    conv_w_in = din("conv_w_in", [128, 64])
    conv_b_in = din("conv_b_in", [128, 16])
    b_dt_in = din("b_dt_in", [128, 16])
    gate_w_t = din("gate_w_t", [D, 16])
    gate_b_in = din("gate_b_in", [16, 1])
    w13_t = din("w13_t", [NEXP, D, 2 * INTER])
    w2_t = din("w2_t", [NEXP, INTER, D])
    w13s_t = din("w13s_t", [D, 2 * INTER])
    w2s_t = din("w2s_t", [INTER, D])
    wq_t = din("wq_t", [D, HD]); wk_t = din("wk_t", [D, HD]); wv_t = din("wv_t", [D, HD])
    wo_t = din("wo_t", [HD, D])
    bq_in = din("bq_in", [128, 1]); bk_in = din("bk_in", [128, 1]); bv_in = din("bv_in", [128, 1])
    bo_in = din("bo_in", [128, 8])
    wavel_in = din("wavel_in", [128, 1]); phaseb_in = din("phaseb_in", [128, 1])
    w1_t = din("w1_t", [D, 4096])
    w2l_t = din("w2l_t", [4096, D])
    b1_in = din("b1_in", [128, 32]); b2_in = din("b2_in", [128, 8])
    gamma_in = din("gamma_in", [128, 8]); beta_in = din("beta_in", [128, 8])
    cmod = din("cmod", [2, 128, 8])
    wmodsl = din("wmodsl", [2, D, D])

    yout = nc.dram_tensor("yout", [D, T], F32, kind="ExternalOutput").ap()
    dbg = {tp: nc.dram_tensor("dbg_" + tp, [D, T], F32, kind="ExternalOutput").ap()
           for tp in taps}

    xres = nc.dram_tensor("xres", [D, T], F32).ap()
    modin = nc.dram_tensor("modin", [2, D], F32)
    modout = nc.dram_tensor("modout", [16, D], F32)
    kvin = nc.dram_tensor("kvin", [384, T], BF16)
    kvout = nc.dram_tensor("kvout", [384 * NCOR, T], BF16)
    wq1d = nc.dram_tensor("wq1d", [D, 4096], BF16).ap()
    wq2d = nc.dram_tensor("wq2d", [4096, D], BF16).ap()

    AR = Arena(nc, 212480)
    _pp0 = nc.alloc_psum_tensor("pp0", [128, 1024], F32).ap()
    _pp1 = nc.alloc_psum_tensor("pp1", [128, 1024], F32).ap()
    _ps_mid = [nc.alloc_psum_tensor(f"ps{i}", [128, 512], F32).ap() for i in range(2, 6)]
    ps = [_pp0[:, 0:512], _pp0[:, 512:1024],
          _ps_mid[0], _ps_mid[1], _ps_mid[2], _ps_mid[3],
          _pp1[:, 0:512], _pp1[:, 512:1024]]
    pspair = [_pp0, _pp1]

    def chw(t, c, w):
        return t.ap()[:, c * w:(c + 1) * w]

    identf = AR.t("identf", [128, 128], F32)
    identb = AR.t("identb", [128, 128], BF16)
    eall = AR.t("eall", [16, 16 * 128], BF16)
    ones_cb = AR.t("ones_cb", [128, 1], BF16)
    ones_cf = AR.t("ones_cf", [128, 1], F32)
    ones_rb = AR.t("ones_rb", [1, 128], BF16)
    ones_rf = AR.t("ones_rf", [1, 128], F32)
    cst = AR.t("cst", [128, 2], F32)
    hmsk = AR.t("hmsk", [128, 1], F32)
    convw = AR.t("convw", [128, 64], F32)
    convb = AR.t("convb", [128, 16], F32)
    bdt = AR.t("bdt", [128, 16], F32)
    gatewt = AR.t("gatewt", [128, 8 * 16], F32)
    gateb = AR.t("gateb", [16, 1], F32)
    cmod_s = AR.t("cmod_s", [128, 16], F32)
    bq_s = AR.t("bq_s", [128, 1], F32); bk_s = AR.t("bk_s", [128, 1], F32)
    bv_s = AR.t("bv_s", [128, 1], F32); bo_s = AR.t("bo_s", [128, 8], F32)
    invw = AR.t("invw", [128, 1], F32); phb = AR.t("phb", [128, 1], F32)
    b1_s = AR.t("b1_s", [128, 32], F32); b2_s = AR.t("b2_s", [128, 8], F32)
    gam_s = AR.t("gam_s", [128, 8], F32); bet_s = AR.t("bet_s", [128, 8], F32)
    wqkv_bf = AR.t("wqkv_bf", [128, C8 * 3 * HD], BF16)
    wo_bf = AR.t("wo_bf", [HD, 1024], BF16)
    wmean = AR.t("wmean", [128, 2], F32)
    wred = AR.t("wred", [128, 1], F32)
    swrep = AR.t("swrep", [128, 2], F32)
    cmrep = AR.t("cmrep", [128, 2], F32)
    modv = {}
    for l in (0, 1):
        for nm in ("sa", "ca1", "ga", "sb", "cb1", "gb"):
            modv[(l, nm)] = AR.t(f"m{l}{nm}", [128, 8], F32)
    gtmp = AR.t("gtmp", [128, 8], F32)
    gtmp2 = AR.t("gtmp2", [128, 8], F32)
    rowm = AR.t("rowm", [1, TT], F32)
    rowq = AR.t("rowq", [1, TT], F32)
    rowt = AR.t("rowt", [1, TT], F32)
    rhiA = AR.t("rhiA", [1, TT], BF16)
    rloA = AR.t("rloA", [1, TT], F32)
    rloAb = AR.t("rloAb", [1, TT], BF16)
    rhiB = AR.t("rhiB", [1, TT], BF16)
    rloB = AR.t("rloB", [1, TT], F32)
    rloBb = AR.t("rloBb", [1, TT], BF16)
    rw4 = AR.t("rw4", [1, 4], F32)
    rw4h = AR.t("rw4h", [1, 4], BF16)
    rw4l = AR.t("rw4l", [1, 4], F32)
    rw4lb = AR.t("rw4lb", [1, 4], BF16)

    P_BASE = AR.mark()

    import contextlib
    _es = contextlib.ExitStack()
    sems = {s: _es.enter_context(nc.semaphore("sm_" + s)) for s in SEMS}
    block = _es.enter_context(nc.Block())
    k = K(nc, block, sems)

    def SY(f): block.sync(f)
    def TE(f): block.tensor(f)
    def VE(f): block.vector(f)
    def SC(f): block.scalar(f)
    def GP(f): block.gpsimd(f)

    def emit_ln(xap, xbf_t, out_fn, s1c, s2c, W, tmpf, tmp2, prev_apply_dv,
                extra_gp=None, extra_ac=None):
        def _gp(g):
            if extra_gp is not None:
                k.w(g, extra_gp[0], extra_gp[1])
            for c in range(C8):
                ins = g.tensor_copy(chw(xbf_t, c, W), xap(c))
            k.inc(ins, "gp", 1)
        GP(_gp); gp_cast = k.c["gp"]

        def _pe1(t):
            k.w(t, "gp", gp_cast)
            k.w(t, "dv", prev_apply_dv)
            for c in range(C8):
                ins = t.matmul(ps[2][0:1, 0:W], ones_cb.ap(), chw(xbf_t, c, W),
                               start=(c == 0), stop=(c == C8 - 1))
            k.inc(ins, "pe", 1)
        TE(_pe1); pe_s1 = k.c["pe"]

        def _ac(a):
            k.w(a, "pe", pe_s1)
            if extra_ac is not None:
                k.w(a, extra_ac[0], extra_ac[1])
            for c in range(C8):
                ins = a.activation(chw(xbf_t, c, W), xap(c), AF.Square)
            k.inc(ins, "ac", 1)
        SC(_ac); ac_sq = k.c["ac"]

        def _pe2(t):
            k.w(t, "ac", ac_sq)
            for c in range(C8):
                ins = t.matmul(ps[3][0:1, 0:W], ones_cb.ap(), chw(xbf_t, c, W),
                               start=(c == 0), stop=(c == C8 - 1))
            k.inc(ins, "pe", 1)
        TE(_pe2); pe_s2 = k.c["pe"]

        def _dvr(v):
            k.w(v, "pe", pe_s2)
            rm = rowm.ap()[:, 0:W]; rq = rowq.ap()[:, 0:W]; rt = rowt.ap()[:, 0:W]
            v.tensor_scalar(rm, ps[2][0:1, 0:W], 1.0 / D, None, op0=OP.mult)
            v.tensor_scalar(rq, ps[3][0:1, 0:W], 1.0 / D, None, op0=OP.mult)
            v.drain()
            v.scalar_tensor_tensor(rt, rm, -1.0, rm, op0=OP.mult, op1=OP.mult)
            v.drain()
            v.tensor_tensor(rt, rq, rt, op=OP.add)
            k.inc(v.drain(), "dv", 1)
        VE(_dvr); dv_r1 = k.c["dv"]

        def _acs(a):
            k.w(a, "dv", dv_r1)
            ins = a.activation(rowt.ap()[0:1, 0:W], rowt.ap()[0:1, 0:W],
                               AF.Sqrt, bias=cst.ap()[0:1, 0:1])
            k.inc(ins, "ac", 1)
        SC(_acs); ac_sd = k.c["ac"]

        def _dvr2(v):
            k.w(v, "ac", ac_sd)
            rt = rowt.ap()[:, 0:W]
            v.reciprocal(rt, rt)
            v.drain()
            v.tensor_copy(rhiA.ap()[:, 0:W], rowm.ap()[:, 0:W])
            v.tensor_copy(rhiB.ap()[:, 0:W], rt)
            v.drain()
            v.tensor_tensor(rloAb.ap()[:, 0:W], rowm.ap()[:, 0:W], rhiA.ap()[:, 0:W],
                            op=OP.subtract)
            v.tensor_tensor(rloBb.ap()[:, 0:W], rt, rhiB.ap()[:, 0:W], op=OP.subtract)
            k.inc(v.drain(), "dv", 1)
        VE(_dvr2); dv_r2 = k.c["dv"]

        def _pebc(t):
            k.w(t, "dv", dv_r2)
            t.matmul(ps[4][:, 0:W], ones_rb.ap(), rhiA.ap()[:, 0:W], start=True, stop=False)
            t.matmul(ps[4][:, 0:W], ones_rb.ap(), rloAb.ap()[:, 0:W], start=False, stop=True)
            t.matmul(ps[5][:, 0:W], ones_rb.ap(), rhiB.ap()[:, 0:W], start=True, stop=False)
            ins = t.matmul(ps[5][:, 0:W], ones_rb.ap(), rloBb.ap()[:, 0:W], start=False, stop=True)
            k.inc(ins, "pe", 1)
        TE(_pebc); pe_bc = k.c["pe"]

        def _dva(v):
            k.w(v, "pe", pe_bc)
            t0 = tmpf.ap()[:, 0:W]; t1 = tmp2.ap()[:, 0:W]
            for c0 in range(0, C8, 2):
                v.tensor_tensor(t0, xap(c0), ps[4][:, 0:W], op=OP.subtract)
                v.tensor_tensor(t1, xap(c0 + 1), ps[4][:, 0:W], op=OP.subtract)
                v.drain()
                v.tensor_tensor(out_fn(c0), t0, ps[5][:, 0:W], op=OP.mult)
                v.tensor_tensor(out_fn(c0 + 1), t1, ps[5][:, 0:W], op=OP.mult)
                v.drain()
                v.tensor_scalar(out_fn(c0), out_fn(c0), s1c(c0), s2c(c0),
                                op0=OP.mult, op1=OP.add)
                v.tensor_scalar(out_fn(c0 + 1), out_fn(c0 + 1), s1c(c0 + 1), s2c(c0 + 1),
                                op0=OP.mult, op1=OP.add)
                v.drain()
            k.inc(v.nop(), "dv", 1)
        VE(_dva)
        return k.c["dv"]


    def emit_sincos_multi(chains, W, pre_dv=0):
        ac_prev = k.c["ac"]

        def _dv1(v):
            k.w(v, "ac", ac_prev)
            k.w(v, "dv", pre_dv)
            for th, co, si, u, n in chains:
                v.tensor_scalar(n, th, INV2PI, MAGIC, op0=OP.mult, op1=OP.add)
            v.drain()
            for th, co, si, u, n in chains:
                v.tensor_scalar(n, n, MAGIC, None, op0=OP.subtract)
            v.drain()
            k.inc(v.nop(), "dv", 1)
        VE(_dv1); e1 = k.c["dv"]

        def _gpf(g):
            k.w(g, "dv", e1)
            for th, co, si, u, n in chains:
                ins = g.tensor_copy(n.bitcast(mybir.dt.int32), n.bitcast(mybir.dt.int32))
            k.inc(ins, "gp", 1)
        GP(_gpf); e2 = k.c["gp"]

        def _dv2(v):
            k.w(v, "gp", e2)
            for th, co, si, u, n in chains:
                v.scalar_tensor_tensor(u, n, -CW1, th, op0=OP.mult, op1=OP.add)
            v.drain()
            for th, co, si, u, n in chains:
                v.scalar_tensor_tensor(u, n, -CW2, u, op0=OP.mult, op1=OP.add)
            v.drain()
            for th, co, si, u, n in chains:
                v.tensor_scalar(u, u, PICL, -PICL, op0=OP.min, op1=OP.max)
            v.drain()
            k.inc(v.nop(), "dv", 1)
        VE(_dv2); e3 = k.c["dv"]

        def _ac1(a):
            k.w(a, "dv", e3)
            for th, co, si, u, n in chains:
                ins = a.activation(si, u, AF.Sin)
            k.inc(ins, "ac", 1)
        SC(_ac1); e4 = k.c["ac"]

        def _dv3(v):
            k.w(v, "ac", e4)
            for th, co, si, u, n in chains:
                v.tensor_scalar(n, th, INV2PI, 0.25, op0=OP.mult, op1=OP.add)
            v.drain()
            for th, co, si, u, n in chains:
                v.tensor_scalar(n, n, MAGIC, None, op0=OP.add)
            v.drain()
            for th, co, si, u, n in chains:
                v.tensor_scalar(n, n, MAGIC, None, op0=OP.subtract)
            v.drain()
            k.inc(v.nop(), "dv", 1)
        VE(_dv3); e5 = k.c["dv"]

        def _gpf2(g):
            k.w(g, "dv", e5)
            for th, co, si, u, n in chains:
                ins = g.tensor_copy(n.bitcast(mybir.dt.int32), n.bitcast(mybir.dt.int32))
            k.inc(ins, "gp", 1)
        GP(_gpf2); e6 = k.c["gp"]

        def _dv4(v):
            k.w(v, "gp", e6)
            for th, co, si, u, n in chains:
                v.scalar_tensor_tensor(u, n, -CW1, th, op0=OP.mult, op1=OP.add)
            v.drain()
            for th, co, si, u, n in chains:
                v.scalar_tensor_tensor(u, n, -CW2, u, op0=OP.mult, op1=OP.add)
            v.drain()
            for th, co, si, u, n in chains:
                v.tensor_scalar(u, u, PICL - HALFPI, -PICL - HALFPI, op0=OP.min, op1=OP.max)
            v.drain()
            k.inc(v.nop(), "dv", 1)
        VE(_dv4); e7 = k.c["dv"]

        def _ac2(a):
            k.w(a, "dv", e7)
            for th, co, si, u, n in chains:
                ins = a.activation(co, u, AF.Sin, bias=cst.ap()[:, 1:2])
            k.inc(ins, "ac", 1)
        SC(_ac2)
        return k.c["ac"]

    def emit_sincos(th_ap, cos_out, sin_out, W, u_f, n_f):
        ac_prev = k.c["ac"]

        def _dv1(v):
            k.w(v, "ac", ac_prev)
            v.tensor_scalar(n_f, th_ap, INV2PI, MAGIC, op0=OP.mult, op1=OP.add)
            v.drain()
            v.tensor_scalar(n_f, n_f, MAGIC, None, op0=OP.subtract)
            v.drain()
            k.inc(v.nop(), "dv", 1)
        VE(_dv1); e1 = k.c["dv"]

        def _gpf(g):
            k.w(g, "dv", e1)
            ins = g.tensor_copy(n_f.bitcast(mybir.dt.int32), n_f.bitcast(mybir.dt.int32))
            k.inc(ins, "gp", 1)
        GP(_gpf); e2 = k.c["gp"]

        def _dv2(v):
            k.w(v, "gp", e2)
            v.scalar_tensor_tensor(u_f, n_f, -CW1, th_ap, op0=OP.mult, op1=OP.add)
            v.drain()
            v.scalar_tensor_tensor(u_f, n_f, -CW2, u_f, op0=OP.mult, op1=OP.add)
            v.drain()
            v.tensor_scalar(u_f, u_f, PICL, -PICL, op0=OP.min, op1=OP.max)
            v.drain()
            k.inc(v.nop(), "dv", 1)
        VE(_dv2); e3 = k.c["dv"]

        def _ac1(a):
            k.w(a, "dv", e3)
            ins = a.activation(sin_out, u_f, AF.Sin)
            k.inc(ins, "ac", 1)
        SC(_ac1); e4 = k.c["ac"]

        def _dv3(v):
            k.w(v, "ac", e4)
            v.tensor_scalar(n_f, th_ap, INV2PI, 0.25, op0=OP.mult, op1=OP.add)
            v.drain()
            v.tensor_scalar(n_f, n_f, MAGIC, None, op0=OP.add)
            v.drain()
            v.tensor_scalar(n_f, n_f, MAGIC, None, op0=OP.subtract)
            v.drain()
            k.inc(v.nop(), "dv", 1)
        VE(_dv3); e5 = k.c["dv"]

        def _gpf2(g):
            k.w(g, "dv", e5)
            ins = g.tensor_copy(n_f.bitcast(mybir.dt.int32), n_f.bitcast(mybir.dt.int32))
            k.inc(ins, "gp", 1)
        GP(_gpf2); e6 = k.c["gp"]

        def _dv4(v):
            k.w(v, "gp", e6)
            v.scalar_tensor_tensor(u_f, n_f, -CW1, th_ap, op0=OP.mult, op1=OP.add)
            v.drain()
            v.scalar_tensor_tensor(u_f, n_f, -CW2, u_f, op0=OP.mult, op1=OP.add)
            v.drain()
            v.tensor_scalar(u_f, u_f, PICL - HALFPI, -PICL - HALFPI, op0=OP.min, op1=OP.max)
            v.drain()
            k.inc(v.nop(), "dv", 1)
        VE(_dv4); e7 = k.c["dv"]

        def _ac2(a):
            k.w(a, "dv", e7)
            ins = a.activation(cos_out, u_f, AF.Sin, bias=cst.ap()[:, 1:2])
            k.inc(ins, "ac", 1)
        SC(_ac2)
        return k.c["ac"]

    def _sy0(sy):
        for src, dst in [(ident_in, identf.ap()), (conv_w_in, convw.ap()),
                         (conv_b_in, convb.ap()), (b_dt_in, bdt.ap()),
                         (gate_b_in, gateb.ap()),
                         (bq_in, bq_s.ap()), (bk_in, bk_s.ap()), (bv_in, bv_s.ap()),
                         (bo_in, bo_s.ap()), (wavel_in, invw.ap()), (phaseb_in, phb.ap()),
                         (b1_in, b1_s.ap()), (b2_in, b2_s.ap()),
                         (gamma_in, gam_s.ap()), (beta_in, bet_s.ap()),
                         (halomask, hmsk.ap())]:
            k.inc(sy.dma_start(out=dst, in_=src), "ld", 16)
        k.inc(sy.dma_start(out=gatewt.ap().rearrange("p (c e) -> p c e", e=16),
                           in_=gate_w_t.rearrange("(c p) e -> p c e", p=128)), "ld", 16)
        k.inc(sy.dma_start(out=cmod_s.ap().rearrange("p (s c) -> p s c", c=8),
                           in_=cmod.rearrange("s p c -> p s c")), "ld", 16)
    SY(_sy0)
    ld0 = k.c["ld"]

    AR.reset(P_BASE)
    w_in_bf = AR.t("w_in_bf", [128, C8 * 4096], BF16)
    w_out_bf = AR.t("w_out_bf", [128, NDI * 1024], BF16)
    w_x32_bf = AR.t("w_x32_bf", [128, NDI * DTR], BF16)
    w_dt_bf = AR.t("w_dt_bf", [DTR, DI], BF16)

    def _gp0(g):
        k.inc(g.dma_start(out=w_in_bf.ap().rearrange("p (c f) -> p c f", f=4096),
                          in_=w_in_t.rearrange("(c p) f -> p c f", p=128)), "gd", 16)
        k.inc(g.dma_start(out=w_out_bf.ap().rearrange("p (c f) -> p c f", f=1024),
                          in_=w_out_t.rearrange("(c p) f -> p c f", p=128)), "gd", 16)
        k.inc(g.dma_start(out=w_x32_bf.ap().rearrange("p (c f) -> p c f", f=DTR),
                          in_=w_x32_t.rearrange("(c p) f -> p c f", p=128)), "gd", 16)
        k.inc(g.dma_start(out=w_dt_bf.ap(), in_=w_dt_t), "gd", 16)
        for i, wsrc in enumerate((wq_t, wk_t, wv_t)):
            k.inc(g.dma_start(
                out=wqkv_bf.ap().rearrange("p (c i f) -> p i c f", i=3, f=HD)[:, i],
                in_=wsrc.rearrange("(c p) f -> p c f", p=128)), "gd", 16)
        k.inc(g.dma_start(out=wo_bf.ap(), in_=wo_t), "gd", 16)
        k.inc(g.dma_start(out=eall.ap(), in_=eall_in), "gd", 16)
    GP(_gp0)
    gd0 = k.c["gd"]

    def _v0(v):
        k.w(v, "ld", ld0)
        v.memset(ones_cb.ap(), 1.0)
        v.memset(ones_cf.ap(), 1.0)
        v.memset(ones_rb.ap(), 1.0)
        v.memset(ones_rf.ap(), 1.0)
        v.memset(cst.ap()[:, 0:1], EPS)
        v.memset(cst.ap()[:, 1:2], HALFPI)
        v.memset(wmean.ap(), 0.0)
        v.tensor_copy(identb.ap(), identf.ap())
        v.tensor_scalar(invw.ap(), invw.ap(), 1e-8, None, op0=OP.add)
        v.drain()
        v.reciprocal(invw.ap(), invw.ap())
        v.drain()
        k.inc(v.nop(), "dv", 1)
    VE(_v0)
    dv0 = k.c["dv"]

    ph0_mark = AR.mark()
    wms = AR.t("wms", [128, C8 * 1024], F32)
    silc = AR.t("silc", [128, 8], F32)
    modrow = AR.t("modrow", [1, 1024], F32)

    for sl in (0, 1):
        def _sy(sy, sl=sl):
            k.w(sy, "pe", k.c["pe"])
            k.inc(sy.dma_start(out=wms.ap().rearrange("p (c f) -> p c f", f=1024),
                               in_=wmodsl[sl].rearrange("(c p) f -> p c f", p=128)), "ld", 16)
        SY(_sy); ldv = k.c["ld"]

        def _ac(a, sl=sl, ldv=ldv):
            k.w(a, "ld", ldv)
            k.inc(a.activation(silc.ap(), cmod_s.ap()[:, sl * 8:(sl + 1) * 8], AF.Silu), "ac", 1)
        SC(_ac); acv = k.c["ac"]

        def _pe(t, ldv=ldv, acv=acv):
            k.w(t, "ld", ldv)
            k.w(t, "ac", acv)
            k.w(t, "dv", max(dv0, k.c["dv"]))
            for half in (0, 1):
                for kk in range(C8):
                    ins = t.matmul(ps[half][0:1, :], silc.ap()[:, kk:kk + 1],
                                   chw(wms, kk, 1024)[:, half * 512:(half + 1) * 512],
                                   start=(kk == 0), stop=(kk == C8 - 1))
            k.inc(ins, "pe", 1)
        TE(_pe); pev = k.c["pe"]

        def _dv(v, pev=pev):
            k.w(v, "pe", pev)
            v.tensor_copy(modrow.ap()[:, 0:512], ps[0][0:1, :])
            k.inc(v.tensor_copy(modrow.ap()[:, 512:1024], ps[1][0:1, :]), "dv", 1)
        VE(_dv); dvv = k.c["dv"]

        def _sy2(sy, sl=sl, dvv=dvv):
            k.w(sy, "dv", dvv)
            k.inc(sy.dma_start(out=modin[sl:sl + 1, :], in_=modrow.ap()), "sxr", 16)
        SY(_sy2)
    sxr_mod = k.c["sxr"]

    def _gpcc(g):
        k.w(g, "sxr", sxr_mod)
        k.inc(g.collective_compute(
            "AllGather", OP.bypass, replica_groups=[list(range(NCOR))],
            ins=[modin.ap().opt()], outs=[modout.ap().opt()]), "cc", 1)
    GP(_gpcc)
    cc_mod = k.c["cc"]

    names6 = ("sa", "ca1", "ga", "sb", "cb1", "gb")
    def _symod(sy):
        k.w(sy, "cc", cc_mod)
        for l in (0, 1):
            for vv in range(6):
                s = 6 * l + vv
                row = 2 * s if s < 8 else 2 * (s - 8) + 1
                k.inc(sy.dma_start(
                    out=modv[(l, names6[vv])].ap(),
                    in_=modout.ap()[row:row + 1, :].rearrange("o (p c) -> o p c", p=128)[0]),
                    "ld", 16)
    SY(_symod)
    ld_mod = k.c["ld"]

    def _vca(v):
        k.w(v, "ld", ld_mod)
        for l in (0, 1):
            v.tensor_scalar(modv[(l, "ca1")].ap(), modv[(l, "ca1")].ap(), 1.0, None, op0=OP.add)
            v.tensor_scalar(modv[(l, "cb1")].ap(), modv[(l, "cb1")].ap(), 1.0, None, op0=OP.add)
        v.drain()
        k.inc(v.nop(), "dv", 1)
    VE(_vca)

    for l in (0, 1):
        for gn in ("ga", "gb"):
            def _acg(a, l=l, gn=gn):
                k.w(a, "ld", ld_mod)
                k.w(a, "dv", k.c["dv"])
                a.activation(gtmp.ap(), modv[(l, gn)].ap(), AF.Abs)
                a.drain()
                a.activation(gtmp2.ap(), modv[(l, gn)].ap(), AF.Sign)
                a.drain()
                k.inc(a.nop(), "ac", 1)
            SC(_acg); acv = k.c["ac"]

            def _vg(v, l=l, gn=gn, acv=acv):
                k.w(v, "ac", acv)
                v.tensor_scalar(gtmp.ap(), gtmp.ap(), 0.25, None, op0=OP.max)
                v.drain()
                v.tensor_tensor(modv[(l, gn)].ap(), gtmp2.ap(), gtmp.ap(), op=OP.mult)
                v.drain()
                k.inc(v.nop(), "dv", 1)
            VE(_vg)

    AR.reset(ph0_mark)
    k.barrier()

    if "modv" in dbg:
        def _symv(sy):
            i = 0
            for l in (0, 1):
                for nm in names6:
                    k.inc(sy.dma_start(out=dbg["modv"][0:128, i * 8:(i + 1) * 8],
                                       in_=modv[(l, nm)].ap()), "sto", 16)
                    i += 1
        SY(_symv)
        k.barrier()

    tap_points = {}

    def emit_tap(name):
        if name not in dbg:
            return
        def _syt(sy, name=name):
            k.inc(sy.dma_start(out=dbg[name], in_=xres), "sto", 16)
        SY(_syt)
        k.barrier()

    if n_phases >= 1:
        ph1_mark = AR.mark()
        _xt0 = AR.t("xtl0", [128, C8 * TT], F32)
        xtile = [_xt0, _xt0]
        h_bf = AR.t("h_bf", [128, C8 * TT], BF16)
        zs_b = AR.t("zs_b", [128, NDI * TT], BF16)
        xbf = AR.at(AR.mark() - NDI * TT * 2, "xbf", [128, C8 * TT], BF16)
        xs_b = AR.t("xs_b", [128, NDI * TT], BF16)
        tail_b = AR.t("tail_b", [128, NDI * 3], BF16)
        dt_b = AR.t("dt_b", [DTR, TT], BF16)
        scr0 = AR.mark()
        tmpD = AR.at(scr0, "tmpD", [128, 2 * TT], F32)
        tmpL = AR.at(scr0, "tmpL", [128, 2 * TT], F32)
        tmpG = AR.at(scr0 + 4096, "tmpG", [128, 2 * TT], F32)
        avf = AR.at(scr0 + 4096, "avf", [128, 2 * TT], F32)
        tGm = AR.at(scr0 + 8192, "tGm", [128, 2 * TT], F32)
        tmpG2 = AR.at(scr0 + 8192, "tmpG2", [128, TT], F32)
        tmpA = AR.at(scr0 + 8192 + 2048, "tmpA", [128, TT], F32)
        hhalo = AR.at(scr0, "hhalo", [128, C8 * 4], BF16)
        xhal = AR.at(scr0 + 256, "xhal", [128, C8 * 4], F32)
        xsqh = AR.at(scr0 + 512, "xsqh", [128, C8 * 4], F32)
        hrow = AR.at(scr0 + 768, "hrow", [1, 16], F32)
        xshT = AR.at(scr0 + 1024, "xshT", [3, 512], F32)
        AR.reset(scr0 + 12288)
        lnt2a = AR.t("lnt2a", [128, TT], F32)
        assert AR.mark() <= 212480

        def _syh(sy):
            k.inc(sy.dma_start(
                out=xhal.ap().rearrange("p (c w) -> p c w", w=4)[:, :, 0:3],
                in_=xprevT.rearrange("(c p) t -> p c t", p=128)), "ld", 16)
        SY(_syh); ld_h = k.c["ld"]

        def _peh1(t):
            k.w(t, "ld", ld_h)
            for c in range(C8):
                ins = t.matmul(ps[4][0:1, 0:3], ones_cf.ap(), chw(xhal, c, 4)[:, 0:3],
                               start=(c == 0), stop=(c == C8 - 1))
            k.inc(ins, "pe", 1)
        TE(_peh1); pe_h1 = k.c["pe"]

        def _ach(a):
            k.w(a, "ld", ld_h)
            for c in range(C8):
                ins = a.activation(chw(xsqh, c, 4)[:, 0:3], chw(xhal, c, 4)[:, 0:3], AF.Square)
            k.inc(ins, "ac", 1)
        SC(_ach); ac_h = k.c["ac"]

        def _peh2(t):
            k.w(t, "ac", ac_h)
            for c in range(C8):
                ins = t.matmul(ps[5][0:1, 0:3], ones_cf.ap(), chw(xsqh, c, 4)[:, 0:3],
                               start=(c == 0), stop=(c == C8 - 1))
            k.inc(ins, "pe", 1)
        TE(_peh2); pe_h2 = k.c["pe"]

        def _dvh(v):
            k.w(v, "pe", pe_h2)
            rm = hrow.ap()[:, 0:3]; rq = hrow.ap()[:, 4:7]; rt = hrow.ap()[:, 8:11]
            v.tensor_scalar(rm, ps[4][0:1, 0:3], 1.0 / D, None, op0=OP.mult)
            v.tensor_scalar(rq, ps[5][0:1, 0:3], 1.0 / D, None, op0=OP.mult)
            v.drain()
            v.tensor_tensor(rt, rm, rm, op=OP.mult)
            v.drain()
            v.tensor_tensor(rt, rq, rt, op=OP.subtract)
            v.drain()
            k.inc(v.nop(), "dv", 1)
        VE(_dvh); dv_h1 = k.c["dv"]

        def _ach2(a):
            k.w(a, "dv", dv_h1)
            ins = a.activation(hrow.ap()[0:1, 8:11], hrow.ap()[0:1, 8:11],
                               AF.Sqrt, bias=cst.ap()[0:1, 0:1])
            k.inc(ins, "ac", 1)
        SC(_ach2); ac_h2 = k.c["ac"]

        def _dvh2(v):
            k.w(v, "ac", ac_h2)
            v.reciprocal(hrow.ap()[:, 8:11], hrow.ap()[:, 8:11])
            v.drain()
            v.tensor_copy(rhiA.ap()[:, 0:3], hrow.ap()[:, 0:3])
            v.tensor_copy(rhiB.ap()[:, 0:3], hrow.ap()[:, 8:11])
            v.drain()
            v.tensor_tensor(rloA.ap()[:, 0:3], hrow.ap()[:, 0:3], rhiA.ap()[:, 0:3], op=OP.subtract)
            v.tensor_tensor(rloB.ap()[:, 0:3], hrow.ap()[:, 8:11], rhiB.ap()[:, 0:3], op=OP.subtract)
            v.drain()
            v.tensor_copy(rloAb.ap()[:, 0:3], rloA.ap()[:, 0:3])
            v.tensor_copy(rloBb.ap()[:, 0:3], rloB.ap()[:, 0:3])
            k.inc(v.nop(), "dv", 1)
        VE(_dvh2); dv_h2 = k.c["dv"]

        def _peh3(t):
            k.w(t, "dv", dv_h2)
            t.matmul(ps[4][:, 4:7], ones_rb.ap(), rhiA.ap()[:, 0:3], start=True, stop=False)
            t.matmul(ps[4][:, 4:7], ones_rb.ap(), rloAb.ap()[:, 0:3], start=False, stop=True)
            t.matmul(ps[5][:, 4:7], ones_rb.ap(), rhiB.ap()[:, 0:3], start=True, stop=False)
            ins = t.matmul(ps[5][:, 4:7], ones_rb.ap(), rloBb.ap()[:, 0:3], start=False, stop=True)
            k.inc(ins, "pe", 1)
        TE(_peh3); pe_h3 = k.c["pe"]

        def _dvh3(v):
            k.w(v, "pe", pe_h3)
            for c in range(C8):
                tf = tmpA.ap()[:, 0:3]
                v.tensor_tensor(tf, chw(xhal, c, 4)[:, 0:3], ps[4][:, 4:7], op=OP.subtract)
                v.drain()
                v.tensor_tensor(tf, tf, ps[5][:, 4:7], op=OP.mult)
                v.drain()
                v.tensor_scalar(chw(hhalo, c, 4)[:, 0:3], tf,
                                modv[(0, "ca1")].ap()[:, c:c + 1],
                                modv[(0, "sa")].ap()[:, c:c + 1],
                                op0=OP.mult, op1=OP.add)
                v.drain()
            k.inc(v.nop(), "dv", 1)
        VE(_dvh3); dv_h3 = k.c["dv"]

        def _peh4(t):
            k.w(t, "dv", dv_h3)
            k.w(t, "gd", 16)
            for kk in range(C8):
                for ft in range(4):
                    ins = t.matmul(ps[ft][0:3, :], chw(hhalo, kk, 4)[:, 0:3],
                                   chw(w_in_bf, kk, 4096)[:, ft * 512:(ft + 1) * 512],
                                   start=(kk == 0), stop=(kk == C8 - 1))
            k.inc(ins, "pe", 1)
        TE(_peh4); pe_h4 = k.c["pe"]

        for ft in range(4):
            def _dvh4(v, ft=ft):
                k.w(v, "pe", pe_h4)
                v.tensor_copy(xshT.ap()[0:3, :], ps[ft][0:3, :])
                v.drain()
                k.inc(v.nop(), "dv", 1)
            VE(_dvh4); dv_x = k.c["dv"]

            def _peh5(t, dv_x=dv_x):
                k.w(t, "dv", dv_x)
                for c4 in range(4):
                    ins = t.transpose(ps[6][:, c4 * 4:c4 * 4 + 3],
                                      xshT.ap()[0:3, c4 * 128:(c4 + 1) * 128],
                                      identf.ap()[0:3, 0:3])
                k.inc(ins, "pe", 1)
            TE(_peh5); pe_x = k.c["pe"]

            def _dvh5(v, ft=ft, pe_x=pe_x):
                k.w(v, "pe", pe_x)
                for c4 in range(4):
                    c = ft * 4 + c4
                    v.tensor_scalar(chw(tail_b, c, 3), ps[6][:, c4 * 4:c4 * 4 + 3],
                                    hmsk.ap(), None, op0=OP.mult)
                v.drain()
                k.inc(v.nop(), "dv", 1)
            VE(_dvh5)

        k.barrier()

        store_ev = [0, 0]
        prev_apply = 0
        prev_y2gp = 0
        for j in range(NT):
            slot = 0
            lsem = "lxa" if j % 2 == 0 else "lxb"

            def _syx(sy, j=j, slot=slot, lsem=lsem):
                k.w(sy, "sxr", store_ev[slot])
                k.inc(sy.dma_start(
                    out=xtile[slot].ap().rearrange("p (c w) -> p c w", w=TT),
                    in_=xT.rearrange("(c p) t -> p c t", p=128)[:, :, j * TT:(j + 1) * TT]),
                    lsem, 16)
            SY(_syx)
            lx_ev = k.c[lsem]

            def xap(c, slot=slot):
                return chw(xtile[slot], c, TT)

            hb = emit_ln(xap, xbf,
                         lambda c: chw(h_bf, c, TT),
                         lambda c: modv[(0, "ca1")].ap()[:, c:c + 1],
                         lambda c: modv[(0, "sa")].ap()[:, c:c + 1],
                         TT, tmpA, lnt2a, prev_apply,
                         extra_gp=(lsem, lx_ev),
                         extra_ac=("gp", prev_y2gp))
            prev_apply = hb

            pe_base = k.c["pe"]
            dv_base = k.c["dv"]
            ac_base = k.c["ac"]
            cons_pred = []
            dvp, acp = dv_base, ac_base
            for m in range(32):
                if m < 16:
                    dvp += 1; cons_pred.append(("dv", dvp))
                else:
                    acp += 1; cons_pred.append(("ac", acp))

            def _pexz(t, hb=hb, cons_pred=cons_pred):
                k.w(t, "dv", hb)
                k.w(t, "gd", 16)
                for m in range(32):
                    if m >= 2:
                        s, vv = cons_pred[m - 2]
                        k.w(t, s, vv)
                    for kk in range(C8):
                        ins = t.matmul(ps[m % 2][:, 0:TT],
                                       chw(w_in_bf, kk, 4096)[:, m * 128:(m + 1) * 128],
                                       chw(h_bf, kk, TT),
                                       start=(kk == 0), stop=(kk == C8 - 1))
                    k.inc(ins, "pe", 1)
            TE(_pexz)

            def _dvxz(v, pe_base=pe_base):
                for m in range(16):
                    k.w(v, "pe", pe_base + m + 1)
                    v.tensor_copy(chw(xs_b, m, TT), ps[m % 2][:, 0:TT])
                    k.inc(v.drain(), "dv", 1)
            VE(_dvxz)

            def _acxz(a, pe_base=pe_base, prev_y2gp=prev_y2gp):
                k.w(a, "gp", prev_y2gp)
                for m in range(16, 32):
                    k.w(a, "pe", pe_base + m + 1)
                    ins = a.activation(chw(zs_b, m - 16, TT), ps[m % 2][:, 0:TT], AF.Silu)
                    k.inc(ins, "ac", 1)
            SC(_acxz)
            xz_ac_done = cons_pred[31][1]

            wv = convw.ap()

            def conv_emit(eng, c, accap, is_gp, scr):
                first = True
                for kk in range(4):
                    n_tail = 3 - kk
                    if n_tail > 0:
                        seg_t = tail_b.ap()[:, c * 3 + (3 - n_tail):c * 3 + 3]
                        if first:
                            eng.tensor_scalar(accap[:, 0:n_tail], seg_t,
                                              wv[:, c * 4 + kk:c * 4 + kk + 1], 0.0,
                                              op0=OP.mult, op1=OP.add)
                        elif is_gp:
                            eng.tensor_scalar(scr[:, 0:n_tail], seg_t,
                                              wv[:, c * 4 + kk:c * 4 + kk + 1], 0.0,
                                              op0=OP.mult, op1=OP.add)
                            eng.drain()
                            eng.tensor_tensor(accap[:, 0:n_tail], accap[:, 0:n_tail],
                                              scr[:, 0:n_tail], op=OP.add)
                        else:
                            eng.scalar_tensor_tensor(accap[:, 0:n_tail], seg_t,
                                                     wv[:, c * 4 + kk:c * 4 + kk + 1],
                                                     accap[:, 0:n_tail],
                                                     op0=OP.mult, op1=OP.add)
                        eng.drain()
                    seg_x = chw(xs_b, c, TT)[:, 0:TT - n_tail]
                    oseg = accap[:, n_tail:TT]
                    if first:
                        eng.tensor_scalar(oseg, seg_x, wv[:, c * 4 + kk:c * 4 + kk + 1],
                                          0.0, op0=OP.mult, op1=OP.add)
                        first = False
                    elif is_gp:
                        eng.tensor_scalar(scr[:, n_tail:TT], seg_x,
                                          wv[:, c * 4 + kk:c * 4 + kk + 1], 0.0,
                                          op0=OP.mult, op1=OP.add)
                        eng.drain()
                        eng.tensor_tensor(oseg, oseg, scr[:, n_tail:TT], op=OP.add)
                    else:
                        eng.scalar_tensor_tensor(oseg, seg_x,
                                                 wv[:, c * 4 + kk:c * 4 + kk + 1],
                                                 oseg, op0=OP.mult, op1=OP.add)
                    eng.drain()

            silu_ev = {}
            convD = [0, 2, 4, 6, 8, 10, 12, 14]
            convG = [1, 3, 5, 7, 9, 11, 13, 15]
            dorder = []
            for i in range(8):
                dorder.append(convD[i]); dorder.append(convG[i])
            for idx, c in enumerate(dorder):
                is_gp = c in convG
                sidx = (c // 2) % 2
                if is_gp:
                    def _g(g, c=c, sidx=sidx):
                        k.w(g, "dv", dv_base + c + 1)
                        prevc = c - 4
                        if prevc in silu_ev:
                            k.w(g, "ac", silu_ev[prevc])
                        acc = tmpG.ap()[:, sidx * TT:(sidx + 1) * TT]
                        conv_emit(g, c, acc, True, tmpG2.ap()[:, 0:TT])
                        g.tensor_copy(tail_b.ap()[:, c * 3:(c + 1) * 3],
                                      chw(xs_b, c, TT)[:, TT - 3:TT])
                        k.inc(g.drain(), "gp", 1)
                    GP(_g); cev = ("gp", k.c["gp"])
                else:
                    def _v(v, c=c, sidx=sidx):
                        prevc = c - 4
                        if prevc in silu_ev:
                            k.w(v, "ac", silu_ev[prevc])
                        acc = tmpD.ap()[:, sidx * TT:(sidx + 1) * TT]
                        conv_emit(v, c, acc, False, None)
                        v.tensor_copy(tail_b.ap()[:, c * 3:(c + 1) * 3],
                                      chw(xs_b, c, TT)[:, TT - 3:TT])
                        k.inc(v.drain(), "dv", 1)
                    VE(_v); cev = ("dv", k.c["dv"])

                def _a(a, c=c, sidx=sidx, cev=cev, is_gp=is_gp):
                    k.w(a, cev[0], cev[1])
                    src = (tmpG if is_gp else tmpD).ap()[:, sidx * TT:(sidx + 1) * TT]
                    ins = a.activation(chw(xs_b, c, TT), src, AF.Silu,
                                       bias=convb.ap()[:, c:c + 1])
                    k.inc(ins, "ac", 1)
                SC(_a)
                silu_ev[c] = k.c["ac"]
            xc_ready = k.c["ac"]

            if "p1" in dbg and j == 0:
                def _syd1(g, xc_ready=xc_ready):
                    k.w(g, "ac", xc_ready)
                    k.w(g, "dv", k.c["dv"])
                    for c in range(C8):
                        k.inc(g.dma_start(out=dbg["p1"][c * 128:(c + 1) * 128, 0:512],
                                          in_=chw(h_bf, c, TT)), "sto", 16)
                        k.inc(g.dma_start(out=dbg["p1"][c * 128:(c + 1) * 128, 512:1024],
                                          in_=chw(zs_b, c, TT)), "sto", 16)
                        k.inc(g.dma_start(out=dbg["p1"][c * 128:(c + 1) * 128, 1024:1536],
                                          in_=chw(xs_b, c, TT)), "sto", 16)
                GP(_syd1)
                dmp_ev = k.c["sto"]
                def _ped(t, dmp_ev=dmp_ev):
                    k.w(t, "sto", dmp_ev)
                TE(_ped)
                def _gpd(g, dmp_ev=dmp_ev):
                    k.w(g, "sto", dmp_ev)
                GP(_gpd)

            def _pedt(t, xc_ready=xc_ready):
                k.w(t, "ac", xc_ready)
                k.w(t, "gd", 64)
                for kk in range(NDI):
                    ins = t.matmul(ps[6][0:DTR, 0:TT],
                                   chw(w_x32_bf, kk, DTR),
                                   chw(xs_b, kk, TT),
                                   start=(kk == 0), stop=(kk == NDI - 1))
                k.inc(ins, "pe", 1)
            TE(_pedt); pe_dt = k.c["pe"]

            def _dvdt(v, pe_dt=pe_dt):
                k.w(v, "pe", pe_dt)
                v.tensor_copy(dt_b.ap(), ps[6][0:DTR, 0:TT])
                k.inc(v.drain(), "dv", 1)
            VE(_dvdt); dv_dt = k.c["dv"]

            pa_base = k.c["pe"]
            aa_base = k.c["ac"]
            ga_base = k.c["gp"]

            def _pea(t, dv_dt=dv_dt, aa_base=aa_base):
                k.w(t, "dv", dv_dt)
                for m in range(NDI):
                    if m >= 4:
                        k.w(t, "ac", aa_base + 4 * (m - 4) + 4)
                    bank = [0, 1, 6, 7][m % 4]
                    ins = t.matmul(ps[bank][0:128, 0:TT],
                                   w_dt_bf.ap()[:, m * 128:(m + 1) * 128],
                                   dt_b.ap(),
                                   start=True, stop=True)
                    k.inc(ins, "pe", 1)
            TE(_pea)

            def _aca(a, pa_base=pa_base, ga_base=ga_base):
                for m in range(NDI):
                    k.w(a, "pe", pa_base + m + 1)
                    if m >= 2:
                        k.w(a, "gp", ga_base + (m - 2) + 1)
                    bank = [0, 1, 6, 7][m % 4]
                    sl = m % 2
                    L = tmpL.ap()[:, sl * TT:(sl + 1) * TT]
                    R = avf.ap()[:, sl * TT:(sl + 1) * TT]
                    a.activation(L, ps[bank][0:128, 0:TT], AF.Abs, bias=bdt.ap()[:, m:m + 1])
                    a.drain()
                    a.activation(L, L, AF.Exp, scale=-1.0)
                    a.drain()
                    a.activation(L, L, AF.Ln, bias=ones_cf.ap())
                    a.drain()
                    ins = a.activation(R, ps[bank][0:128, 0:TT], AF.Relu,
                                       bias=bdt.ap()[:, m:m + 1])
                    k.inc(ins, "ac", 4)
            SC(_aca)

            def _gpa(g, aa_base=aa_base):
                for m in range(NDI):
                    k.w(g, "ac", aa_base + 4 * m + 4)
                    sl = m % 2
                    L = tmpL.ap()[:, sl * TT:(sl + 1) * TT]
                    R = avf.ap()[:, sl * TT:(sl + 1) * TT]
                    G = tGm.ap()[:, sl * TT:(sl + 1) * TT]
                    g.tensor_tensor(G, R, L, op=OP.add)
                    g.drain()
                    g.tensor_tensor(G, G, chw(xs_b, m, TT), op=OP.mult)
                    g.drain()
                    g.tensor_tensor(chw(xs_b, m, TT), G, chw(zs_b, m, TT), op=OP.mult)
                    k.inc(g.drain(), "gp", 1)
            GP(_gpa)
            y2_gp = k.c["gp"]
            prev_y2gp = y2_gp

            if "p1" in dbg and j == 0:
                def _syd2(g, y2_gp=y2_gp):
                    for c in range(C8):
                        k.inc(g.dma_start(out=dbg["p1"][c * 128:(c + 1) * 128, 1536:2048],
                                          in_=chw(xs_b, c, TT)), "sto", 16)
                GP(_syd2)
                dmp2 = k.c["sto"]
                def _ped2(t, dmp2=dmp2):
                    k.w(t, "sto", dmp2)
                TE(_ped2)

            po_base = k.c["pe"]
            do_base = k.c["dv"]

            def _peo(t, y2_gp=y2_gp, do_base=do_base):
                k.w(t, "gp", y2_gp)
                k.w(t, "gd", 32)
                for m in range(C8):
                    if m >= 2:
                        k.w(t, "dv", do_base + (m - 2) + 1)
                    for kk in range(NDI):
                        ins = t.matmul(ps[m % 2][:, 0:TT],
                                       chw(w_out_bf, kk, 1024)[:, m * 128:(m + 1) * 128],
                                       chw(xs_b, kk, TT),
                                       start=(kk == 0), stop=(kk == NDI - 1))
                    k.inc(ins, "pe", 1)
            TE(_peo)

            def _dvo(v, po_base=po_base, slot=slot):
                for m in range(C8):
                    k.w(v, "pe", po_base + m + 1)
                    v.tensor_tensor(tmpA.ap()[:, 0:TT], ps[m % 2][:, 0:TT],
                                    chw(h_bf, m, TT), op=OP.add)
                    v.drain()
                    v.scalar_tensor_tensor(chw(xtile[slot], m, TT), tmpA.ap()[:, 0:TT],
                                           modv[(0, "ga")].ap()[:, m:m + 1],
                                           chw(xtile[slot], m, TT),
                                           op0=OP.mult, op1=OP.add)
                    k.inc(v.drain(), "dv", 1)
            VE(_dvo)
            dv_upd = k.c["dv"]

            def _syst(sy, j=j, slot=slot, dv_upd=dv_upd):
                k.w(sy, "dv", dv_upd)
                k.inc(sy.dma_start(
                    out=xres.rearrange("(c p) t -> p c t", p=128)[:, :, j * TT:(j + 1) * TT],
                    in_=xtile[slot].ap().rearrange("p (c w) -> p c w", w=TT)), "sxr", 16)
            SY(_syst)
            store_ev[slot] = k.c["sxr"]

        AR.reset(P_BASE)
        k.barrier()
        emit_tap("l0attn")

    ph23_mark = AR.mark()
    h2_b = AR.t("h2_b", [128, C8 * T], BF16)
    comb_b = AR.t("comb_b", [16, T], BF16)
    y_t = AR.t("y_t", [128, C8 * T], F32)
    ph2_mark = AR.mark()

    if n_phases >= 2:
        xtile = [AR.t(f"x2tl{i}", [128, C8 * TT], F32) for i in range(2)]
        h2f = AR.t("h2f", [128, C8 * TT], F32)
        xbf2 = AR.t("xbf2", [128, C8 * TT], BF16)
        biasedf = AR.at(AR.mark() - C8 * TT * 2, "biasedf", [16, T], F32)
        tmpA2 = AR.t("tmpA2", [128, TT], F32)
        lnt2b = AR.t("lnt2b", [128, TT], F32)
        logits = AR.t("logits", [16, T], F32)
        btm = AR.t("btm", [128, 16 * 16], F32)
        stm = AR.t("stm", [128, 16 * 16], F32)
        selm = AR.t("selm", [128, 16], F32)
        mskb = AR.t("mskb", [128, 16], F32)
        wbuf = AR.t("wbuf", [128, 16], F32)
        mrow = AR.t("mrow", [128, 1], F32)
        wsum = AR.t("wsum", [128, 1], F32)
        assert AR.mark() <= 212480

        prev_apply = 0
        for j in range(NT):
            slot = j % 2
            lsem = "lxa" if slot == 0 else "lxb"

            def _syx(sy, j=j, slot=slot, lsem=lsem):
                k.w(sy, "pe", k.c["pe"])
                k.inc(sy.dma_start(
                    out=xtile[slot].ap().rearrange("p (c w) -> p c w", w=TT),
                    in_=xres.rearrange("(c p) t -> p c t", p=128)[:, :, j * TT:(j + 1) * TT]),
                    lsem, 16)
            SY(_syx)
            lx_ev = k.c[lsem]

            def xap(c, slot=slot):
                return chw(xtile[slot], c, TT)

            hb = emit_ln(xap, xbf2,
                         lambda c: chw(h2f, c, TT),
                         lambda c: modv[(0, "cb1")].ap()[:, c:c + 1],
                         lambda c: modv[(0, "sb")].ap()[:, c:c + 1],
                         TT, tmpA2, lnt2b, prev_apply,
                         extra_gp=(lsem, lx_ev))
            prev_apply = hb

            def _gph(g, j=j, hb=hb):
                k.w(g, "dv", hb)
                for c in range(C8):
                    ins = g.tensor_copy(
                        h2_b.ap()[:, c * T + j * TT:c * T + (j + 1) * TT],
                        chw(h2f, c, TT))
                k.inc(ins, "gp", 1)
            GP(_gph)

            def _peg(t, hb=hb):
                k.w(t, "dv", hb)
                for kk in range(C8):
                    ins = t.matmul(ps[6][0:16, 0:TT], gatewt.ap()[:, kk * 16:(kk + 1) * 16],
                                   chw(h2f, kk, TT), start=(kk == 0), stop=(kk == C8 - 1))
                k.inc(ins, "pe", 1)
            TE(_peg); pe_g = k.c["pe"]

            def _dvg(v, j=j, pe_g=pe_g):
                k.w(v, "pe", pe_g)
                v.tensor_copy(logits.ap()[:, j * TT:(j + 1) * TT], ps[6][0:16, 0:TT])
                k.inc(v.drain(), "dv", 1)
            VE(_dvg)
        dv_logit = k.c["dv"]

        def _acsg(a):
            k.w(a, "dv", dv_logit)
            ins = a.activation(logits.ap(), logits.ap(), AF.Sigmoid)
            k.inc(ins, "ac", 1)
        SC(_acsg); ac_sg = k.c["ac"]

        def _dvbi(v):
            k.w(v, "ac", ac_sg)
            v.tensor_scalar(biasedf.ap(), logits.ap(), gateb.ap(), None, op0=OP.add)
            k.inc(v.drain(), "dv", 1)
        VE(_dvbi); dv_bi = k.c["dv"]

        for g16 in range(16):
            def _peT(t, g16=g16):
                k.w(t, "dv", dv_bi if g16 == 0 else k.c["dv"])
                t.transpose(ps[6][:, 0:16],
                            logits.ap()[0:16, g16 * 128:(g16 + 1) * 128],
                            identf.ap()[0:16, 0:16])
                ins = t.transpose(ps[7][:, 0:16],
                                  biasedf.ap()[0:16, g16 * 128:(g16 + 1) * 128],
                                  identf.ap()[0:16, 0:16])
                k.inc(ins, "pe", 1)
            TE(_peT); peT = k.c["pe"]

            def _dvt(v, g16=g16, peT=peT):
                k.w(v, "pe", peT)
                stmg = chw(stm, g16, 16)
                btmg = chw(btm, g16, 16)
                v.tensor_copy(stmg, ps[6][:, 0:16])
                v.tensor_copy(btmg, ps[7][:, 0:16])
                v.drain()
                for it in range(4):
                    v.tensor_reduce(mrow.ap(), btmg, axis=mybir.AxisListType.X, op=OP.max)
                    v.drain()
                    dst = selm.ap() if it == 0 else mskb.ap()
                    v.tensor_scalar(dst, btmg, mrow.ap(), None, op0=OP.is_ge)
                    v.drain()
                    if it > 0:
                        v.tensor_tensor(selm.ap(), selm.ap(), mskb.ap(), op=OP.add)
                    if it < 3:
                        msk = selm.ap() if it == 0 else mskb.ap()
                        v.scalar_tensor_tensor(btmg, msk, -1e30, btmg,
                                               op0=OP.mult, op1=OP.add)
                    v.drain()
                v.tensor_tensor(wbuf.ap(), stmg, selm.ap(), op=OP.mult)
                v.drain()
                v.tensor_reduce(wsum.ap(), wbuf.ap(), axis=mybir.AxisListType.X, op=OP.add)
                v.drain()
                v.reciprocal(wsum.ap(), wsum.ap())
                v.drain()
                v.tensor_scalar(wbuf.ap(), wbuf.ap(), wsum.ap(), RSCALE,
                                op0=OP.mult, op1=OP.mult)
                k.inc(v.drain(), "dv", 1)
            VE(_dvt); dv_c = k.c["dv"]

            def _peT2(t, g16=g16, dv_c=dv_c):
                k.w(t, "dv", dv_c)
                ins = t.transpose(ps[6][0:16, 0:128], wbuf.ap(), identf.ap())
                k.inc(ins, "pe", 1)
            TE(_peT2); peT2 = k.c["pe"]

            def _dvc2(v, g16=g16, peT2=peT2):
                k.w(v, "pe", peT2)
                v.tensor_copy(comb_b.ap()[0:16, g16 * 128:(g16 + 1) * 128],
                              ps[6][0:16, 0:128])
                k.inc(v.drain(), "dv", 1)
            VE(_dvc2)

        AR.reset(ph2_mark)
        k.barrier()

    if n_phases >= 3:
        ph3_mark = AR.mark()
        w13sl = [AR.t(f"w13sl{i}", [128, C8 * 1024], BF16) for i in range(2)]
        w2sl = [AR.t(f"w2sl{i}", [128, 4 * 1024], BF16) for i in range(2)]
        he_b = AR.t("he_b", [128, 2 * 4 * 512], BF16)
        sgf = AR.t("sgf", [128, 2 * 512], F32)
        uuf = AR.t("uuf", [128, 2 * 512], F32)
        combsb = AR.t("combsb", [128, 512], F32)
        tGm3 = AR.t("tGm3", [128, 2 * 512], F32)
        _mw = AR.t("mwsl0", [128, 1024], F32)
        mwsl = [_mw, _mw]
        xtile = [AR.at(ph3_mark + i * C8 * TT * 4, f"x3tl{i}", [128, C8 * TT], F32)
                 for i in range(2)]
        assert AR.mark() <= 212480

        def _gpy(g):
            ins = g.memset(y_t.ap(), 0.0)
            k.inc(ins, "gp", 1)
        GP(_gpy)
        gp_y0 = k.c["gp"]

        mean_jobs = ([("w1", kk, q) for kk in range(C8) for q in range(4)]
                     + [("w2", kk, 0) for kk in range(32)])
        mean_store_ev = [0, 0]
        mean_idx = [0]

        def emit_mean_jobs(nj):
            for _ in range(nj):
                if mean_idx[0] >= len(mean_jobs):
                    return
                kind, kk, q = mean_jobs[mean_idx[0]]
                msl = mean_idx[0] % 2
                tsem = "twa" if msl == 0 else "twb"
                W = 1024
                src = (w1_t if kind == "w1" else w2l_t)

                def _sym(sy, kind=kind, kk=kk, q=q, msl=msl, tsem=tsem, W=W, src=src):
                    k.w(sy, "dv", max(mean_store_ev))
                    k.inc(sy.dma_start(out=mwsl[msl].ap()[:, 0:W],
                                       in_=src[kk * 128:(kk + 1) * 128,
                                               q * 1024:(q + 1) * 1024]), tsem, 16)
                SY(_sym)
                tev = k.c[tsem]

                def _dvm(v, kind=kind, msl=msl, tsem=tsem, W=W, tev=tev):
                    k.w(v, tsem, tev)
                    v.tensor_reduce(wred.ap(), mwsl[msl].ap()[:, 0:W],
                                    axis=mybir.AxisListType.X, op=OP.add,
                                    apply_absolute_value=True)
                    v.drain()
                    col = 0 if kind == "w1" else 1
                    v.tensor_tensor(wmean.ap()[:, col:col + 1], wmean.ap()[:, col:col + 1],
                                    wred.ap(), op=OP.add)
                    k.inc(v.drain(), "dv", 1)
                VE(_dvm)
                mean_store_ev[msl] = k.c["dv"]
                mean_idx[0] += 1

        pe_done = {}
        comb_ac_ev = 0
        combsb_gp = 0
        he_w2_pe = [0, 0]
        for e in range(NEXP + 1):
            shared = (e == NEXP)
            sl = e % 2
            gsem = "gwa" if sl == 0 else "gwb"

            def _gpw(g, e=e, sl=sl, gsem=gsem, shared=shared):
                if e >= 2:
                    k.w(g, "pe", pe_done[e - 2])
                if e == 0:
                    k.w(g, "gp", gp_y0)
                src13 = w13s_t if shared else w13_t[e]
                src2 = w2s_t if shared else w2_t[e]
                k.inc(g.dma_start(
                    out=w13sl[sl].ap().rearrange("p (c f) -> p c f", f=1024),
                    in_=src13.rearrange("(c p) f -> p c f", p=128)), gsem, 16)
                k.inc(g.dma_start(
                    out=w2sl[sl].ap().rearrange("p (c f) -> p c f", f=1024),
                    in_=src2.rearrange("(c p) f -> p c f", p=128)), gsem, 16)
            GP(_gpw)
            gw_ev = k.c[gsem]

            stage = []
            for ct in range(NT + 1):
                if ct < NT:
                    hsl = ct % 2
                    if not shared:
                        def _pec(t, e=e, ct=ct):
                            k.w(t, "ac", comb_ac_ev)
                            k.w(t, "gd", 144)
                            ins = t.matmul(ps[6][:, 0:512],
                                           eall.ap()[:, e * 128:(e + 1) * 128],
                                           comb_b.ap()[0:16, ct * 512:(ct + 1) * 512],
                                           start=True, stop=True)
                            k.inc(ins, "pe", 1)
                        TE(_pec); pe_c = k.c["pe"]

                        def _acc(a, pe_c=pe_c, combsb_gp=combsb_gp):
                            k.w(a, "pe", pe_c)
                            k.w(a, "gp", combsb_gp)
                            ins = a.copy(combsb.ap(), ps[6][:, 0:512])
                            k.inc(ins, "ac", 1)
                        SC(_acc)
                        comb_ac_ev = k.c["ac"]

                    ac13_base = k.c["ac"]
                    gp13_base = k.c["gp"]

                    def _pe13(t, e=e, ct=ct, sl=sl, gw_ev=gw_ev, gsem=gsem,
                              ac13_base=ac13_base):
                        if ct == 0:
                            k.w(t, gsem, gw_ev)
                        for c in range(4):
                            if c >= 2:
                                k.w(t, "ac", ac13_base + 2 * (c - 2) + 2)
                            b0 = [0, 2][c % 2]; b1 = [1, 3][c % 2]
                            for kk in range(C8):
                                t.matmul(ps[b0][:, 0:512],
                                         chw(w13sl[sl], kk, 1024)[:, c * 128:(c + 1) * 128],
                                         h2_b.ap()[:, kk * T + ct * 512:kk * T + (ct + 1) * 512],
                                         start=(kk == 0), stop=(kk == C8 - 1))
                            for kk in range(C8):
                                ins = t.matmul(ps[b1][:, 0:512],
                                               chw(w13sl[sl], kk, 1024)[:, 512 + c * 128:512 + (c + 1) * 128],
                                               h2_b.ap()[:, kk * T + ct * 512:kk * T + (ct + 1) * 512],
                                               start=(kk == 0), stop=(kk == C8 - 1))
                            k.inc(ins, "pe", 1)
                    TE(_pe13)
                    pe13_base = k.c["pe"] - 4

                    def _ac13(a, gp13_base=gp13_base, pe13_base=pe13_base):
                        for c in range(4):
                            k.w(a, "pe", pe13_base + c + 1)
                            if c >= 2:
                                k.w(a, "gp", gp13_base + (c - 2) + 1)
                            b0 = [0, 2][c % 2]; b1 = [1, 3][c % 2]
                            csl = c % 2
                            a.activation(sgf.ap()[:, csl * 512:(csl + 1) * 512],
                                         ps[b0][:, 0:512], AF.Silu)
                            a.drain()
                            ins = a.copy(uuf.ap()[:, csl * 512:(csl + 1) * 512],
                                         ps[b1][:, 0:512])
                            k.inc(ins, "ac", 2)
                    SC(_ac13)

                    def _gp13(g, shared=shared, hsl=hsl, ac13_base=ac13_base,
                              he_free=he_w2_pe[hsl]):
                        for c in range(4):
                            k.w(g, "ac", ac13_base + 2 * c + 2)
                            if c == 0:
                                k.w(g, "pe", he_free)
                            csl = c % 2
                            G = tGm3.ap()[:, csl * 512:(csl + 1) * 512]
                            g.tensor_tensor(G, sgf.ap()[:, csl * 512:(csl + 1) * 512],
                                            uuf.ap()[:, csl * 512:(csl + 1) * 512], op=OP.mult)
                            g.drain()
                            dst = he_b.ap()[:, hsl * 2048 + c * 512:hsl * 2048 + (c + 1) * 512]
                            if shared:
                                g.tensor_copy(dst, G)
                            else:
                                g.tensor_tensor(dst, G, combsb.ap(), op=OP.mult)
                            k.inc(g.drain(), "gp", 1)
                    GP(_gp13)
                    combsb_gp = k.c["gp"]
                    stage.append((ct, k.c["gp"]))

                if ct >= 1:
                    pct, pgp = stage[ct - 1]
                    phsl = pct % 2
                    dvy_base = k.c["dv"]

                    def _pe2(t, sl=sl, pgp=pgp, phsl=phsl, dvy_base=dvy_base):
                        k.w(t, "gp", pgp)
                        k.w(t, "dv", dvy_base)
                        for m in range(C8):
                            if m >= 2:
                                k.w(t, "dv", dvy_base + (m - 2) + 1)
                            for kk in range(4):
                                ins = t.matmul(ps[4 + m % 2][:, 0:512],
                                               chw(w2sl[sl], kk, 1024)[:, m * 128:(m + 1) * 128],
                                               he_b.ap()[:, phsl * 2048 + kk * 512:phsl * 2048 + (kk + 1) * 512],
                                               start=(kk == 0), stop=(kk == 3))
                            k.inc(ins, "pe", 1)
                    TE(_pe2)
                    pe2_base = k.c["pe"] - 8
                    he_w2_pe[phsl] = k.c["pe"]

                    def _dvy(v, pct=pct, pe2_base=pe2_base):
                        for m in range(C8):
                            k.w(v, "pe", pe2_base + m + 1)
                            yap = y_t.ap()[:, m * T + pct * 512:m * T + (pct + 1) * 512]
                            v.tensor_tensor(yap, yap, ps[4 + m % 2][:, 0:512], op=OP.add)
                            k.inc(v.drain(), "dv", 1)
                    VE(_dvy)

            pe_done[e] = k.c["pe"]
            emit_mean_jobs(4)
        emit_mean_jobs(99)

        dv_mean = k.c["dv"]

        def _pesw(t):
            k.w(t, "dv", dv_mean)
            ins = t.matmul(ps[6][0:1, 0:2], ones_cf.ap(), wmean.ap(), start=True, stop=True)
            k.inc(ins, "pe", 1)
        TE(_pesw); pe_sw = k.c["pe"]

        def _dvsw(v, pe_sw=pe_sw):
            k.w(v, "pe", pe_sw)
            r4 = rw4.ap()
            v.tensor_scalar(r4[:, 2:4], ps[6][0:1, 0:2], 1.0 / W1ELEMS, EPS,
                            op0=OP.mult, op1=OP.max)
            v.drain()
            v.reciprocal(r4[:, 0:2], r4[:, 2:4])
            v.drain()
            v.tensor_copy(rw4h.ap(), r4)
            v.drain()
            v.tensor_tensor(rw4l.ap(), r4, rw4h.ap(), op=OP.subtract)
            v.drain()
            v.tensor_copy(rw4lb.ap(), rw4l.ap())
            k.inc(v.drain(), "dv", 1)
        VE(_dvsw); dv_sw = k.c["dv"]

        def _pesw2(t, dv_sw=dv_sw):
            k.w(t, "dv", dv_sw)
            t.matmul(ps[6][:, 0:4], ones_rb.ap(), rw4h.ap(), start=True, stop=False)
            ins = t.matmul(ps[6][:, 0:4], ones_rb.ap(), rw4lb.ap(), start=False, stop=True)
            k.inc(ins, "pe", 1)
        TE(_pesw2); pe_sw2 = k.c["pe"]

        def _dvsw2(v, pe_sw2=pe_sw2):
            k.w(v, "pe", pe_sw2)
            v.tensor_copy(swrep.ap(), ps[6][:, 0:2])
            v.tensor_copy(cmrep.ap(), ps[6][:, 2:4])
            k.inc(v.drain(), "dv", 1)
        VE(_dvsw2)

        store_ev = [0, 0]
        for j in range(NT):
            slot = j % 2
            lsem = "lxa" if slot == 0 else "lxb"

            def _syx(sy, j=j, slot=slot, lsem=lsem):
                k.w(sy, "sxr", store_ev[slot])
                k.w(sy, "pe", pe_done[NEXP])
                k.w(sy, "gp", k.c["gp"])
                k.inc(sy.dma_start(
                    out=xtile[slot].ap().rearrange("p (c w) -> p c w", w=TT),
                    in_=xres.rearrange("(c p) t -> p c t", p=128)[:, :, j * TT:(j + 1) * TT]),
                    lsem, 16)
            SY(_syx)
            lx_ev = k.c[lsem]

            def _dvf(v, j=j, slot=slot, lsem=lsem, lx_ev=lx_ev):
                k.w(v, lsem, lx_ev)
                for c in range(C8):
                    ins = v.scalar_tensor_tensor(
                        chw(xtile[slot], c, TT),
                        y_t.ap()[:, c * T + j * TT:c * T + (j + 1) * TT],
                        modv[(0, "gb")].ap()[:, c:c + 1],
                        chw(xtile[slot], c, TT),
                        op0=OP.mult, op1=OP.add)
                k.inc(ins, "dv", 1)
            VE(_dvf); dv_f = k.c["dv"]

            def _syst(sy, j=j, slot=slot, dv_f=dv_f):
                k.w(sy, "dv", dv_f)
                k.inc(sy.dma_start(
                    out=xres.rearrange("(c p) t -> p c t", p=128)[:, :, j * TT:(j + 1) * TT],
                    in_=xtile[slot].ap().rearrange("p (c w) -> p c w", w=TT)), "sxr", 16)
            SY(_syst)
            store_ev[slot] = k.c["sxr"]

        AR.reset(ph23_mark)
        k.barrier()
        emit_tap("l0")

    if n_phases >= 4:
        ph4_mark = AR.mark()
        cosq = AR.t("cosq", [128, T], BF16)
        sinq = AR.t("sinq", [128, T], BF16)
        xtile = [AR.t(f"x4tl{i}", [128, C8 * TT], F32) for i in range(2)]
        ph4a = AR.mark()
        h1_b = AR.t("h1_b", [128, C8 * T], BF16)
        xbf4 = AR.t("xbf4", [128, C8 * TT], BF16)
        tmpA4 = AR.t("tmpA4", [128, TT], F32)
        lnt2c = AR.t("lnt2c", [128, TT], F32)
        thk = AR.t("thk", [128, T], F32)
        thq = AR.t("thq", [128, T], F32)
        u_f4 = AR.t("u_f4", [128, T], F32)
        n_f4 = AR.t("n_f4", [128, T], F32)
        ksh_b = AR.t("ksh_b", [128, 2 * T], BF16)
        vtm_b = AR.t("vtm_b", [128, T], BF16)
        vf_f = AR.t("vf_f", [128, TT], F32)
        vf_b = AR.t("vf_b", [128, TT], BF16)
        assert AR.mark() <= 212480

        prev_apply = 0
        for j in range(NT):
            slot = j % 2
            lsem = "lxa" if slot == 0 else "lxb"

            def _syx(sy, j=j, slot=slot, lsem=lsem):
                k.w(sy, "dv", prev_apply)
                k.inc(sy.dma_start(
                    out=xtile[slot].ap().rearrange("p (c w) -> p c w", w=TT),
                    in_=xres.rearrange("(c p) t -> p c t", p=128)[:, :, j * TT:(j + 1) * TT]),
                    lsem, 16)
            SY(_syx)
            lx_ev = k.c[lsem]

            def xap(c, slot=slot):
                return chw(xtile[slot], c, TT)

            def h1out(c, j=j):
                return h1_b.ap()[:, c * T + j * TT:c * T + (j + 1) * TT]

            prev_apply = emit_ln(xap, xbf4, h1out,
                                 lambda c: modv[(1, "ca1")].ap()[:, c:c + 1],
                                 lambda c: modv[(1, "sa")].ap()[:, c:c + 1],
                                 TT, tmpA4, lnt2c, prev_apply,
                                 extra_gp=(lsem, lx_ev))
        h1_done = prev_apply

        def qkvw(i):
            return lambda c: wqkv_bf.ap()[:, (c * 3 + i) * HD:(c * 3 + i) * HD + HD]

        for ct in range(NT):
            def _pek(t, ct=ct):
                k.w(t, "dv", h1_done if ct == 0 else k.c["dv"])
                k.w(t, "gd", 112)
                for kk in range(C8):
                    ins = t.matmul(ps[0][:, 0:TT], qkvw(1)(kk),
                                   h1_b.ap()[:, kk * T + ct * TT:kk * T + (ct + 1) * TT],
                                   start=(kk == 0), stop=(kk == C8 - 1))
                k.inc(ins, "pe", 1)
                for kk in range(C8):
                    ins = t.matmul(ps[2][:, 0:TT], qkvw(2)(kk),
                                   h1_b.ap()[:, kk * T + ct * TT:kk * T + (ct + 1) * TT],
                                   start=(kk == 0), stop=(kk == C8 - 1))
                k.inc(ins, "pe", 1)
                for kk in range(C8):
                    ins = t.matmul(ps[1][:, 0:TT], qkvw(0)(kk),
                                   h1_b.ap()[:, kk * T + ct * TT:kk * T + (ct + 1) * TT],
                                   start=(kk == 0), stop=(kk == C8 - 1))
                k.inc(ins, "pe", 1)
            TE(_pek); pe_k = k.c["pe"]

            def _dvk(v, ct=ct, pe_k=pe_k):
                k.w(v, "pe", pe_k - 2)
                thw = thk.ap()[:, ct * TT:(ct + 1) * TT]
                v.tensor_scalar(thw, ps[0][:, 0:TT], bk_s.ap(), None, op0=OP.add)
                v.drain()
                v.tensor_scalar(thw, thw, invw.ap(), phb.ap(), op0=OP.mult, op1=OP.add)
                v.drain()
                k.w(v, "pe", pe_k - 1)
                v.tensor_scalar(vf_f.ap(), ps[2][:, 0:TT], bv_s.ap(), None, op0=OP.add)
                v.drain()
                v.tensor_copy(vf_b.ap(), vf_f.ap())
                v.drain()
                k.w(v, "pe", pe_k)
                thw = thq.ap()[:, ct * TT:(ct + 1) * TT]
                v.tensor_scalar(thw, ps[1][:, 0:TT], bq_s.ap(), None, op0=OP.add)
                v.drain()
                v.tensor_scalar(thw, thw, invw.ap(), phb.ap(), op0=OP.mult, op1=OP.add)
                k.inc(v.drain(), "dv", 1)
            VE(_dvk); dv_v = k.c["dv"]

            pevt_base = k.c["pe"]
            dvvt_base = k.c["dv"]

            def _pevt(t, ct=ct, dv_v=dv_v, dvvt_base=dvvt_base):
                k.w(t, "dv", dv_v)
                for b4 in range(4):
                    if b4 >= 2:
                        k.w(t, "dv", dvvt_base + (b4 - 2) + 1)
                    ins = t.transpose(ps[6 + b4 % 2].bitcast(BF16)[:, 0:128],
                                      vf_b.ap()[:, b4 * 128:(b4 + 1) * 128], identb.ap())
                    k.inc(ins, "pe", 1)
            TE(_pevt)

            def _dvvt(v, ct=ct, pevt_base=pevt_base):
                for b4 in range(4):
                    k.w(v, "pe", pevt_base + b4 + 1)
                    v.tensor_copy(
                        vtm_b.ap()[:, ct * TT + b4 * 128:ct * TT + (b4 + 1) * 128],
                        ps[6 + b4 % 2].bitcast(BF16)[:, 0:128])
                    k.inc(v.drain(), "dv", 1)
            VE(_dvvt)
        thq_dv = k.c["dv"]

        ksc_ac = emit_sincos_multi(
            [(thk.ap(), ksh_b.ap()[:, 0:T], ksh_b.ap()[:, T:2 * T], u_f4.ap(), n_f4.ap())],
            T, pre_dv=thq_dv)
        kv_dv = k.c["dv"]
        kv_ac = ksc_ac

        kv_dv = k.c["dv"]
        kv_ac = k.c["ac"]

        if p4cut < 2:
            def _syskip(sy):
                k.inc(sy.dma_start(out=yout, in_=xres), "sto", 16)
            SY(_syskip)
            k.barrier()
            _es.close()
            return nc

        def _sykv(sy):
            k.w(sy, "dv", kv_dv)
            k.w(sy, "ac", kv_ac)
            k.inc(sy.dma_start(out=kvin[0:128, :], in_=ksh_b.ap()[:, 0:T]), "sxr", 16)
            k.inc(sy.dma_start(out=kvin[128:256, :], in_=ksh_b.ap()[:, T:2 * T]), "sxr", 16)
            k.inc(sy.dma_start(out=kvin[256:384, :], in_=vtm_b.ap()), "sxr", 16)
        SY(_sykv)
        sxr_kv = k.c["sxr"]

        def _gpkv(g):
            k.w(g, "sxr", sxr_kv)
            k.inc(g.collective_compute(
                "AllGather", OP.bypass, replica_groups=[list(range(NCOR))],
                ins=[kvin.ap().opt()], outs=[kvout.ap().opt()]), "cc", 1)
        GP(_gpkv)
        cc_kv = k.c["cc"]

        emit_sincos_multi(
            [(thq.ap(), cosq.ap(), sinq.ap(), u_f4.ap(), n_f4.ap())], T, pre_dv=thq_dv)
        q_dv = k.c["dv"]
        q_ac = k.c["ac"]

        if p4cut < 3:
            def _syskip2(sy):
                k.w(sy, "cc", cc_kv)
                k.w(sy, "dv", q_dv)
                k.w(sy, "ac", q_ac)
                k.inc(sy.dma_start(out=yout, in_=xres), "sto", 16)
            SY(_syskip2)
            k.barrier()
            _es.close()
            return nc

        AR.reset(ph4a)
        ckv = AR.t("ckv", [128, 16384], BF16)
        skv = AR.t("skv", [128, 16384], BF16)
        vfull = AR.t("vfull", [128, 129 * 128], BF16)
        pb_b = AR.t("pb_b", [128, 2 * 1024], BF16)
        ofm = AR.t("ofm", [128, T], BF16)
        otm = AR.t("otm", [128, 128], BF16)
        rl4 = AR.t("rl4", [128, 1], F32)
        twsl = AR.t("twsl", [128, 4096], F32)
        tq_b = AR.t("tq_b", [128, 4096], BF16)
        tmpWo = AR.t("tmpWo", [128, TT], F32)
        assert AR.mark() <= 212480

        def _sykv2(sy):
            k.w(sy, "cc", cc_kv)
            k.w(sy, "dv", q_dv)
            k.w(sy, "ac", q_ac)
            k.w(sy, "gp", k.c["gp"])
            kvr = kvout.ap().rearrange("(r s p) t -> s p r t", s=3, p=128)
            k.inc(sy.dma_start(
                out=ckv.ap().rearrange("p (r t) -> p r t", t=T), in_=kvr[0]), "ld", 16)
            k.inc(sy.dma_start(
                out=skv.ap().rearrange("p (r t) -> p r t", t=T), in_=kvr[1]), "ld", 16)
            vtv = vfull.ap().rearrange("p (kt w) -> p kt w", w=129)
            for r8 in range(8):
                k.inc(sy.dma_start(
                    out=vtv[:, r8 * 16:(r8 + 1) * 16, 0:128],
                    in_=kvr[2][:, r8].rearrange("p (t w) -> p t w", w=128)), "ld", 16)
        SY(_sykv2)
        ld_kv = k.c["ld"]

        def _dvones(v):
            k.w(v, "ld", ld_kv)
            ins = v.memset(vfull.ap().rearrange("p (kt w) -> p kt w", w=129)[:, :, 128:129], 1.0)
            k.inc(ins, "dv", 1)
        VE(_dvones)
        vones_dv = k.c["dv"]

        tern_jobs = [("w1", kk) for kk in range(C8)] + [("w2", kk) for kk in range(32)]
        tern_idx = [0]
        tern_state = {"dv": 0, "st": 0}

        def emit_tern_jobs(nj):
            for _ in range(nj):
                if tern_idx[0] >= len(tern_jobs):
                    return
                kind, kk = tern_jobs[tern_idx[0]]
                tsem = "twa" if tern_idx[0] % 2 == 0 else "twb"
                W = 4096 if kind == "w1" else 1024
                src = w1_t if kind == "w1" else w2l_t
                dst = wq1d if kind == "w1" else wq2d
                scol = 0 if kind == "w1" else 1

                def _syt(sy, kind=kind, kk=kk, tsem=tsem, W=W, src=src):
                    k.w(sy, "dv", tern_state["dv"])
                    k.inc(sy.dma_start(out=twsl.ap()[:, 0:W],
                                       in_=src[kk * 128:(kk + 1) * 128, :]), tsem, 16)
                SY(_syt)
                tev = k.c[tsem]

                def _dvt(v, tsem=tsem, W=W, scol=scol, tev=tev, stp=tern_state["st"]):
                    k.w(v, tsem, tev)
                    k.w(v, "sxr", stp)
                    tw = twsl.ap()[:, 0:W]
                    v.tensor_scalar(tw, tw, swrep.ap()[:, scol:scol + 1],
                                    MAGIC, op0=OP.mult, op1=OP.add)
                    v.drain()
                    v.tensor_scalar(tw, tw, MAGIC + 1.0, MAGIC - 1.0, op0=OP.min, op1=OP.max)
                    v.drain()
                    v.tensor_scalar(tq_b.ap()[:, 0:W], tw,
                                    -MAGIC, 1.0, op0=OP.add, op1=OP.mult)
                    k.inc(v.drain(), "dv", 1)
                VE(_dvt)
                tern_state["dv"] = k.c["dv"]
                dv_ev = k.c["dv"]

                def _syt2(sy, kind=kind, kk=kk, W=W, dst=dst, dv_ev=dv_ev):
                    k.w(sy, "dv", dv_ev)
                    k.inc(sy.dma_start(out=dst[kk * 128:(kk + 1) * 128, :],
                                       in_=tq_b.ap()[:, 0:W]), "sxr", 16)
                SY(_syt2)
                tern_state["st"] = k.c["sxr"]
                tern_idx[0] += 1

        if p4cut < 4:
            def _syskip3(sy):
                k.w(sy, "ld", ld_kv)
                k.w(sy, "dv", k.c["dv"])
                k.inc(sy.dma_start(out=yout, in_=xres), "sto", 16)
            SY(_syskip3)
            k.barrier()
            _es.close()
            return nc

        NKT = 128
        o_evac_dv = 0
        for qb in range(NT):
            pe_base = k.c["pe"]
            ac_base = k.c["ac"]
            def _peat(t, qb=qb, pe_base=pe_base, ac_base=ac_base, o_evac_dv=o_evac_dv):
                if qb == 0:
                    k.w(t, "ld", ld_kv)
                    k.w(t, "dv", max(q_dv, vones_dv))
                    k.w(t, "ac", q_ac)
                else:
                    k.w(t, "dv", o_evac_dv)
                NKP = NKT // 2
                for kp in range(NKP):
                    pr = kp % 2
                    if kp >= 2:
                        k.w(t, "ac", ac_base + (kp - 2) + 1)
                    for half in (0, 1):
                        kt = 2 * kp + half
                        t.matmul(ps[[0, 6][pr] + half][:, 0:TT],
                                 ckv.ap()[:, kt * 128:(kt + 1) * 128],
                                 cosq.ap()[:, qb * TT:(qb + 1) * TT],
                                 start=True, stop=False)
                        ins = t.matmul(ps[[0, 6][pr] + half][:, 0:TT],
                                       skv.ap()[:, kt * 128:(kt + 1) * 128],
                                       sinq.ap()[:, qb * TT:(qb + 1) * TT],
                                       start=False, stop=True)
                    k.inc(ins, "pe", 1)
                    if kp >= 1:
                        kpp = kp - 1
                        k.w(t, "ac", ac_base + kpp + 1)
                        for half in (0, 1):
                            ktp = 2 * kpp + half
                            for qs in range(4):
                                ins = t.matmul(
                                    ps[2 + qs][:, 0:129],
                                    pb_b.ap()[:, (kpp % 2) * 1024 + half * TT + qs * 128:
                                              (kpp % 2) * 1024 + half * TT + (qs + 1) * 128],
                                    vfull.ap()[:, ktp * 129:(ktp + 1) * 129],
                                    start=(ktp == 0), stop=False)
                        k.inc(ins, "pe", 1)
                kpp = NKP - 1
                k.w(t, "ac", ac_base + kpp + 1)
                for half in (0, 1):
                    ktp = 2 * kpp + half
                    for qs in range(4):
                        ins = t.matmul(
                            ps[2 + qs][:, 0:129],
                            pb_b.ap()[:, (kpp % 2) * 1024 + half * TT + qs * 128:
                                      (kpp % 2) * 1024 + half * TT + (qs + 1) * 128],
                            vfull.ap()[:, ktp * 129:(ktp + 1) * 129],
                            start=False, stop=(ktp == NKT - 1))
                k.inc(ins, "pe", 1)
            TE(_peat)

            def s_ev(kp, pe_base=pe_base):
                return pe_base + 2 * kp + 1

            def _acat(a, qb=qb, pe_base=pe_base, ac_base=ac_base):
                for kp in range(NKT // 2):
                    k.w(a, "pe", s_ev(kp))
                    ins = a.activation(
                        pb_b.ap()[:, (kp % 2) * 1024:(kp % 2 + 1) * 1024],
                        pspair[kp % 2][:, 0:1024], AF.Exp, scale=ASCALE)
                    k.inc(ins, "ac", 1)
            SC(_acat)
            pe_attn_done = k.c["pe"]

            for qs in range(4):
                def _dvo(v, qb=qb, qs=qs, pe_attn_done=pe_attn_done):
                    k.w(v, "pe", pe_attn_done)
                    v.reciprocal(rl4.ap(), ps[2 + qs][:, 128:129])
                    v.drain()
                    v.tensor_scalar(otm.ap(), ps[2 + qs][:, 0:128], rl4.ap(), None,
                                    op0=OP.mult)
                    k.inc(v.drain(), "dv", 1)
                VE(_dvo); dv_o = k.c["dv"]

                def _peot(t, dv_o=dv_o):
                    k.w(t, "dv", dv_o)
                    ins = t.transpose(ps[6].bitcast(BF16)[:, 0:128], otm.ap(), identb.ap())
                    k.inc(ins, "pe", 1)
                TE(_peot); pe_ot = k.c["pe"]

                def _dvo2(v, qb=qb, qs=qs, pe_ot=pe_ot):
                    k.w(v, "pe", pe_ot)
                    v.tensor_copy(ofm.ap()[:, qb * TT + qs * 128:qb * TT + (qs + 1) * 128],
                                  ps[6].bitcast(BF16)[:, 0:128])
                    k.inc(v.drain(), "dv", 1)
                VE(_dvo2)
            o_evac_dv = k.c["dv"]
            emit_tern_jobs(10)
        emit_tern_jobs(99)

        if p4cut < 5:
            def _syskip4(sy):
                k.w(sy, "dv", o_evac_dv)
                k.inc(sy.dma_start(out=yout, in_=xres), "sto", 16)
            SY(_syskip4)
            k.barrier()
            _es.close()
            return nc

        store_ev = [0, 0]
        for j in range(NT):
            slot = j % 2
            lsem = "lxa" if slot == 0 else "lxb"

            def _syx(sy, j=j, slot=slot, lsem=lsem):
                k.w(sy, "sxr", store_ev[slot])
                k.w(sy, "dv", o_evac_dv)
                k.inc(sy.dma_start(
                    out=xtile[slot].ap().rearrange("p (c w) -> p c w", w=TT),
                    in_=xres.rearrange("(c p) t -> p c t", p=128)[:, :, j * TT:(j + 1) * TT]),
                    lsem, 16)
            SY(_syx)
            lx_ev = k.c[lsem]

            dwo_base = k.c["dv"]

            def _pewo(t, j=j, lsem=lsem, lx_ev=lx_ev, dwo_base=dwo_base):
                k.w(t, "dv", max(o_evac_dv, dwo_base))
                k.w(t, "gd", 128)
                for m in range(C8):
                    if m >= 2:
                        k.w(t, "dv", dwo_base + (m - 2) + 1)
                    ins = t.matmul(ps[m % 2][:, 0:TT],
                                   wo_bf.ap()[:, m * 128:(m + 1) * 128],
                                   ofm.ap()[:, j * TT:(j + 1) * TT],
                                   start=True, stop=True)
                    k.inc(ins, "pe", 1)
            TE(_pewo)
            pwo_base = k.c["pe"] - 8

            def _dvwo(v, j=j, slot=slot, lsem=lsem, lx_ev=lx_ev, pwo_base=pwo_base):
                k.w(v, lsem, lx_ev)
                for m in range(C8):
                    k.w(v, "pe", pwo_base + m + 1)
                    v.tensor_scalar(tmpWo.ap(), ps[m % 2][:, 0:TT],
                                    bo_s.ap()[:, m:m + 1], modv[(1, "ga")].ap()[:, m:m + 1],
                                    op0=OP.add, op1=OP.mult)
                    v.drain()
                    v.tensor_tensor(chw(xtile[slot], m, TT), chw(xtile[slot], m, TT),
                                    tmpWo.ap(), op=OP.add)
                    k.inc(v.drain(), "dv", 1)
            VE(_dvwo)
            dv_upd = k.c["dv"]

            def _syst(sy, j=j, slot=slot, dv_upd=dv_upd):
                k.w(sy, "dv", dv_upd)
                k.inc(sy.dma_start(
                    out=xres.rearrange("(c p) t -> p c t", p=128)[:, :, j * TT:(j + 1) * TT],
                    in_=xtile[slot].ap().rearrange("p (c w) -> p c w", w=TT)), "sxr", 16)
            SY(_syst)
            store_ev[slot] = k.c["sxr"]

        AR.reset(ph4_mark)
        k.barrier()
        emit_tap("attn")

    if n_phases >= 5:
        TT5 = 512
        NT5 = 4
        ph5_mark = AR.mark()
        xtile = [AR.t(f"x5tl{i}", [128, C8 * TT5], F32) for i in range(2)]
        h2f5 = AR.t("h2f5", [128, C8 * TT5], F32)
        tmpA5 = AR.t("tmpA5", [128, TT5], F32)
        trb = AR.t("trb", [128, TT5], F32)
        am5 = AR.t("am5", [128, 12], F32)
        xq1_b = AR.t("xq1_b", [128, C8 * TT5], BF16)
        h5 = AR.t("h5", [128, 32 * TT5], F32)
        xq2_b = AR.at(AR.mark() - 32 * TT5 * 4, "xq2_b", [128, 32 * TT5], BF16)
        xbf5 = AR.t("xbf5", [128, C8 * TT5], BF16)
        sc1sb = AR.t("sc1sb", [128, TT5], F32)
        rs1sb = AR.t("rs1sb", [128, TT5], F32)
        sc2sb = AR.t("sc2sb", [128, TT5], F32)
        rs2sb = AR.t("rs2sb", [128, TT5], F32)
        tq5 = AR.t("tq5", [128, 2 * TT5], F32)
        qw1sl = [AR.t(f"qw1sl{i}", [128, C8 * 512], BF16) for i in range(2)]
        qw2sl = [AR.t(f"qw2sl{i}", [128, 32 * 128], BF16) for i in range(2)]
        assert AR.mark() <= 212480, AR.mark()

        def emit_quant_scales(parent, nch, cmcol, scb, rsb, pre_dv=0, pre_ac=0):
            W = TT5
            def _dvt(v):
                k.w(v, "dv", pre_dv)
                k.w(v, "ac", pre_ac)
                xv = parent.ap()[:, 0:nch * W].rearrange("p (c t) -> p t c", c=nch)
                v.tensor_reduce(trb.ap()[:, 0:W], xv, axis=mybir.AxisListType.X, op=OP.max)
                v.tensor_reduce(ps[7][:, 0:W], xv, axis=mybir.AxisListType.X, op=OP.min)
                v.drain()
                v.scalar_tensor_tensor(trb.ap()[:, 0:W], ps[7][:, 0:W], -1.0,
                                       trb.ap()[:, 0:W], op0=OP.mult, op1=OP.max)
                k.inc(v.drain(), "dv", 1)
            VE(_dvt)
            dv_tree = k.c["dv"]

            def _peA(t, dv_tree=dv_tree):
                k.w(t, "dv", dv_tree)
                for g4 in range(4):
                    ins = t.transpose(ps[6][:, g4 * 128:(g4 + 1) * 128],
                                      trb.ap()[:, g4 * 128:(g4 + 1) * 128], identf.ap())
                k.inc(ins, "pe", 1)
            TE(_peA); pe_trA = k.c["pe"]

            def _dvB(v, pe_trA=pe_trA, cmcol=cmcol):
                k.w(v, "pe", pe_trA)
                for g4 in range(4):
                    v.tensor_reduce(am5.ap()[:, g4:g4 + 1],
                                    ps[6][:, g4 * 128:(g4 + 1) * 128],
                                    axis=mybir.AxisListType.X, op=OP.max)
                v.drain()
                v.tensor_scalar(am5.ap()[:, 0:4], am5.ap()[:, 0:4], EPS, None, op0=OP.max)
                v.drain()
                v.reciprocal(am5.ap()[:, 4:8], am5.ap()[:, 0:4])
                v.tensor_scalar(am5.ap()[:, 8:12], am5.ap()[:, 0:4],
                                cmrep.ap()[:, cmcol:cmcol + 1], 1.0 / 127.0,
                                op0=OP.mult, op1=OP.mult)
                v.drain()
                v.tensor_scalar(am5.ap()[:, 4:8], am5.ap()[:, 4:8], 127.0, None,
                                op0=OP.mult)
                k.inc(v.drain(), "dv", 1)
            VE(_dvB); dv_sc = k.c["dv"]

            def _peB(t, dv_sc=dv_sc):
                k.w(t, "dv", dv_sc)
                for g4 in range(4):
                    t.transpose(ps[6][0:1, g4 * 128:(g4 + 1) * 128],
                                am5.ap()[:, 4 + g4:5 + g4], identf.ap())
                    ins = t.transpose(ps[7][0:1, g4 * 128:(g4 + 1) * 128],
                                      am5.ap()[:, 8 + g4:9 + g4], identf.ap())
                k.inc(ins, "pe", 1)
            TE(_peB); pe_trB = k.c["pe"]

            def _dvC(v, pe_trB=pe_trB):
                k.w(v, "pe", pe_trB)
                v.tensor_copy(rowm.ap()[:, 0:W], ps[6][0:1, 0:W])
                v.tensor_copy(rowq.ap()[:, 0:W], ps[7][0:1, 0:W])
                v.drain()
                v.tensor_copy(rhiA.ap()[:, 0:W], rowm.ap()[:, 0:W])
                v.tensor_copy(rhiB.ap()[:, 0:W], rowq.ap()[:, 0:W])
                v.drain()
                v.tensor_tensor(rloAb.ap()[:, 0:W], rowm.ap()[:, 0:W],
                                rhiA.ap()[:, 0:W], op=OP.subtract)
                v.tensor_tensor(rloBb.ap()[:, 0:W], rowq.ap()[:, 0:W],
                                rhiB.ap()[:, 0:W], op=OP.subtract)
                k.inc(v.drain(), "dv", 1)
            VE(_dvC); dv_hl = k.c["dv"]

            def _peC(t, dv_hl=dv_hl):
                k.w(t, "dv", dv_hl)
                t.matmul(ps[6][:, 0:W], ones_rb.ap(), rhiA.ap()[:, 0:W],
                         start=True, stop=False)
                t.matmul(ps[6][:, 0:W], ones_rb.ap(), rloAb.ap()[:, 0:W],
                         start=False, stop=True)
                t.matmul(ps[7][:, 0:W], ones_rb.ap(), rhiB.ap()[:, 0:W],
                         start=True, stop=False)
                ins = t.matmul(ps[7][:, 0:W], ones_rb.ap(), rloBb.ap()[:, 0:W],
                               start=False, stop=True)
                k.inc(ins, "pe", 1)
            TE(_peC); pe_bcq = k.c["pe"]

            def _dvD(v, pe_bcq=pe_bcq, scb=scb, rsb=rsb):
                k.w(v, "pe", pe_bcq)
                v.tensor_copy(scb.ap()[:, 0:W], ps[6][:, 0:W])
                v.tensor_copy(rsb.ap()[:, 0:W], ps[7][:, 0:W])
                k.inc(v.drain(), "dv", 1)
            VE(_dvD)
            return k.c["dv"]

        st5 = {"prev_apply": 0, "store": [0, 0], "qw_pe": {0: 0, 1: 0},
               "w1_pe_end": 0, "w2_pe_end": 0, "xq1_dv": 0, "xq2_dv": 0,
               "gelu_end": 0, "out_dv": 0}

        def front_end(j):
            slot = j % 2
            lsem = "lxa" if slot == 0 else "lxb"

            def _syx(sy, j=j, slot=slot, lsem=lsem):
                k.w(sy, "sto", st5["store"][slot])
                k.inc(sy.dma_start(
                    out=xtile[slot].ap().rearrange("p (c w) -> p c w", w=TT5),
                    in_=xres.rearrange("(c p) t -> p c t", p=128)[:, :, j * TT5:(j + 1) * TT5]),
                    lsem, 16)
            SY(_syx)
            lx_ev = k.c[lsem]

            def xap(c, slot=slot):
                return chw(xtile[slot], c, TT5)

            ap_ev = emit_ln(xap, xbf5,
                            lambda c: chw(h2f5, c, TT5),
                            lambda c: gam_s.ap()[:, c:c + 1],
                            lambda c: bet_s.ap()[:, c:c + 1],
                            TT5, tmpA5, trb, st5["prev_apply"],
                            extra_gp=(lsem, lx_ev),
                            extra_ac=("dv", st5["xq1_dv"]))
            st5["prev_apply"] = ap_ev

            emit_quant_scales(h2f5, C8, 0, sc1sb, rs1sb,
                              pre_dv=ap_ev, pre_ac=st5["gelu_end"])
            sc1_dv = k.c["dv"]

            dq_pred = []
            w1_done = st5["w1_pe_end"]
            for c in range(C8):
                def _g1(g, c=c, sc1_dv=sc1_dv):
                    if c == 0:
                        k.w(g, "dv", sc1_dv)
                    if c >= 2:
                        k.w(g, "dv", dq_pred[c - 2])
                    g.tensor_tensor(tq5.ap()[:, (c % 2) * TT5:(c % 2 + 1) * TT5],
                                    chw(h2f5, c, TT5), sc1sb.ap(), op=OP.mult)
                    k.inc(g.drain(), "gp", 1)
                GP(_g1); g_ev = k.c["gp"]

                def _d1(v, c=c, g_ev=g_ev, w1_done=w1_done):
                    k.w(v, "gp", g_ev)
                    if c == 0:
                        k.w(v, "pe", w1_done)
                    v.tensor_scalar(chw(xq1_b, c, TT5),
                                    tq5.ap()[:, (c % 2) * TT5:(c % 2 + 1) * TT5],
                                    MAGIC, MAGIC, op0=OP.add, op1=OP.subtract)
                    k.inc(v.drain(), "dv", 1)
                VE(_d1)
                dq_pred.append(k.c["dv"])
            st5["xq1_dv"] = k.c["dv"]

        def w1_stage(j):
            w1_dv0 = k.c["dv"]
            w1_ac0 = k.c["ac"]
            xq1_dv = st5["xq1_dv"]
            for mb in range(C8):
                qsl = mb % 2
                qsem = "qwa" if qsl == 0 else "qwb"

                def _syw(sy, mb=mb, qsl=qsl, qsem=qsem):
                    k.w(sy, "pe", st5["qw_pe"][qsl])
                    k.inc(sy.dma_start(
                        out=qw1sl[qsl].ap().rearrange("p (c f) -> p c f", f=512),
                        in_=wq1d.rearrange("(c p) f -> p c f", p=128)[:, :, mb * 512:(mb + 1) * 512]),
                        qsem, 16)
                SY(_syw)
                qw_ev = k.c[qsem]

                def _pew1(t, mb=mb, qsl=qsl, qsem=qsem, qw_ev=qw_ev, xq1_dv=xq1_dv):
                    k.w(t, qsem, qw_ev)
                    if mb == 0:
                        k.w(t, "dv", max(xq1_dv, st5["out_dv"]))
                    for mc in range(4):
                        gi = mb * 4 + mc
                        if gi >= 2:
                            k.w(t, "dv", w1_dv0 + (gi - 2) + 1)
                        for kk in range(C8):
                            ins = t.matmul(ps[mc % 2][:, 0:TT5],
                                           chw(qw1sl[qsl], kk, 512)[:, mc * 128:(mc + 1) * 128],
                                           chw(xq1_b, kk, TT5),
                                           start=(kk == 0), stop=(kk == C8 - 1))
                        k.inc(ins, "pe", 1)
                TE(_pew1)
                st5["qw_pe"][qsl] = k.c["pe"]
                p1_base = k.c["pe"] - 4

                for mc in range(4):
                    def _dvw1(v, mb=mb, mc=mc, p1_base=p1_base):
                        k.w(v, "pe", p1_base + mc + 1)
                        gi = mb * 4 + mc
                        if gi >= 2:
                            k.w(v, "ac", w1_ac0 + (gi - 2) + 1)
                        v.tensor_tensor(tq5.ap()[:, (mc % 2) * TT5:(mc % 2 + 1) * TT5],
                                        ps[mc % 2][:, 0:TT5], rs1sb.ap(), op=OP.mult)
                        k.inc(v.drain(), "dv", 1)
                    VE(_dvw1)
                    dv_m = k.c["dv"]

                    def _acw1(a, mb=mb, mc=mc, dv_m=dv_m):
                        k.w(a, "dv", dv_m)
                        mm = mb * 4 + mc
                        ins = a.activation(chw(h5, mm, TT5),
                                           tq5.ap()[:, (mc % 2) * TT5:(mc % 2 + 1) * TT5],
                                           AF.Gelu, bias=b1_s.ap()[:, mm:mm + 1])
                        k.inc(ins, "ac", 1)
                    SC(_acw1)
            st5["w1_pe_end"] = k.c["pe"]
            st5["gelu_end"] = k.c["ac"]

        def quant2_stage(j):
            emit_quant_scales(h5, 32, 1, sc2sb, rs2sb,
                              pre_ac=st5["gelu_end"])
            sc2_dv = k.c["dv"]
            dq2_pred = []
            w2_done = st5["w2_pe_end"]
            for c in range(32):
                def _g2(g, c=c, sc2_dv=sc2_dv):
                    if c == 0:
                        k.w(g, "dv", sc2_dv)
                    if c >= 2:
                        k.w(g, "dv", dq2_pred[c - 2])
                    g.tensor_tensor(tq5.ap()[:, (c % 2) * TT5:(c % 2 + 1) * TT5],
                                    chw(h5, c, TT5), sc2sb.ap(), op=OP.mult)
                    k.inc(g.drain(), "gp", 1)
                GP(_g2); g_ev = k.c["gp"]

                def _d2(v, c=c, g_ev=g_ev, w2_done=w2_done):
                    k.w(v, "gp", g_ev)
                    if c == 0:
                        k.w(v, "pe", w2_done)
                    v.tensor_scalar(chw(xq2_b, c, TT5),
                                    tq5.ap()[:, (c % 2) * TT5:(c % 2 + 1) * TT5],
                                    MAGIC, MAGIC, op0=OP.add, op1=OP.subtract)
                    k.inc(v.drain(), "dv", 1)
                VE(_d2)
                dq2_pred.append(k.c["dv"])
            st5["xq2_dv"] = k.c["dv"]

        def w2_stage(j):
            slot = j % 2
            xq2_dv = st5["xq2_dv"]
            dvw2_pred = []
            for mb in range(C8):
                qsl = mb % 2
                qsem = "qwa" if qsl == 0 else "qwb"

                def _syw2(sy, mb=mb, qsl=qsl, qsem=qsem):
                    k.w(sy, "pe", st5["qw_pe"][qsl])
                    k.inc(sy.dma_start(
                        out=qw2sl[qsl].ap().rearrange("p (c f) -> p c f", f=128),
                        in_=wq2d.rearrange("(c p) f -> p c f", p=128)[:, :, mb * 128:(mb + 1) * 128]),
                        qsem, 16)
                SY(_syw2)
                qw_ev = k.c[qsem]

                def _pew2(t, mb=mb, qsl=qsl, qsem=qsem, qw_ev=qw_ev, xq2_dv=xq2_dv,
                          dvw2_pred=dvw2_pred):
                    k.w(t, qsem, qw_ev)
                    if mb == 0:
                        k.w(t, "dv", xq2_dv)
                    if mb >= 2:
                        k.w(t, "dv", dvw2_pred[mb - 2])
                    for kk in range(32):
                        ins = t.matmul(ps[mb % 2][:, 0:TT5],
                                       chw(qw2sl[qsl], kk, 128),
                                       chw(xq2_b, kk, TT5),
                                       start=(kk == 0), stop=(kk == 31))
                    k.inc(ins, "pe", 1)
                TE(_pew2)
                st5["qw_pe"][qsl] = k.c["pe"]
                p2_ev = k.c["pe"]

                def _dvw2(v, mb=mb, slot=slot, p2_ev=p2_ev):
                    k.w(v, "pe", p2_ev)
                    v.tensor_tensor(tmpA5.ap(), ps[mb % 2][:, 0:TT5], rs2sb.ap(), op=OP.mult)
                    v.drain()
                    v.tensor_scalar(tmpA5.ap(), tmpA5.ap(), b2_s.ap()[:, mb:mb + 1],
                                    modv[(1, "gb")].ap()[:, mb:mb + 1],
                                    op0=OP.add, op1=OP.mult)
                    v.drain()
                    v.tensor_tensor(chw(xtile[slot], mb, TT5), chw(xtile[slot], mb, TT5),
                                    tmpA5.ap(), op=OP.add)
                    k.inc(v.drain(), "dv", 1)
                VE(_dvw2)
                dvw2_pred.append(k.c["dv"])
            st5["w2_pe_end"] = k.c["pe"]
            st5["out_dv"] = k.c["dv"]
            dv_out = k.c["dv"]

            def _syout(sy, j=j, slot=slot, dv_out=dv_out):
                k.w(sy, "dv", dv_out)
                k.inc(sy.dma_start(
                    out=yout.rearrange("(c p) t -> p c t", p=128)[:, :, j * TT5:(j + 1) * TT5],
                    in_=xtile[slot].ap().rearrange("p (c w) -> p c w", w=TT5)), "sto", 16)
            SY(_syout)
            st5["store"][slot] = k.c["sto"]

        front_end(0)
        w1_stage(0)
        for j in range(NT5):
            if j + 1 < NT5:
                front_end(j + 1)
            quant2_stage(j)
            w2_stage(j)
            if j + 1 < NT5:
                w1_stage(j + 1)

        AR.reset(ph5_mark)

    if n_phases < 5:
        def _syfin(sy):
            k.inc(sy.dma_start(out=yout, in_=xres), "sto", 16)
        SY(_syfin)

    k.barrier()
    sto_fin = k.c["sto"]

    def _syw8(sy):
        k.w(sy, "sto", sto_fin)
    SY(_syw8)

    _es.close()
    return nc



def _vec(v, p=128):
    v = np.asarray(v, np.float32).reshape(-1)
    n = v.size // p
    return np.ascontiguousarray(v.reshape(n, p).T)


_CACHE = {}


def _get_nc(taps=(), n_phases=N_PHASES, p4cut=9):
    key = (tuple(taps), n_phases, p4cut)
    if key not in _CACHE:
        _CACHE[key] = build_program(taps, n_phases, p4cut)
    return _CACHE[key]


def kernel(L0_attn_x, L0_attn_W_in, L0_attn_conv_w, L0_attn_conv_b, L0_attn_W_x,
           L0_attn_W_dt, L0_attn_b_dt, L0_attn_W_out, L0_ffn_input_ids,
           L0_ffn_gate_w, L0_ffn_gate_b, L0_ffn_w13, L0_ffn_w2, L0_ffn_w13_s,
           L0_ffn_w2_s, L0_c, L0_wmod, L1_attn_wavelengths, L1_attn_phase_bias,
           L1_attn_Wq, L1_attn_bq, L1_attn_Wk, L1_attn_bk, L1_attn_Wv, L1_attn_bv,
           L1_attn_Wo, L1_attn_bo, L1_ffn_w1, L1_ffn_b1, L1_ffn_w2, L1_ffn_b2,
           L1_ffn_gamma, L1_ffn_beta, L1_c, L1_wmod, _taps=(), _n_phases=N_PHASES, _p4cut=9):
    f32 = lambda a: np.ascontiguousarray(np.asarray(a, np.float32))
    x = f32(L0_attn_x).reshape(S, D)

    perm = np.empty(1024, np.int64)
    for p in range(128):
        for cc in range(8):
            perm[8 * p + cc] = 128 * cc + p

    wmods = [f32(L0_wmod), f32(L1_wmod)]
    cs = [f32(L0_c).reshape(-1), f32(L1_c).reshape(-1)]

    eall_h = np.zeros((16, 16 * 128), np.float32)
    for e in range(16):
        eall_h[e, e * 128:(e + 1) * 128] = 1.0

    common = dict(
        ident_in=np.eye(128, dtype=np.float32),
        eall_in=eall_h,
        w_in_t=f32(L0_attn_W_in).T.copy(),
        w_x32_t=f32(L0_attn_W_x)[:DTR].T.copy(),
        w_dt_t=f32(L0_attn_W_dt).T.copy(),
        w_out_t=f32(L0_attn_W_out).T.copy(),
        conv_w_in=np.ascontiguousarray(
            f32(L0_attn_conv_w).reshape(16, 128, 4).transpose(1, 0, 2).reshape(128, 64)),
        conv_b_in=_vec(L0_attn_conv_b),
        b_dt_in=_vec(L0_attn_b_dt),
        gate_w_t=f32(L0_ffn_gate_w).T.copy(),
        gate_b_in=f32(L0_ffn_gate_b).reshape(16, 1),
        w13_t=np.ascontiguousarray(f32(L0_ffn_w13).transpose(0, 2, 1)),
        w2_t=np.ascontiguousarray(f32(L0_ffn_w2).transpose(0, 2, 1)),
        w13s_t=f32(L0_ffn_w13_s).T.copy(),
        w2s_t=f32(L0_ffn_w2_s).T.copy(),
        wq_t=f32(L1_attn_Wq).T.copy(), wk_t=f32(L1_attn_Wk).T.copy(),
        wv_t=f32(L1_attn_Wv).T.copy(), wo_t=f32(L1_attn_Wo).T.copy(),
        bq_in=f32(L1_attn_bq).reshape(128, 1), bk_in=f32(L1_attn_bk).reshape(128, 1),
        bv_in=f32(L1_attn_bv).reshape(128, 1), bo_in=_vec(L1_attn_bo),
        wavel_in=f32(L1_attn_wavelengths).reshape(128, 1),
        phaseb_in=f32(L1_attn_phase_bias).reshape(128, 1),
        w1_t=f32(L1_ffn_w1).T.copy(),
        w2l_t=f32(L1_ffn_w2).T.copy(),
        b1_in=_vec(L1_ffn_b1), b2_in=_vec(L1_ffn_b2),
        gamma_in=_vec(L1_ffn_gamma), beta_in=_vec(L1_ffn_beta),
    )

    in_maps = []
    for r in range(NCOR):
        m = dict(common)
        xs = x[r * T:(r + 1) * T]
        m["xT"] = np.ascontiguousarray(xs.T)
        if r == 0:
            m["xprevT"] = np.zeros((D, 3), np.float32)
            m["halomask"] = np.zeros((128, 1), np.float32)
        else:
            m["xprevT"] = np.ascontiguousarray(x[r * T - 3:r * T].T)
            m["halomask"] = np.ones((128, 1), np.float32)
        cm = np.zeros((2, 128, 8), np.float32)
        wsl = np.zeros((2, D, D), np.float32)
        for idx, s in enumerate((r, 8 + r)):
            if s >= 12:
                continue
            l, v = divmod(s, 6)
            cm[idx] = _vec(cs[l])
            wsl[idx] = wmods[l][:, v * 1024:(v + 1) * 1024][:, perm]
        m["cmod"] = cm
        m["wmodsl"] = wsl
        in_maps.append(m)

    nc = _get_nc(_taps, _n_phases, _p4cut)
    res = run_bass_kernel_spmd(nc, in_maps, core_ids=list(range(NCOR)))
    out = np.concatenate([np.asarray(r["yout"]).T for r in res.results], axis=0)
    if _taps:
        taps_out = {tp: np.concatenate(
            [np.asarray(r["dbg_" + tp]).T for r in res.results], axis=0)
            for tp in _taps}
        return out.reshape(1, S, D), taps_out
    return out.reshape(1, S, D)
